# version 06 (final)
# baseline (speedup 1.0000x reference)
import numpy as np
import ml_dtypes
from contextlib import ExitStack
import concourse.bass as bass
import concourse.mybir as mybir
from concourse.bass_utils import run_bass_kernel_spmd

F32 = mybir.dt.float32
BF16 = mybir.dt.bfloat16
AF = mybir.ActivationFunctionType
ALU = mybir.AluOpType
AX = mybir.AxisListType.X

NCORES = 8
D = 1024
NH = 16
DK = 64
DV = 128
VD = 2048
FF = 2752
NE = 8
FH = 4096
HD = 64
CHUNK = 128
LN_EPS = 1e-5
EPS = 1e-6
QSCALE = DK ** (-0.5)
BSCALE = HD ** (-0.5)

ENGINES = ("tensor", "vector", "scalar", "gpsimd", "sync")

BF = ml_dtypes.bfloat16


class Sem:
    __slots__ = ("h", "v", "name")

    def __init__(self, h, name):
        self.h = h
        self.v = 0
        self.name = name


class Slot:
    __slots__ = ("ap", "marks", "stsem")

    def __init__(self, ap):
        self.ap = ap
        self.marks = []
        self.stsem = None

    def take(self):
        m, self.marks = self.marks, []
        return m

    def mark(self, sem):
        self.marks.append((sem, sem.v))

    def st(self, p):
        if self.stsem is None:
            self.stsem = p.sem()
        return self.stsem


class Rot:

    def __init__(self, aps):
        self.slots = [Slot(a) for a in aps]
        self.i = 0

    def next(self):
        s = self.slots[self.i % len(self.slots)]
        self.i += 1
        return s

    def mark_all(self, sem):
        for s in self.slots:
            s.marks = [(sem, sem.v)]


class Prog:

    def __init__(self, nc, ctx):
        self.nc = nc
        self.ctx = ctx
        self.q = {e: [] for e in ENGINES}
        self.wm = {e: {} for e in ENGINES}
        self.nsem = 0
        self.sems = []

    def sem(self, name=None):
        name = name or f"s{self.nsem}"
        self.nsem += 1
        h = self.ctx.enter_context(self.nc.semaphore(name))
        s = Sem(h, name)
        self.sems.append(s)
        return s

    def do(self, eng, fn, wait=(), inc=()):
        resolved = []
        for w in wait:
            s, val = (w if isinstance(w, tuple) else (w, None))
            val = s.v if val is None else val
            if val > 0 and self.wm[eng].get(s, 0) < val:
                resolved.append((s, val))
                self.wm[eng][s] = val
        incs = []
        for i in inc:
            s, n = (i if isinstance(i, tuple) else (i, 1))
            s.v += n
            incs.append((s, n))
        self.q[eng].append((resolved, fn, incs))

    def dma(self, eng, out, in_, wait=(), inc=(), **kw):
        self.do(eng, lambda e: e.dma_start(out=out, in_=in_, **kw),
                wait=wait, inc=inc)

    def barrier(self, name=None, engines=ENGINES):
        b = self.sem(name or f"bar{self.nsem}")
        n = len(engines)
        flush = [(s, s.v) for s in self.sems if s.v > 0 and s is not b]
        for e in engines:
            self.do(e, lambda eng: eng.nop(), wait=flush, inc=[b])
        for e in engines:
            self.do(e, lambda eng: eng.nop(), wait=[(b, n)])
        return b

    def build_block(self):
        with self.nc.Block() as block:
            for ename in ENGINES:
                ops = self.q[ename]

                def make(ops=ops):
                    def body(eng):
                        for (waits, fn, incs) in ops:
                            for (s, v) in waits:
                                eng.wait_ge(s.h, v)
                            instr = fn(eng) if fn is not None else None
                            for (s, n) in incs:
                                if instr is None:
                                    instr = eng.nop()
                                instr.then_inc(s.h, n)
                    return body

                getattr(block, ename)(make())


def cdiv(a, b):
    return (a + b - 1) // b



def host_consts():
    c = {}
    c["c_id128"] = np.eye(128, dtype=BF)
    c["c_id128_f"] = np.eye(128, dtype=np.float32)
    c["c_ones_row_f"] = np.ones((1, 128), np.float32)
    c["c_ones_row_b"] = np.ones((1, 128), BF)
    c["c_ones_col_b"] = np.ones((128, 1), BF)
    c["c_ones_col_f"] = np.ones((128, 1), np.float32)
    c["c_ones_f128"] = np.ones((128, 128), np.float32)
    c["c_slmask"] = np.tril(np.ones((128, 128), np.float32), -1)
    c["c_slmask2"] = np.tile(np.tril(np.ones((128, 128), np.float32), -1), (1, 2))
    c["c_negmask2"] = np.tile(np.tril(np.ones((128, 128), np.float32), -1) * -1e6,
                              (1, 2))
    c["c_negmask"] = (np.tril(np.ones((128, 128), np.float32), -1) * -1e6)
    ps = np.zeros((128, 2), BF)
    ps[:64, 0] = 1
    ps[64:, 1] = 1
    c["c_pairsum"] = ps
    pe = np.zeros((2, 128), BF)
    pe[0, :64] = 1
    pe[1, 64:] = 1
    c["c_pairexp"] = pe
    c["c_pairexp_f"] = pe.astype(np.float32)
    sel = np.zeros((8, 8 * 128), BF)
    for e in range(8):
        sel[e, e * 128:(e + 1) * 128] = 1
    c["c_sel"] = sel
    c["c_fconst"] = np.tile(np.array([[LN_EPS, EPS, 1.0, 0.5, -0.5, -1.0, 2.0,
                                        0.0]], np.float32), (128, 1))
    return c


CONST_NAMES = ("c_fconst", "c_slmask2", "c_negmask2",
               "c_id128", "c_id128_f", "c_ones_row_f", "c_ones_row_b", "c_ones_col_b", "c_ones_col_f", "c_ones_f128",
               "c_slmask", "c_negmask", "c_pairsum", "c_pairexp",
               "c_pairexp_f", "c_sel")


class Consts:

    def __init__(self, p, ctx, io, s_ld):
        nc = p.nc
        self.id128 = ctx.enter_context(nc.sbuf_tensor("k_id128", [128, 128], BF16))
        self.id128_f = ctx.enter_context(nc.sbuf_tensor("k_id128f", [128, 128], F32))
        self.ones_row_f = ctx.enter_context(nc.sbuf_tensor("k_or_f", [1, 128], F32))
        self.ones_row_b = ctx.enter_context(nc.sbuf_tensor("k_or_b", [1, 128], BF16))
        self.ones_col_b = ctx.enter_context(nc.sbuf_tensor("k_oc_b", [128, 1], BF16))
        self.ones_col_f = ctx.enter_context(nc.sbuf_tensor("k_oc_f", [128, 1], F32))
        self.ones_f128 = ctx.enter_context(nc.sbuf_tensor("k_of128", [128, 128], F32))
        self.slmask = ctx.enter_context(nc.sbuf_tensor("k_sl", [128, 128], F32))
        self.slmask2 = ctx.enter_context(nc.sbuf_tensor("k_sl2", [128, 256], F32))
        self.negmask2 = ctx.enter_context(nc.sbuf_tensor("k_neg2", [128, 256], F32))
        self.negmask = ctx.enter_context(nc.sbuf_tensor("k_neg", [128, 128], F32))
        self.pairsum = ctx.enter_context(nc.sbuf_tensor("k_psum", [128, 2], BF16))
        self.pairexp = ctx.enter_context(nc.sbuf_tensor("k_pexp", [2, 128], BF16))
        self.pairexp_f = ctx.enter_context(nc.sbuf_tensor("k_pexpf", [2, 128], F32))
        self.sel = ctx.enter_context(nc.sbuf_tensor("k_sel", [8, 8 * 128], BF16))
        self.fconst = ctx.enter_context(nc.sbuf_tensor("k_fc", [128, 8], F32))
        for i, v in enumerate((LN_EPS, EPS, 1.0, 0.5, -0.5, -1.0, 2.0)):
            nc.const_aps.register(F32, v, self.fconst[:, i:i + 1])
        for sb, key in ((self.fconst, "c_fconst"),
                        (self.id128, "c_id128"), (self.id128_f, "c_id128_f"),
                        (self.ones_row_f, "c_ones_row_f"),
                        (self.ones_row_b, "c_ones_row_b"), (self.ones_col_b, "c_ones_col_b"),
                        (self.ones_col_f, "c_ones_col_f"), (self.ones_f128, "c_ones_f128"),
                        (self.slmask, "c_slmask"), (self.negmask, "c_negmask"),
                        (self.slmask2, "c_slmask2"), (self.negmask2, "c_negmask2"),
                        (self.pairsum, "c_pairsum"), (self.pairexp, "c_pairexp"),
                        (self.pairexp_f, "c_pairexp_f"), (self.sel, "c_sel")):
            p.dma("sync", sb[:, :], io[key][:, :], inc=[(s_ld, 16)])



def emit_stage_a(p, ctx, C, cfg, io):
    from contextlib import ExitStack
    pctx = ExitStack()
    nc = p.nc
    Tl, TB, TILE = cfg["Tl"], cfg["TB"], cfg["TILE"]
    ntiles = Tl // TILE
    tiles_per_batch = TB // TILE

    s_ld = p.sem("a_ld")
    s_mm = p.sem("a_mm")
    s_dve = p.sem("a_dve")
    s_act = p.sem("a_act")
    s_st = p.sem("a_st")
    s_wld = p.sem("a_wld")
    s_wcv = p.sem("a_wcv")

    wstg = Rot([ctx.enter_context(nc.sbuf_tensor(f"a_wstg{i}", [128, 1024], F32))
                for i in range(2)])
    w_q = ctx.enter_context(nc.sbuf_tensor("w_q", [128, 8 * D], BF16))
    w_k = ctx.enter_context(nc.sbuf_tensor("w_k", [128, 8 * D], BF16))
    w_v = ctx.enter_context(nc.sbuf_tensor("w_v", [128, 8 * VD], BF16))
    w_g = ctx.enter_context(nc.sbuf_tensor("w_g", [128, 8 * VD], BF16))
    w_b = ctx.enter_context(nc.sbuf_tensor("w_b", [128, 8 * NH], BF16))
    w_a = ctx.enter_context(nc.sbuf_tensor("w_a", [128, 8 * NH], BF16))

    wld_sems = [p.sem("a_wld0"), p.sem("a_wld1")]

    def load_w(dram, sb, kdim, mdim):
        for kc in range(cdiv(kdim, 128)):
            rows = min(128, kdim - kc * 128)
            for m0 in range(0, mdim, 1024):
                mw = min(1024, mdim - m0)
                sl = wstg.next()
                ws = wld_sems[(wstg.i - 1) % 2]
                p.dma("sync", sl.ap[:rows, :mw],
                      dram[kc * 128:kc * 128 + rows, m0:m0 + mw],
                      wait=sl.take(), inc=[(ws, 16)])
                p.do("gpsimd", lambda e, sl=sl, kc=kc, rows=rows, sb=sb,
                     mdim=mdim, m0=m0, mw=mw:
                     e.tensor_copy(sb[:rows, kc * mdim + m0:kc * mdim + m0 + mw],
                                   sl.ap[:rows, :mw]),
                     wait=[ws], inc=[s_wcv])
                sl.mark(s_wcv)

    load_w(io["L0_attn_a_q_w"], w_q, D, D)
    load_w(io["L0_attn_a_k_w"], w_k, D, D)
    load_w(io["L0_attn_a_v_w"], w_v, D, VD)
    load_w(io["L0_attn_a_g_w"], w_g, D, VD)
    load_w(io["L0_attn_a_b_w"], w_b, D, NH)
    load_w(io["L0_attn_a_a_w"], w_a, D, NH)

    normw = ctx.enter_context(nc.sbuf_tensor("a_normw", [128, 8], F32))
    convw = ctx.enter_context(nc.sbuf_tensor("a_convw", [128, 128], F32))
    alog = ctx.enter_context(nc.sbuf_tensor("a_alog", [NH, 2], F32))
    nea = ctx.enter_context(nc.sbuf_tensor("a_nea", [NH, 1], F32))
    p.dma("sync", normw[:, :], io["pre_normw"][:, :], inc=[(s_ld, 16)])
    p.dma("sync", convw[:, :], io["pre_convw"][:, :], inc=[(s_ld, 16)])
    p.dma("sync", alog[:, 0:1], io["L0_attn_a_A_log"].rearrange("(h o) -> h o", o=1),
          inc=[(s_ld, 16)])
    p.dma("sync", alog[:, 1:2], io["L0_attn_a_dt_bias"].rearrange("(h o) -> h o", o=1),
          inc=[(s_ld, 16)])
    p.do("scalar", lambda e: e.activation(nea[:, :], alog[:, 0:1], AF.Exp),
         wait=[s_ld], inc=[s_act])
    p.do("vector", lambda e: e.tensor_scalar_mul(nea[:, :], nea[:, :], -1.0),
         wait=[s_act], inc=[s_dve])

    p.barrier("a_w")

    S_xt = Slot(ctx.enter_context(nc.sbuf_tensor("a_xt", [128, 8 * TILE], F32)))
    R_sq = Rot([ctx.enter_context(nc.sbuf_tensor(f"a_sq{i}", [128, TILE], BF16))
                for i in range(8)])
    S_hb = Slot(ctx.enter_context(nc.sbuf_tensor("a_hb", [128, 8 * TILE], BF16)))
    stat = ctx.enter_context(nc.sbuf_tensor("a_stat", [1, 5 * TILE], F32))
    qpre = [Slot(ctx.enter_context(nc.sbuf_tensor(f"a_qpre{m}", [128, TILE + 4], BF16)))
            for m in range(8)]
    kpre = [Slot(ctx.enter_context(nc.sbuf_tensor(f"a_kpre{m}", [128, TILE + 4], BF16)))
            for m in range(8)]
    vpre = [Slot(ctx.enter_context(nc.sbuf_tensor(f"a_vpre{m}", [128, TILE + 4], BF16)))
            for m in range(16)]
    R_cvt = Rot([ctx.enter_context(nc.sbuf_tensor(f"a_cvt{i}", [128, TILE], F32))
                 for i in range(2)])
    R_cvs = Rot([ctx.enter_context(nc.sbuf_tensor(f"a_cvs{i}", [128, TILE], BF16))
                 for i in range(2)])
    S_small = Slot(ctx.enter_context(nc.sbuf_tensor("a_small", [16, 4 * TILE], F32)))
    rst2 = Slot(ctx.enter_context(nc.sbuf_tensor("a_rst2", [2, TILE], F32)))

    P_st1 = Slot(pctx.enter_context(nc.psum_tensor("a_ps1", [1, 512], F32)))
    P_st2 = Slot(pctx.enter_context(nc.psum_tensor("a_ps2", [1, 512], F32)))
    P_bc = Slot(pctx.enter_context(nc.psum_tensor("a_pb", [128, 512], F32)))
    R_pm = Rot([pctx.enter_context(nc.psum_tensor(f"a_pm{i}", [128, 512], F32))
                for i in range(2)])
    P_sm = Slot(pctx.enter_context(nc.psum_tensor("a_psm", [16, 512], F32)))
    P_l2 = Slot(pctx.enter_context(nc.psum_tensor("a_pl2", [2, 512], F32)))
    P_l2b = Slot(pctx.enter_context(nc.psum_tensor("a_pl2b", [128, 512], F32)))

    ST = TILE
    for t in range(ntiles):
        tok0 = t * TILE
        first_in_batch = (t % tiles_per_batch) == 0

        xw = S_xt.take()
        for c in range(8):
            p.dma("sync", S_xt.ap[:, c * TILE:(c + 1) * TILE],
                  io["x_fm"][c * 128:(c + 1) * 128, tok0:tok0 + TILE],
                  wait=(xw if c == 0 else []), inc=[(s_ld, 16)])

        for c in range(8):
            p.do("tensor", lambda e, c=c: e.matmul(
                P_st1.ap[:, :TILE], C.ones_col_f[:, :],
                S_xt.ap[:, c * TILE:(c + 1) * TILE],
                start=(c == 0), stop=(c == 7)),
                wait=[s_ld] + (P_st1.take() if c == 0 else []),
                inc=([s_mm] if c == 7 else []))
        S_xt.mark(s_mm)
        for c in range(8):
            sq = R_sq.next()
            p.do("scalar", lambda e, c=c, sq=sq: e.activation(
                sq.ap[:, :], S_xt.ap[:, c * TILE:(c + 1) * TILE], AF.Square),
                wait=[s_ld] + sq.take(), inc=[s_act])
            p.do("tensor", lambda e, c=c, sq=sq: e.matmul(
                P_st2.ap[:, :TILE], C.ones_col_b[:, :], sq.ap[:, :],
                start=(c == 0), stop=(c == 7)),
                wait=[s_act] + (P_st2.take() if c == 0 else []),
                inc=([s_mm] if c == 7 else []))
        R_sq.mark_all(s_mm)
        S_xt.mark(s_act)
        p.do("vector", lambda e: e.tensor_scalar_mul(
            stat[:, 0 * ST:0 * ST + TILE], P_st1.ap[:, :TILE], 1.0 / D),
            wait=[s_mm], inc=[s_dve])
        P_st1.mark(s_dve)
        p.do("scalar", lambda e: e.activation(
            stat[:, 1 * ST:1 * ST + TILE], stat[:, 0 * ST:0 * ST + TILE],
            AF.Square), wait=[s_dve], inc=[s_act])
        p.do("vector", lambda e: e.scalar_tensor_tensor(
            stat[:, 4 * ST:4 * ST + TILE], P_st2.ap[:, :TILE], 1.0 / D,
            stat[:, 1 * ST:1 * ST + TILE], ALU.mult, ALU.subtract),
            wait=[s_mm, s_act], inc=[s_dve])
        P_st2.mark(s_dve)
        p.do("scalar", lambda e: e.activation(
            stat[:, 2 * ST:2 * ST + TILE], stat[:, 4 * ST:4 * ST + TILE],
            AF.Ln, bias=LN_EPS), wait=[s_dve], inc=[s_act])
        p.do("scalar", lambda e: e.activation(
            stat[:, 2 * ST:2 * ST + TILE], stat[:, 2 * ST:2 * ST + TILE],
            AF.Exp, scale=-0.5), wait=[s_act], inc=[s_act])
        p.do("tensor", lambda e: e.matmul(
            P_bc.ap[:, :TILE], C.ones_row_f[:, :], stat[:, 0 * ST:0 * ST + TILE],
            start=True, stop=True), wait=[(s_dve, s_dve.v)] + P_bc.take(),
            inc=[s_mm])
        for c in range(8):
            p.do("vector", lambda e, c=c: e.tensor_sub(
                S_xt.ap[:, c * TILE:(c + 1) * TILE],
                S_xt.ap[:, c * TILE:(c + 1) * TILE], P_bc.ap[:, :TILE]),
                wait=[s_mm], inc=[s_dve])
        P_bc.mark(s_dve)
        p.do("tensor", lambda e: e.matmul(
            P_bc.ap[:, :TILE], C.ones_row_f[:, :], stat[:, 2 * ST:2 * ST + TILE],
            start=True, stop=True), wait=[s_act] + P_bc.take(), inc=[s_mm])
        for c in range(8):
            p.do("vector", lambda e, c=c: e.tensor_mul(
                S_xt.ap[:, c * TILE:(c + 1) * TILE],
                S_xt.ap[:, c * TILE:(c + 1) * TILE], P_bc.ap[:, :TILE]),
                wait=[s_mm], inc=[s_dve])
        P_bc.mark(s_dve)
        for c in range(8):
            p.dma("gpsimd", io["ln0_s"][c * 128:(c + 1) * 128, tok0:tok0 + TILE],
                  S_xt.ap[:, c * TILE:(c + 1) * TILE],
                  wait=[(s_dve, s_dve.v)], inc=[(S_xt.st(p), 16)])
        S_xt.mark(S_xt.st(p))

        for c in range(8):
            sq = R_sq.next()
            p.do("scalar", lambda e, c=c, sq=sq: e.activation(
                sq.ap[:, :], S_xt.ap[:, c * TILE:(c + 1) * TILE], AF.Square),
                wait=[(s_dve, s_dve.v)] + sq.take(), inc=[s_act])
            p.do("tensor", lambda e, c=c, sq=sq: e.matmul(
                P_st1.ap[:, :TILE], C.ones_col_b[:, :], sq.ap[:, :],
                start=(c == 0), stop=(c == 7)),
                wait=[s_act] + (P_st1.take() if c == 0 else []),
                inc=([s_mm] if c == 7 else []))
        R_sq.mark_all(s_mm)
        p.do("scalar", lambda e: e.activation(
            stat[:, 3 * ST:3 * ST + TILE], P_st1.ap[:, :TILE], AF.Ln,
            scale=1.0 / D, bias=EPS), wait=[s_mm], inc=[s_act])
        P_st1.mark(s_act)
        p.do("scalar", lambda e: e.activation(
            stat[:, 3 * ST:3 * ST + TILE], stat[:, 3 * ST:3 * ST + TILE],
            AF.Exp, scale=-0.5), wait=[s_act], inc=[s_act])
        p.do("tensor", lambda e: e.matmul(
            P_bc.ap[:, :TILE], C.ones_row_f[:, :], stat[:, 3 * ST:3 * ST + TILE],
            start=True, stop=True), wait=[s_act] + P_bc.take(), inc=[s_mm])
        hbw = S_hb.take()
        for c in range(8):
            p.do("vector", lambda e, c=c: e.scalar_tensor_tensor(
                S_hb.ap[:, c * TILE:(c + 1) * TILE],
                S_xt.ap[:, c * TILE:(c + 1) * TILE],
                normw[:, c:c + 1], P_bc.ap[:, :TILE], ALU.mult, ALU.mult),
                wait=[s_mm] + (hbw if c == 0 else []), inc=[s_dve])
        P_bc.mark(s_dve)
        S_xt.mark(s_dve)
        hb_done = s_dve.v

        def proj_group(w_sb, mdim, mc, pm):
            for kc in range(8):
                p.do("tensor", lambda e, kc=kc, mc=mc, pm=pm, w_sb=w_sb, mdim=mdim:
                     e.matmul(pm.ap[:, :TILE],
                              w_sb[:, kc * mdim + mc * 128: kc * mdim + mc * 128 + 128],
                              S_hb.ap[:, kc * TILE:(kc + 1) * TILE],
                              start=(kc == 0), stop=(kc == 7)),
                     wait=[(s_dve, hb_done)] + (pm.take() if kc == 0 else []),
                     inc=([s_mm] if kc == 7 else []))

        for (w_sb, pre, nmc, mdim) in ((w_q, qpre, 8, D), (w_k, kpre, 8, D),
                                       (w_v, vpre, 16, VD)):
            for mc in range(nmc):
                pm = R_pm.next()
                proj_group(w_sb, mdim, mc, pm)
                pr = pre[mc]
                p.do("scalar", lambda e, pm=pm, pr=pr: e.activation(
                    pr.ap[:, 4:4 + TILE], pm.ap[:, :TILE], AF.Copy),
                    wait=[s_mm] + pr.take(), inc=[s_act])
                pm.mark(s_act)
                pr.mark(s_act)

        for kc in range(8):
            p.do("tensor", lambda e, kc=kc: e.matmul(
                P_sm.ap[:, :TILE], w_b[:, kc * NH:(kc + 1) * NH],
                S_hb.ap[:, kc * TILE:(kc + 1) * TILE],
                start=(kc == 0), stop=(kc == 7)),
                wait=[(s_dve, hb_done)] + (P_sm.take() if kc == 0 else []),
                inc=([s_mm] if kc == 7 else []))
        smw = S_small.take()
        p.do("scalar", lambda e: e.activation(
            S_small.ap[:, 0:TILE], P_sm.ap[:, :TILE], AF.Sigmoid),
            wait=[s_mm] + smw, inc=[s_act])
        P_sm.mark(s_act)
        p.dma("gpsimd", io["beta_s"][:, tok0:tok0 + TILE], S_small.ap[:, 0:TILE],
              wait=[s_act], inc=[(S_small.st(p), 16)])
        for kc in range(8):
            p.do("tensor", lambda e, kc=kc: e.matmul(
                P_sm.ap[:, :TILE], w_a[:, kc * NH:(kc + 1) * NH],
                S_hb.ap[:, kc * TILE:(kc + 1) * TILE],
                start=(kc == 0), stop=(kc == 7)),
                wait=[(s_dve, hb_done)] + (P_sm.take() if kc == 0 else []),
                inc=([s_mm] if kc == 7 else []))
        p.do("scalar", lambda e: e.activation(
            S_small.ap[:, ST:ST + TILE], P_sm.ap[:, :TILE], AF.Relu,
            bias=alog[:, 1:2]), wait=[s_mm], inc=[s_act])
        p.do("scalar", lambda e: e.activation(
            S_small.ap[:, 2 * ST:2 * ST + TILE], P_sm.ap[:, :TILE], AF.Abs,
            bias=alog[:, 1:2]), wait=[s_mm], inc=[s_act])
        P_sm.mark(s_act)
        p.do("scalar", lambda e: e.activation(
            S_small.ap[:, 2 * ST:2 * ST + TILE],
            S_small.ap[:, 2 * ST:2 * ST + TILE], AF.Exp, scale=-1.0),
            wait=[s_act], inc=[s_act])
        p.do("scalar", lambda e: e.activation(
            S_small.ap[:, 2 * ST:2 * ST + TILE],
            S_small.ap[:, 2 * ST:2 * ST + TILE], AF.Ln, bias=1.0),
            wait=[s_act], inc=[s_act])
        p.do("vector", lambda e: e.scalar_tensor_tensor(
            S_small.ap[:, 3 * ST:3 * ST + TILE], S_small.ap[:, ST:ST + TILE],
            1.0, S_small.ap[:, 2 * ST:2 * ST + TILE], ALU.mult, ALU.add),
            wait=[s_act], inc=[s_dve])
        p.do("vector", lambda e: e.tensor_scalar(
            S_small.ap[:, 3 * ST:3 * ST + TILE],
            S_small.ap[:, 3 * ST:3 * ST + TILE], nea[:, 0:1], None, ALU.mult),
            wait=[s_dve], inc=[s_dve])
        p.dma("gpsimd", io["g_s"][:, tok0:tok0 + TILE],
              S_small.ap[:, 3 * ST:3 * ST + TILE],
              wait=[(s_dve, s_dve.v)], inc=[(S_small.st(p), 16)])
        S_small.mark(S_small.st(p))

        for mc in range(16):
            pm = R_pm.next()
            for kc in range(8):
                p.do("tensor", lambda e, kc=kc, mc=mc, pm=pm: e.matmul(
                    pm.ap[:, :TILE],
                    w_g[:, kc * VD + mc * 128: kc * VD + mc * 128 + 128],
                    S_hb.ap[:, kc * TILE:(kc + 1) * TILE],
                    start=(kc == 0), stop=(kc == 7)),
                    wait=[(s_dve, hb_done)] + (pm.take() if kc == 0 else []),
                    inc=([s_mm] if kc == 7 else []))
            cv = R_cvs.next()
            p.do("scalar", lambda e, pm=pm, cv=cv: e.activation(
                cv.ap[:, :], pm.ap[:, :TILE], AF.Silu),
                wait=[s_mm] + cv.take(), inc=[s_act])
            pm.mark(s_act)
            p.dma("gpsimd", io["gsil_s"][mc * 128:(mc + 1) * 128, tok0:tok0 + TILE],
                  cv.ap[:, :], wait=[s_act], inc=[(cv.st(p), 16)])
            cv.mark(cv.st(p))
        S_hb.mark(s_mm)

        def conv_one(pre, nmc, wofs, out_dram, l2, tag):
            for mc in range(nmc):
                pr = pre[mc]
                if first_in_batch:
                    p.do("vector", lambda e, pr=pr: e.memset(pr.ap[:, 0:4], 0.0),
                         wait=[(s_act, s_act.v)], inc=[s_dve])
                cvt = R_cvt.next()
                p.do("vector", lambda e, pr=pr, mc=mc, cvt=cvt: e.tensor_scalar(
                    cvt.ap[:, :], pr.ap[:, 1:1 + TILE],
                    convw[:, wofs + mc * 4: wofs + mc * 4 + 1], None, ALU.mult),
                    wait=[(s_act, s_act.v), (s_dve, s_dve.v)] + cvt.take(),
                    inc=[s_dve])
                for i in (1, 2, 3):
                    p.do("vector", lambda e, pr=pr, mc=mc, i=i, cvt=cvt:
                         e.scalar_tensor_tensor(
                             cvt.ap[:, :], pr.ap[:, 1 + i:1 + i + TILE],
                             convw[:, wofs + mc * 4 + i: wofs + mc * 4 + i + 1],
                             cvt.ap[:, :], ALU.mult, ALU.add),
                         wait=[s_dve], inc=[s_dve])
                p.do("vector", lambda e, pr=pr: e.tensor_copy(
                    pr.ap[:, 0:4], pr.ap[:, TILE:TILE + 4]),
                    wait=[s_dve], inc=[s_dve])
                pr.marks = [(s_dve, s_dve.v)]
                if not l2:
                    cv = R_cvs.next()
                    p.do("scalar", lambda e, cvt=cvt, cv=cv: e.activation(
                        cv.ap[:, :], cvt.ap[:, :], AF.Silu),
                        wait=[(s_dve, s_dve.v)] + cv.take(), inc=[s_act])
                    cvt.mark(s_act)
                    p.dma("gpsimd",
                          out_dram[mc * 128:(mc + 1) * 128, tok0:tok0 + TILE],
                          cv.ap[:, :], wait=[s_act], inc=[(cv.st(p), 16)])
                    cv.mark(cv.st(p))
                else:
                    p.do("scalar", lambda e, cvt=cvt: e.activation(
                        cvt.ap[:, :], cvt.ap[:, :], AF.Silu),
                        wait=[(s_dve, s_dve.v)], inc=[s_act])
                    sq = R_sq.next()
                    p.do("scalar", lambda e, cvt=cvt, sq=sq: e.activation(
                        sq.ap[:, :], cvt.ap[:, :], AF.Square),
                        wait=[s_act] + sq.take(), inc=[s_act])
                    p.do("tensor", lambda e, sq=sq: e.matmul(
                        P_l2.ap[:, :TILE], C.pairsum[:, :], sq.ap[:, :],
                        start=True, stop=True),
                        wait=[s_act] + P_l2.take(), inc=[s_mm])
                    sq.mark(s_mm)
                    rw = rst2.take()
                    p.do("scalar", lambda e: e.activation(
                        rst2.ap[:, :], P_l2.ap[:, :TILE], AF.Ln, bias=EPS),
                        wait=[s_mm] + rw, inc=[s_act])
                    P_l2.mark(s_act)
                    p.do("scalar", lambda e: e.activation(
                        rst2.ap[:, :], rst2.ap[:, :], AF.Exp, scale=-0.5),
                        wait=[s_act], inc=[s_act])
                    p.do("tensor", lambda e: e.matmul(
                        P_l2b.ap[:, :TILE], C.pairexp_f[:, :], rst2.ap[:, :],
                        start=True, stop=True),
                        wait=[s_act] + P_l2b.take(), inc=[s_mm])
                    rst2.mark(s_mm)
                    cv = R_cvs.next()
                    scale = QSCALE if tag == "q" else 1.0
                    p.do("vector", lambda e, cvt=cvt, cv=cv, scale=scale:
                         e.scalar_tensor_tensor(
                             cv.ap[:, :], cvt.ap[:, :], scale,
                             P_l2b.ap[:, :TILE], ALU.mult, ALU.mult),
                         wait=[s_mm] + cv.take(), inc=[s_dve])
                    cvt.mark(s_dve)
                    P_l2b.mark(s_dve)
                    p.dma("gpsimd",
                          out_dram[mc * 128:(mc + 1) * 128, tok0:tok0 + TILE],
                          cv.ap[:, :], wait=[(s_dve, s_dve.v)],
                          inc=[(cv.st(p), 16)])
                    cv.mark(cv.st(p))

        conv_one(qpre, 8, 0, io["q_s"], True, "q")
        conv_one(kpre, 8, 32, io["k_s"], True, "k")
        conv_one(vpre, 16, 64, io["v_s"], False, "v")

    p.barrier("a_end")
    pctx.close()



def emit_stage_b(p, ctx, C, cfg, io):
    from contextlib import ExitStack
    pctx = ExitStack()
    nc = p.nc
    Tl, TB = cfg["Tl"], cfg["TB"]
    NB = Tl // TB
    NCH = TB // CHUNK

    s_ld = p.sem("b_ld")
    s_mm = p.sem("b_mm")
    s_dve = p.sem("b_dve")
    s_act = p.sem("b_act")
    s_st = p.sem("b_st")

    onw_row = ctx.enter_context(nc.sbuf_tensor("b_onwr", [1, 128], F32))
    onwB = ctx.enter_context(nc.sbuf_tensor("b_onwB", [128, 128], F32))
    zrow = ctx.enter_context(nc.sbuf_tensor("b_zrow", [1, 128], F32))
    P_tmp = Slot(pctx.enter_context(nc.psum_tensor("b_ptmp", [128, 512], F32)))
    s_ld0 = p.sem("b_ld0")
    p.dma("sync", onw_row[:, :], io["L0_attn_a_o_norm_w"].rearrange("(o d) -> o d", o=1),
          inc=[(s_ld0, 16)])
    p.do("vector", lambda e: e.memset(zrow[:, :], 0.0), inc=[s_dve])
    p.do("tensor", lambda e: e.matmul(
        P_tmp.ap[:, 0:128], C.ones_row_f[:, :], onw_row[:, :],
        start=True, stop=True), wait=[s_ld0], inc=[s_mm])
    p.do("scalar", lambda e: e.activation(onwB[:, :], P_tmp.ap[:, 0:128], AF.Copy),
         wait=[s_mm], inc=[s_act])
    P_tmp.mark(s_act)

    S_q = Slot(ctx.enter_context(nc.sbuf_tensor("b_q", [64, TB], BF16)))
    S_k = Slot(ctx.enter_context(nc.sbuf_tensor("b_k", [64, TB], BF16)))
    S_v = Slot(ctx.enter_context(nc.sbuf_tensor("b_v", [128, TB], BF16)))
    S_gs = Slot(ctx.enter_context(nc.sbuf_tensor("b_gs", [128, TB], BF16)))
    S_gr = Slot(ctx.enter_context(nc.sbuf_tensor("b_grow", [1, TB], F32)))
    S_br = Slot(ctx.enter_context(nc.sbuf_tensor("b_brow", [1, TB], F32)))

    Sst = [ctx.enter_context(nc.sbuf_tensor(f"b_S{i}", [64, 128], F32))
           for i in range(2)]
    S0b = ctx.enter_context(nc.sbuf_tensor("b_S0b", [64, 128], BF16))

    gc_row = ctx.enter_context(nc.sbuf_tensor("b_gc", [1, 128], F32))
    tcol = ctx.enter_context(nc.sbuf_tensor("b_tcol", [128, 2], F32))
    grb = ctx.enter_context(nc.sbuf_tensor("b_grb", [128, 128], F32))
    f0 = ctx.enter_context(nc.sbuf_tensor("b_f0", [128, 128], F32))
    fm_ = ctx.enter_context(nc.sbuf_tensor("b_fm", [128, 128], F32))
    d1 = ctx.enter_context(nc.sbuf_tensor("b_d1", [128, 128], F32))
    g2 = ctx.enter_context(nc.sbuf_tensor("b_g2", [128, 128], F32))
    d2 = ctx.enter_context(nc.sbuf_tensor("b_d2", [128, 128], F32))
    m1a = ctx.enter_context(nc.sbuf_tensor("b_m1a", [128, 128], F32))
    Mb = ctx.enter_context(nc.sbuf_tensor("b_M", [128, 128], BF16))
    MTb = ctx.enter_context(nc.sbuf_tensor("b_MT", [128, 128], BF16))
    Acur = [ctx.enter_context(nc.sbuf_tensor(f"b_A{i}", [128, 128], BF16))
            for i in range(2)]
    Bfac = [ctx.enter_context(nc.sbuf_tensor(f"b_B{j}", [128, 128], BF16))
            for j in range(7)]
    Yb = [ctx.enter_context(nc.sbuf_tensor(f"b_Y{i}", [128, 128], BF16))
          for i in range(2)]
    Kb = ctx.enter_context(nc.sbuf_tensor("b_Kb", [64, 128], BF16))
    Qb = ctx.enter_context(nc.sbuf_tensor("b_Qb", [64, 128], BF16))
    Kd = ctx.enter_context(nc.sbuf_tensor("b_Kd", [64, 128], BF16))
    Kdt = ctx.enter_context(nc.sbuf_tensor("b_Kdt", [128, 64], BF16))
    Vsb = ctx.enter_context(nc.sbuf_tensor("b_Vsb", [128, 128], BF16))
    w2b = ctx.enter_context(nc.sbuf_tensor("b_w2", [128, 128], BF16))
    r1 = ctx.enter_context(nc.sbuf_tensor("b_r1", [128, 128], F32))
    sqo = ctx.enter_context(nc.sbuf_tensor("b_sqo", [128, 128], F32))
    ms = ctx.enter_context(nc.sbuf_tensor("b_ms", [128, 1], F32))
    o1 = ctx.enter_context(nc.sbuf_tensor("b_o1", [128, 128], F32))
    o2 = ctx.enter_context(nc.sbuf_tensor("b_o2", [128, 128], BF16))
    S_oo = Slot(ctx.enter_context(nc.sbuf_tensor("b_oout", [128, 128], BF16)))
    drow = ctx.enter_context(nc.sbuf_tensor("b_drow", [1, 128], F32))
    brf = ctx.enter_context(nc.sbuf_tensor("b_brf", [1, 128], F32))
    bc1 = ctx.enter_context(nc.sbuf_tensor("b_bc1", [1, 1], F32))
    bc64 = ctx.enter_context(nc.sbuf_tensor("b_bc64", [64, 1], F32))

    pA = Slot(pctx.enter_context(nc.psum_tensor("b_pA", [128, 512], F32)))
    pG = Slot(pctx.enter_context(nc.psum_tensor("b_pG", [128, 512], F32)))
    pH = Slot(pctx.enter_context(nc.psum_tensor("b_pH", [128, 512], F32)))
    pC = Slot(pctx.enter_context(nc.psum_tensor("b_pC", [128, 512], F32)))
    pD = Slot(pctx.enter_context(nc.psum_tensor("b_pD", [128, 512], F32)))
    pE = Slot(pctx.enter_context(nc.psum_tensor("b_pE", [128, 512], F32)))
    pF = Slot(pctx.enter_context(nc.psum_tensor("b_pF", [128, 1024], BF16)))

    def A(fn, wait=(), inc=()):
        p.do("scalar", fn, wait=wait, inc=inc)

    for b in range(NB):
        for h in range(NH):
            for (sl_, dram, r0, nr) in (
                    (S_q, io["q_s"], 64 * h, 64), (S_k, io["k_s"], 64 * h, 64),
                    (S_v, io["v_s"], 128 * h, 128), (S_gs, io["gsil_s"], 128 * h, 128)):
                p.dma("sync", sl_.ap[:, :], dram[r0:r0 + nr, b * TB:(b + 1) * TB],
                      wait=sl_.take(), inc=[(s_ld, 16)])
            p.dma("sync", S_gr.ap[:, :], io["g_s"][h:h + 1, b * TB:(b + 1) * TB],
                  wait=S_gr.take(), inc=[(s_ld, 16)])
            p.dma("sync", S_br.ap[:, :], io["beta_s"][h:h + 1, b * TB:(b + 1) * TB],
                  wait=S_br.take(), inc=[(s_ld, 16)])
            ld_done = s_ld.v

            p.do("vector", lambda e: e.memset(Sst[0][:, :], 0.0),
                 wait=[(s_mm, s_mm.v), (s_act, s_act.v), (s_dve, s_dve.v)],
                 inc=[s_dve])

            for cc in range(NCH):
                BCUT = cfg.get("bcut", 99)
                sl = slice(cc * CHUNK, (cc + 1) * CHUNK)
                cur, nxt = Sst[cc % 2], Sst[(cc + 1) % 2]

                if BCUT < 1:
                    continue
                import os as _os
                BVAR = _os.environ.get("BVAR", "")
                if BVAR == "noscan":
                    p.do("vector", lambda e, sl=sl: e.tensor_copy(
                        gc_row[:, :], S_gr.ap[:, sl]),
                        wait=[(s_ld, ld_done), (s_act, s_act.v)], inc=[s_dve])
                else:
                    p.do("vector", lambda e, sl=sl: e.tensor_tensor_scan(
                        gc_row[:, :], S_gr.ap[:, sl], zrow[:, :], 0.0,
                        ALU.add, ALU.add), wait=[(s_ld, ld_done), (s_act, s_act.v)],
                        inc=[s_dve])
                p.do("tensor", lambda e: e.matmul(
                    pG.ap[:, 0:128], gc_row[:, :], C.ones_row_f[0:1, :],
                    start=True, stop=True),
                    wait=[s_dve] + pG.take(), inc=[s_mm])
                p.do("tensor", lambda e, sl=sl: e.matmul(
                    pG.ap[:, 128:256], S_br.ap[:, sl], C.ones_row_f[0:1, :],
                    start=True, stop=True),
                    wait=[(s_ld, ld_done)], inc=[s_mm])
                p.do("tensor", lambda e: e.matmul(
                    pG.ap[:, 256:384], C.ones_row_f[:, :], gc_row[:, :],
                    start=True, stop=True), inc=[s_mm])
                A(lambda e: e.activation(tcol[:, 0:1], pG.ap[:, 0:1], AF.Copy),
                  wait=[s_mm], inc=[s_act])
                A(lambda e: e.activation(tcol[:, 1:2], pG.ap[:, 128:129], AF.Copy),
                  wait=[(s_mm, s_mm.v)], inc=[s_act])
                A(lambda e: e.activation(grb[:, :], pG.ap[:, 256:384], AF.Copy),
                  wait=[(s_mm, s_mm.v)], inc=[s_act])
                pG.mark(s_act)
                if BCUT < 2:
                    continue
                p.do("vector", lambda e: e.tensor_scalar(
                    f0[:, :], grb[:, :], tcol[:, 0:1], None, ALU.subtract),
                    wait=[s_act], inc=[s_dve])
                p.do("vector", lambda e: e.tensor_scalar_max(fm_[:, :], f0[:, :], 0.0),
                     wait=[s_dve], inc=[s_dve])
                A(lambda e: e.activation(d1[:, :], fm_[:, :], AF.Exp, scale=-1.0),
                  wait=[s_dve], inc=[s_act])
                p.do("vector", lambda e: e.tensor_add(g2[:, :], f0[:, :],
                                                      C.negmask[:, :]),
                     wait=[(s_dve, s_dve.v)], inc=[s_dve])
                A(lambda e: e.activation(d2[:, :], g2[:, :], AF.Exp),
                  wait=[s_dve], inc=[s_act])
                if BCUT < 3:
                    continue
                p.do("tensor", lambda e, sl=sl: e.matmul(
                    pA.ap[:, 0:128], S_k.ap[:, sl], S_k.ap[:, sl],
                    start=True, stop=True),
                    wait=[(s_ld, ld_done)] + pA.take(), inc=[s_mm])
                p.do("tensor", lambda e, sl=sl: e.matmul(
                    pA.ap[:, 128:256], S_k.ap[:, sl], S_q.ap[:, sl],
                    start=True, stop=True), inc=[s_mm])
                p.do("vector", lambda e: e.tensor_mul(m1a[:, :], pA.ap[:, 0:128],
                                                      d1[:, :]),
                     wait=[s_mm, s_act], inc=[s_dve])
                p.do("vector", lambda e: e.scalar_tensor_tensor(
                    Mb[:, :], m1a[:, :], tcol[:, 1:2], C.slmask[:, :],
                    ALU.mult, ALU.mult), wait=[s_dve], inc=[s_dve])
                p.do("tensor", lambda e: e.transpose(
                    pF.ap[:, 0:128], Mb[:, :], C.id128[:, :]),
                    wait=[s_dve] + pF.take(), inc=[s_mm])
                p.do("vector", lambda e: e.tensor_copy(MTb[:, :], pF.ap[:, 0:128]),
                     wait=[s_mm], inc=[s_dve])
                pF.mark(s_dve)
                if BCUT < 4:
                    continue
                p.do("vector", lambda e: e.tensor_copy(Bfac[0][:, :], MTb[:, :]),
                     wait=[s_dve], inc=[s_dve])
                p.do("vector", lambda e: e.tensor_copy(Acur[0][:, :], Mb[:, :]),
                     wait=[(s_dve, s_dve.v)], inc=[s_dve])
                for j in range(1, 7):
                    ca, na = Acur[(j + 1) % 2], Acur[j % 2]
                    cb = Bfac[j - 1]
                    p.do("tensor", lambda e, ca=ca, cb=cb: e.matmul(
                        pC.ap[:, 0:128], cb[:, :], ca[:, :],
                        start=True, stop=True),
                        wait=[(s_dve, s_dve.v)] + pC.take(), inc=[s_mm])
                    p.do("tensor", lambda e, ca=ca, cb=cb: e.matmul(
                        pC.ap[:, 128:256], ca[:, :], cb[:, :],
                        start=True, stop=True), inc=[s_mm])
                    p.do("vector", lambda e, na=na: e.tensor_copy(
                        na[:, :], pC.ap[:, 0:128]), wait=[s_mm], inc=[s_dve])
                    p.do("vector", lambda e, j=j: e.tensor_copy(
                        Bfac[j][:, :], pC.ap[:, 128:256]), inc=[s_dve])
                    pC.mark(s_dve)
                if BCUT < 5:
                    continue
                A(lambda e: e.activation(brf[:, :], gc_row[:, :], AF.Exp),
                  wait=[(s_dve, s_dve.v)], inc=[s_act])
                p.do("tensor", lambda e: e.matmul(
                    pH.ap[0:64, 0:128], C.ones_row_f[:, 0:64], brf[:, :],
                    start=True, stop=True), wait=[s_act] + pH.take(), inc=[s_mm])
                p.do("vector", lambda e, sl=sl: e.tensor_mul(
                    Kb[:, :], S_k.ap[:, sl], pH.ap[0:64, 0:128]),
                    wait=[s_mm], inc=[s_dve])
                p.do("vector", lambda e, sl=sl: e.tensor_mul(
                    Qb[:, :], S_q.ap[:, sl], pH.ap[0:64, 0:128]),
                    wait=[(s_mm, s_mm.v)], inc=[s_dve])
                pH.mark(s_dve)
                if cc == 0:
                    A(lambda e, cur=cur: e.activation(S0b[:, :], cur[:, :], AF.Copy),
                      wait=[(s_dve, s_dve.v)], inc=[s_act])
                else:
                    A(lambda e, cur=cur: e.activation(S0b[:, :], cur[:, :], AF.Copy),
                      wait=[(s_dve, s_dve.v)], inc=[s_act])
                p.do("tensor", lambda e: e.matmul(
                    pE.ap[:, 0:128], Kb[:, :], S0b[:, :], start=True, stop=True),
                    wait=[s_dve, s_act] + pE.take(), inc=[s_mm])
                p.do("tensor", lambda e, sl=sl: e.transpose(
                    pF.ap[:, 192:320], S_v.ap[:, sl], C.id128[:, :]),
                    wait=pF.take(), inc=[s_mm])
                p.do("vector", lambda e: e.tensor_copy(Vsb[:, :], pF.ap[:, 192:320]),
                     wait=[s_mm], inc=[s_dve])
                pF.mark(s_dve)
                p.do("vector", lambda e: e.tensor_sub(r1[:, :], Vsb[:, :],
                                                      pE.ap[:, 0:128]),
                     wait=[(s_mm, s_mm.v), s_dve], inc=[s_dve])
                pE.mark(s_dve)
                p.do("vector", lambda e: e.tensor_scalar(
                    Yb[0][:, :], r1[:, :], tcol[:, 1:2], None, ALU.mult),
                    wait=[s_dve], inc=[s_dve])
                for ji, j in enumerate(range(6, -1, -1)):
                    yc, yn = Yb[ji % 2], Yb[(ji + 1) % 2]
                    p.do("tensor", lambda e, j=j, yc=yc: e.matmul(
                        pD.ap[:, 0:128], Bfac[j][:, :], yc[:, :],
                        start=True, stop=True),
                        wait=[(s_dve, s_dve.v)] + pD.take(), inc=[s_mm])
                    p.do("vector", lambda e, yc=yc, yn=yn: e.tensor_sub(
                        yn[:, :], yc[:, :], pD.ap[:, 0:128]),
                        wait=[s_mm], inc=[s_dve])
                    pD.mark(s_dve)
                U = Yb[7 % 2]
                if BCUT < 6:
                    continue
                p.do("vector", lambda e: e.tensor_mul(w2b[:, :], pA.ap[:, 128:256],
                                                      d2[:, :]),
                     wait=[(s_act, s_act.v), (s_mm, s_mm.v)], inc=[s_dve])
                pA.mark(s_dve)
                p.do("tensor", lambda e: e.matmul(
                    pE.ap[:, 128:256], w2b[:, :], U[:, :], start=True, stop=False),
                    wait=[(s_dve, s_dve.v)], inc=[s_mm])
                p.do("tensor", lambda e: e.matmul(
                    pE.ap[:, 128:256], Qb[:, :], S0b[:, :], start=False, stop=True),
                    inc=[s_mm])
                A(lambda e: e.activation(sqo[:, :], pE.ap[:, 128:256], AF.Square),
                  wait=[s_mm], inc=[s_act])
                p.do("vector", lambda e: e.reduce_sum(ms[:, :], sqo[:, :], axis=AX),
                     wait=[s_act], inc=[s_dve])
                A(lambda e: e.activation(ms[:, :], ms[:, :], AF.Ln,
                                         scale=1.0 / DV, bias=EPS),
                  wait=[s_dve], inc=[s_act])
                A(lambda e: e.activation(ms[:, :], ms[:, :], AF.Exp, scale=-0.5),
                  wait=[s_act], inc=[s_act])
                p.do("vector", lambda e: e.tensor_scalar(
                    o1[:, :], pE.ap[:, 128:256], ms[:, 0:1], None, ALU.mult),
                    wait=[s_act], inc=[s_dve])
                pE.mark(s_dve)
                p.do("vector", lambda e: e.tensor_mul(o2[:, :], o1[:, :], onwB[:, :]),
                     wait=[s_dve], inc=[s_dve])
                p.do("tensor", lambda e: e.transpose(
                    pF.ap[:, 0:128], o2[:, :], C.id128[:, :]),
                    wait=[s_dve] + pF.take(), inc=[s_mm])
                p.do("vector", lambda e, sl=sl: e.tensor_mul(
                    S_oo.ap[:, :], pF.ap[:, 0:128], S_gs.ap[:, sl]),
                    wait=[s_mm] + S_oo.take(), inc=[s_dve])
                pF.mark(s_dve)
                p.dma("gpsimd",
                      io["ofm_s"][128 * h:128 * (h + 1),
                                  b * TB + cc * CHUNK: b * TB + (cc + 1) * CHUNK],
                      S_oo.ap[:, :], wait=[(s_dve, s_dve.v)],
                      inc=[(S_oo.st(p), 16)])
                S_oo.mark(S_oo.st(p))
                if BCUT < 7:
                    continue
                A(lambda e: e.activation(drow[:, :], gc_row[:, :], AF.Exp,
                                         scale=-1.0, bias=gc_row[:, 127:128]),
                  wait=[(s_dve, s_dve.v)], inc=[s_act])
                if BCUT < 7.2:
                    continue
                BV2 = _os.environ.get("BVAR2", "").split(",")
                if "nodrow" not in BV2:
                    pass
                p.do("tensor", lambda e: e.matmul(
                    pH.ap[0:64, 128:256], C.ones_row_f[:, 0:64], drow[:, :],
                    start=True, stop=True), wait=[s_act] + pH.take(), inc=[s_mm])
                p.do("vector", lambda e, sl=sl: e.tensor_mul(
                    Kd[:, :], S_k.ap[:, sl], pH.ap[0:64, 128:256]),
                    wait=[s_mm], inc=[s_dve])
                pH.mark(s_dve)
                A(lambda e: e.activation(bc64[:, :],
                                         pG.ap[0:64, 256 + CHUNK - 1:256 + CHUNK],
                                         AF.Exp),
                  wait=[(s_mm, s_mm.v)], inc=[s_act])
                pG.mark(s_act)
                if BCUT < 7.4:
                    continue
                p.do("tensor", lambda e: e.transpose(
                    pF.ap[:, 128:192], Kd[:, :], C.id128[0:64, 0:64]),
                    wait=[s_dve] + pF.take(), inc=[s_mm])
                p.do("vector", lambda e: e.tensor_copy(Kdt[:, :], pF.ap[:, 128:192]),
                     wait=[s_mm], inc=[s_dve])
                pF.mark(s_dve)
                if BCUT < 7.6:
                    continue
                p.do("tensor", lambda e: e.matmul(
                    pE.ap[0:64, 256:384], Kdt[:, :], U[:, :],
                    start=True, stop=True),
                    wait=[(s_dve, s_dve.v)] + pE.take(), inc=[s_mm])
                if BCUT < 7.8:
                    continue
                p.do("vector", lambda e, cur=cur, nxt=nxt: e.scalar_tensor_tensor(
                    nxt[:, :], cur[:, :], bc64[:, 0:1], pE.ap[0:64, 256:384],
                    ALU.mult, ALU.add), wait=[s_mm, s_act], inc=[s_dve])
                pE.mark(s_dve)

            for sl_ in (S_q, S_k, S_v, S_gs, S_gr, S_br):
                sl_.mark(s_dve)
                sl_.mark(s_mm)

    p.barrier("b_end")
    pctx.close()



def emit_load_weights(p, ctx, specs, tag):
    nc = p.nc
    wld_sems = [p.sem(f"wld_{tag}0"), p.sem(f"wld_{tag}1")]
    s_wcv = p.sem(f"wcv_{tag}")
    with_ctx = [ctx.enter_context(nc.sbuf_tensor(f"wstg_{tag}{i}", [128, 1024], F32))
                for i in range(2)]
    stg = Rot(with_ctx)
    for (dram, sb, kdim, mdim) in specs:
        for kc in range(cdiv(kdim, 128)):
            rows = min(128, kdim - kc * 128)
            for m0 in range(0, mdim, 1024):
                mw = min(1024, mdim - m0)
                sl = stg.next()
                ws = wld_sems[(stg.i - 1) % 2]
                p.dma("sync", sl.ap[:rows, :mw],
                      dram[kc * 128:kc * 128 + rows, m0:m0 + mw],
                      wait=sl.take(), inc=[(ws, 16)])
                p.do("gpsimd", lambda e, sl=sl, kc=kc, rows=rows, sb=sb,
                     mdim=mdim, m0=m0, mw=mw:
                     e.tensor_copy(sb[:rows, kc * mdim + m0:kc * mdim + m0 + mw],
                                   sl.ap[:rows, :mw]),
                     wait=[ws], inc=[s_wcv])
                sl.mark(s_wcv)


def emit_norm_fm(p, C, sems, src_slots, src_stride, TILE, out_fn, kind,
                 stat, P_s1, P_bc, R_sq, wvec=None, ln_out=None):
    raise NotImplementedError



def emit_stage_c(p, ctx, C, cfg, io):
    nc = p.nc
    reuse = cfg.get("reuse", True)
    Tl, TILE = cfg["Tl"], cfg["TILE"]
    ntiles = Tl // TILE

    s_ld = p.sem("c_ld")
    s_mm = p.sem("c_mm")
    s_dve = p.sem("c_dve")
    s_act = p.sem("c_act")
    s_st = p.sem("c_st")

    from contextlib import ExitStack
    with ExitStack() as _c1:
        c1 = _c1 if reuse else ctx
        pctx = _c1
        w_o = c1.enter_context(nc.sbuf_tensor("w_o", [128, 16 * D], BF16))
        emit_load_weights(p, c1, [(io["L0_attn_a_o_w"], w_o, VD, D)], "c1")
        postw = c1.enter_context(nc.sbuf_tensor("c_postw", [128, 8], F32))
        p.dma("sync", postw[:, :], io["pre_postw"][:, :], inc=[(s_ld, 16)])
        p.barrier("c1_w")

        S_of = Slot(c1.enter_context(nc.sbuf_tensor("c_of", [128, 16 * TILE], BF16)))
        S_l0 = Slot(c1.enter_context(nc.sbuf_tensor("c_l0", [128, 8 * TILE], F32)))
        S_x1 = Slot(c1.enter_context(nc.sbuf_tensor("c_x1", [128, 8 * TILE], F32)))
        S_h2 = Slot(c1.enter_context(nc.sbuf_tensor("c_h2", [128, 8 * TILE], BF16)))
        R_sq = Rot([c1.enter_context(nc.sbuf_tensor(f"c_sq{i}", [128, TILE], BF16))
                    for i in range(8)])
        stat = c1.enter_context(nc.sbuf_tensor("c_stat", [1, TILE], F32))
        R_pm = Rot([pctx.enter_context(nc.psum_tensor(f"c_pm{i}", [128, 512], F32))
                    for i in range(2)])
        P_s1 = Slot(pctx.enter_context(nc.psum_tensor("c_ps1", [1, 512], F32)))
        P_bc = Slot(pctx.enter_context(nc.psum_tensor("c_pb", [128, 512], F32)))

        for t in range(ntiles):
            tok0 = t * TILE
            ofw = S_of.take()
            for c in range(16):
                p.dma("sync", S_of.ap[:, c * TILE:(c + 1) * TILE],
                      io["ofm_s"][c * 128:(c + 1) * 128, tok0:tok0 + TILE],
                      wait=(ofw if c == 0 else []), inc=[(s_ld, 16)])
            l0w = S_l0.take()
            for c in range(8):
                p.dma("sync", S_l0.ap[:, c * TILE:(c + 1) * TILE],
                      io["ln0_s"][c * 128:(c + 1) * 128, tok0:tok0 + TILE],
                      wait=(l0w if c == 0 else []), inc=[(s_ld, 16)])
            ld_done = s_ld.v
            x1w = S_x1.take()
            for mc in range(8):
                pm = R_pm.next()
                for kc in range(16):
                    p.do("tensor", lambda e, mc=mc, kc=kc, pm=pm: e.matmul(
                        pm.ap[:, :TILE],
                        w_o[:, kc * D + mc * 128: kc * D + mc * 128 + 128],
                        S_of.ap[:, kc * TILE:(kc + 1) * TILE],
                        start=(kc == 0), stop=(kc == 15)),
                        wait=[(s_ld, ld_done)] + (pm.take() if kc == 0 else []),
                        inc=([s_mm] if kc == 15 else []))
                p.do("vector", lambda e, mc=mc, pm=pm: e.tensor_add(
                    S_x1.ap[:, mc * TILE:(mc + 1) * TILE],
                    S_l0.ap[:, mc * TILE:(mc + 1) * TILE], pm.ap[:, :TILE]),
                    wait=[s_mm] + (x1w if mc == 0 else []), inc=[s_dve])
                pm.mark(s_dve)
            S_of.mark(s_mm)
            S_l0.mark(s_dve)
            for c in range(8):
                p.dma("gpsimd", io["x1_s"][c * 128:(c + 1) * 128, tok0:tok0 + TILE],
                      S_x1.ap[:, c * TILE:(c + 1) * TILE],
                      wait=[(s_dve, s_dve.v)], inc=[(S_x1.st(p), 16)])
            S_x1.mark(S_x1.st(p))
            for c in range(8):
                sq = R_sq.next()
                p.do("scalar", lambda e, c=c, sq=sq: e.activation(
                    sq.ap[:, :], S_x1.ap[:, c * TILE:(c + 1) * TILE], AF.Square),
                    wait=[(s_dve, s_dve.v)] + sq.take(), inc=[s_act])
                p.do("tensor", lambda e, c=c, sq=sq: e.matmul(
                    P_s1.ap[:, :TILE], C.ones_col_b[:, :], sq.ap[:, :],
                    start=(c == 0), stop=(c == 7)),
                    wait=[s_act] + (P_s1.take() if c == 0 else []),
                    inc=([s_mm] if c == 7 else []))
            R_sq.mark_all(s_mm)
            stw = stat
            p.do("scalar", lambda e: e.activation(
                stat[:, :TILE], P_s1.ap[:, :TILE], AF.Ln, scale=1.0 / D,
                bias=EPS), wait=[s_mm, (s_dve, s_dve.v)], inc=[s_act])
            P_s1.mark(s_act)
            p.do("scalar", lambda e: e.activation(
                stat[:, :TILE], stat[:, :TILE], AF.Exp, scale=-0.5),
                wait=[s_act], inc=[s_act])
            p.do("tensor", lambda e: e.matmul(
                P_bc.ap[:, :TILE], C.ones_row_f[:, :], stat[:, :TILE],
                start=True, stop=True), wait=[s_act] + P_bc.take(), inc=[s_mm])
            h2w = S_h2.take()
            for c in range(8):
                p.do("vector", lambda e, c=c: e.scalar_tensor_tensor(
                    S_h2.ap[:, c * TILE:(c + 1) * TILE],
                    S_x1.ap[:, c * TILE:(c + 1) * TILE],
                    postw[:, c:c + 1], P_bc.ap[:, :TILE], ALU.mult, ALU.mult),
                    wait=[s_mm] + (h2w if c == 0 else []), inc=[s_dve])
            P_bc.mark(s_dve)
            S_x1.mark(s_dve)
            for c in range(8):
                p.dma("gpsimd", io["h2_s"][c * 128:(c + 1) * 128, tok0:tok0 + TILE],
                      S_h2.ap[:, c * TILE:(c + 1) * TILE],
                      wait=[(s_dve, s_dve.v)], inc=[(S_h2.st(p), 16)])
            S_h2.mark(S_h2.st(p))
        p.barrier("c1_end")

    NFC = cdiv(FF, 128)
    with ExitStack() as _c2:
        c2 = _c2 if reuse else ctx
        pctx = _c2
        w_gt = c2.enter_context(nc.sbuf_tensor("w_gt", [128, 8 * FF], BF16))
        w_up = c2.enter_context(nc.sbuf_tensor("w_up", [128, 8 * FF], BF16))
        emit_load_weights(p, c2, [(io["L0_attn_a_gate_w"], w_gt, D, FF),
                                  (io["L0_attn_a_up_w"], w_up, D, FF)], "c2")
        p.barrier("c2_w")
        S_h2q = Slot(c2.enter_context(nc.sbuf_tensor("c2_h2", [128, 8 * TILE], BF16)))
        S_g1 = Slot(c2.enter_context(nc.sbuf_tensor("c2_g1", [128, NFC * TILE], BF16)))
        R_pmq = Rot([pctx.enter_context(nc.psum_tensor(f"c2_pm{i}", [128, 512], F32))
                    for i in range(2)])
        for t in range(ntiles):
            tok0 = t * TILE
            h2wq = S_h2q.take()
            for c in range(8):
                p.dma("sync", S_h2q.ap[:, c * TILE:(c + 1) * TILE],
                      io["h2_s"][c * 128:(c + 1) * 128, tok0:tok0 + TILE],
                      wait=(h2wq if c == 0 else []), inc=[(s_ld, 16)])
            ld_done = s_ld.v
            g1w = S_g1.take()
            for mc in range(NFC):
                mrows = min(128, FF - mc * 128)
                pm = R_pmq.next()
                for kc in range(8):
                    p.do("tensor", lambda e, mc=mc, kc=kc, pm=pm, mrows=mrows:
                         e.matmul(
                             pm.ap[:mrows, :TILE],
                             w_gt[:, kc * FF + mc * 128: kc * FF + mc * 128 + mrows],
                             S_h2q.ap[:, kc * TILE:(kc + 1) * TILE],
                             start=(kc == 0), stop=(kc == 7)),
                         wait=[(s_ld, ld_done)] + (pm.take() if kc == 0 else []),
                         inc=([s_mm] if kc == 7 else []))
                p.do("scalar", lambda e, mc=mc, pm=pm, mrows=mrows: e.activation(
                    S_g1.ap[:mrows, mc * TILE:mc * TILE + TILE], pm.ap[:mrows, :TILE],
                    AF.Silu), wait=[s_mm] + (g1w if mc == 0 else []), inc=[s_act])
                pm.mark(s_act)
            for mc in range(NFC):
                mrows = min(128, FF - mc * 128)
                pm = R_pmq.next()
                for kc in range(8):
                    p.do("tensor", lambda e, mc=mc, kc=kc, pm=pm, mrows=mrows:
                         e.matmul(
                             pm.ap[:mrows, :TILE],
                             w_up[:, kc * FF + mc * 128: kc * FF + mc * 128 + mrows],
                             S_h2q.ap[:, kc * TILE:(kc + 1) * TILE],
                             start=(kc == 0), stop=(kc == 7)),
                         wait=[(s_act, s_act.v)] + (pm.take() if kc == 0 else []),
                         inc=([s_mm] if kc == 7 else []))
                p.do("vector", lambda e, mc=mc, pm=pm, mrows=mrows: e.tensor_mul(
                    S_g1.ap[:mrows, mc * TILE:mc * TILE + TILE],
                    S_g1.ap[:mrows, mc * TILE:mc * TILE + TILE],
                    pm.ap[:mrows, :TILE]), wait=[s_mm], inc=[s_dve])
                pm.mark(s_dve)
                p.dma("gpsimd",
                      io["mm_s"][mc * 128:mc * 128 + mrows, tok0:tok0 + TILE],
                      S_g1.ap[:mrows, mc * TILE:mc * TILE + TILE],
                      wait=[(s_dve, s_dve.v)], inc=[(S_g1.st(p), 16)])
            S_g1.mark(S_g1.st(p))
            S_h2q.mark(s_mm)
        p.barrier("c2_end")

    with ExitStack() as _c3:
        c3 = _c3 if reuse else ctx
        pctx = _c3
        w_dn = c3.enter_context(nc.sbuf_tensor("w_dn", [128, NFC * D], BF16))
        emit_load_weights(p, c3, [(io["L0_attn_a_down_w"], w_dn, FF, D)], "c3")
        gwt = c3.enter_context(nc.sbuf_tensor("c3_gwt", [128, 8 * NE], BF16))
        emit_load_weights(p, c3, [(io["gate_wT"], gwt, D, NE)], "c3b")
        p.barrier("c3_w")
        S_mmz = Slot(c3.enter_context(nc.sbuf_tensor("c3_mm", [128, NFC * TILE], BF16)))
        S_x1z = Slot(c3.enter_context(nc.sbuf_tensor("c3_x1", [128, 8 * TILE], F32)))
        S_xtz = Slot(c3.enter_context(nc.sbuf_tensor("c3_xt", [128, 8 * TILE], F32)))
        S_lnz = Slot(c3.enter_context(nc.sbuf_tensor("c3_ln", [128, 8 * TILE], BF16)))
        R_sqz = Rot([c3.enter_context(nc.sbuf_tensor(f"c3_sq{i}", [128, TILE], BF16))
                    for i in range(8)])
        statz = c3.enter_context(nc.sbuf_tensor("c3_statz", [1, 4 * TILE], F32))
        lgt = Slot(c3.enter_context(nc.sbuf_tensor("c3_lg", [NE, TILE], F32)))
        R_pmz = Rot([pctx.enter_context(nc.psum_tensor(f"c3_pm{i}", [128, 512], F32))
                    for i in range(2)])
        P_s1z = Slot(pctx.enter_context(nc.psum_tensor("c3_ps1", [1, 512], F32)))
        P_s2z = Slot(pctx.enter_context(nc.psum_tensor("c3_ps2", [1, 512], F32)))
        P_bcz = Slot(pctx.enter_context(nc.psum_tensor("c3_pb", [128, 512], F32)))
        P_lg = Slot(pctx.enter_context(nc.psum_tensor("c3_plg", [NE, 512], F32)))
        ST = TILE
        for t in range(ntiles):
            tok0 = t * TILE
            mmwz = S_mmz.take()
            for c in range(NFC):
                rows = min(128, FF - c * 128)
                p.dma("sync", S_mmz.ap[:rows, c * TILE:(c + 1) * TILE],
                      io["mm_s"][c * 128:c * 128 + rows, tok0:tok0 + TILE],
                      wait=(mmwz if c == 0 else []), inc=[(s_ld, 16)])
            x1wz = S_x1z.take()
            for c in range(8):
                p.dma("sync", S_x1z.ap[:, c * TILE:(c + 1) * TILE],
                      io["x1_s"][c * 128:(c + 1) * 128, tok0:tok0 + TILE],
                      wait=(x1wz if c == 0 else []), inc=[(s_ld, 16)])
            xtwz = S_xtz.take()
            for c in range(8):
                p.dma("sync", S_xtz.ap[:, c * TILE:(c + 1) * TILE],
                      io["x_fm"][c * 128:(c + 1) * 128, tok0:tok0 + TILE],
                      wait=(xtwz if c == 0 else []), inc=[(s_ld, 16)])
            ld_done = s_ld.v
            for mc in range(8):
                pm = R_pmz.next()
                for kc in range(NFC):
                    krows = min(128, FF - kc * 128)
                    p.do("tensor", lambda e, mc=mc, kc=kc, pm=pm, krows=krows:
                         e.matmul(
                             pm.ap[:, :TILE],
                             w_dn[:krows, kc * D + mc * 128: kc * D + mc * 128 + 128],
                             S_mmz.ap[:krows, kc * TILE:(kc + 1) * TILE],
                             start=(kc == 0), stop=(kc == NFC - 1)),
                         wait=[(s_ld, ld_done)] + (pm.take() if kc == 0 else []),
                         inc=([s_mm] if kc == NFC - 1 else []))
                p.do("vector", lambda e, mc=mc, pm=pm: e.tensor_add(
                    S_x1z.ap[:, mc * TILE:(mc + 1) * TILE],
                    S_x1z.ap[:, mc * TILE:(mc + 1) * TILE], pm.ap[:, :TILE]),
                    wait=[s_mm], inc=[s_dve])
                pm.mark(s_dve)
                p.do("vector", lambda e, mc=mc: e.tensor_add(
                    S_xtz.ap[:, mc * TILE:(mc + 1) * TILE],
                    S_xtz.ap[:, mc * TILE:(mc + 1) * TILE],
                    S_x1z.ap[:, mc * TILE:(mc + 1) * TILE]),
                    wait=[s_dve], inc=[s_dve])
            S_mmz.mark(s_mm)
            S_x1z.mark(s_dve)
            for c in range(8):
                p.dma("gpsimd", io["xo"][c * 128:(c + 1) * 128, tok0:tok0 + TILE],
                      S_xtz.ap[:, c * TILE:(c + 1) * TILE],
                      wait=[(s_dve, s_dve.v)], inc=[(S_xtz.st(p), 16)])
            S_xtz.mark(S_xtz.st(p))
            for c in range(8):
                p.do("tensor", lambda e, c=c: e.matmul(
                    P_s1z.ap[:, :TILE], C.ones_col_f[:, :],
                    S_xtz.ap[:, c * TILE:(c + 1) * TILE],
                    start=(c == 0), stop=(c == 7)),
                    wait=[(s_dve, s_dve.v)] + (P_s1z.take() if c == 0 else []),
                    inc=([s_mm] if c == 7 else []))
            for c in range(8):
                sq = R_sqz.next()
                p.do("scalar", lambda e, c=c, sq=sq: e.activation(
                    sq.ap[:, :], S_xtz.ap[:, c * TILE:(c + 1) * TILE], AF.Square),
                    wait=[(s_dve, s_dve.v)] + sq.take(), inc=[s_act])
                p.do("tensor", lambda e, c=c, sq=sq: e.matmul(
                    P_s2z.ap[:, :TILE], C.ones_col_b[:, :], sq.ap[:, :],
                    start=(c == 0), stop=(c == 7)),
                    wait=[s_act] + (P_s2z.take() if c == 0 else []),
                    inc=([s_mm] if c == 7 else []))
            R_sqz.mark_all(s_mm)
            p.do("vector", lambda e: e.tensor_scalar_mul(
                statz[:, 0 * ST:0 * ST + TILE], P_s1z.ap[:, :TILE], 1.0 / D),
                wait=[s_mm], inc=[s_dve])
            P_s1z.mark(s_dve)
            p.do("scalar", lambda e: e.activation(
                statz[:, 1 * ST:1 * ST + TILE], statz[:, 0 * ST:0 * ST + TILE],
                AF.Square), wait=[s_dve], inc=[s_act])
            p.do("vector", lambda e: e.scalar_tensor_tensor(
                statz[:, 2 * ST:2 * ST + TILE], P_s2z.ap[:, :TILE], 1.0 / D,
                statz[:, 1 * ST:1 * ST + TILE], ALU.mult, ALU.subtract),
                wait=[s_mm, s_act], inc=[s_dve])
            P_s2z.mark(s_dve)
            p.do("scalar", lambda e: e.activation(
                statz[:, 2 * ST:2 * ST + TILE], statz[:, 2 * ST:2 * ST + TILE],
                AF.Ln, bias=LN_EPS), wait=[s_dve], inc=[s_act])
            p.do("scalar", lambda e: e.activation(
                statz[:, 2 * ST:2 * ST + TILE], statz[:, 2 * ST:2 * ST + TILE],
                AF.Exp, scale=-0.5), wait=[s_act], inc=[s_act])
            p.do("tensor", lambda e: e.matmul(
                P_bcz.ap[:, :TILE], C.ones_row_f[:, :], statz[:, 0 * ST:0 * ST + TILE],
                start=True, stop=True), wait=[(s_act, s_act.v)] + P_bcz.take(),
                inc=[s_mm])
            lnwz = S_lnz.take()
            for c in range(8):
                p.do("vector", lambda e, c=c: e.tensor_sub(
                    S_lnz.ap[:, c * TILE:(c + 1) * TILE],
                    S_xtz.ap[:, c * TILE:(c + 1) * TILE], P_bcz.ap[:, :TILE]),
                    wait=[s_mm] + (lnwz if c == 0 else []), inc=[s_dve])
            P_bcz.mark(s_dve)
            S_xtz.mark(s_dve)
            p.do("tensor", lambda e: e.matmul(
                P_bcz.ap[:, :TILE], C.ones_row_f[:, :], statz[:, 2 * ST:2 * ST + TILE],
                start=True, stop=True), wait=[(s_dve, s_dve.v)] + P_bcz.take(),
                inc=[s_mm])
            for c in range(8):
                p.do("vector", lambda e, c=c: e.tensor_mul(
                    S_lnz.ap[:, c * TILE:(c + 1) * TILE],
                    S_lnz.ap[:, c * TILE:(c + 1) * TILE], P_bcz.ap[:, :TILE]),
                    wait=[s_mm], inc=[s_dve])
            P_bcz.mark(s_dve)
            for c in range(8):
                p.dma("gpsimd", io["lno"][c * 128:(c + 1) * 128, tok0:tok0 + TILE],
                      S_lnz.ap[:, c * TILE:(c + 1) * TILE],
                      wait=[(s_dve, s_dve.v)], inc=[(S_lnz.st(p), 16)])
            S_lnz.mark(S_lnz.st(p))
            for kc in range(8):
                p.do("tensor", lambda e, kc=kc: e.matmul(
                    P_lg.ap[:, :TILE], gwt[:, kc * NE:(kc + 1) * NE],
                    S_lnz.ap[:, kc * TILE:(kc + 1) * TILE],
                    start=(kc == 0), stop=(kc == 7)),
                    wait=[(s_dve, s_dve.v)] + (P_lg.take() if kc == 0 else []),
                    inc=([s_mm] if kc == 7 else []))
            S_lnz.mark(s_mm)
            lgw = lgt.take()
            p.do("scalar", lambda e: e.activation(
                lgt.ap[:, :TILE], P_lg.ap[:, :TILE], AF.Copy),
                wait=[s_mm] + lgw, inc=[s_act])
            P_lg.mark(s_act)
            p.dma("gpsimd", io["lgo"][:, tok0:tok0 + TILE], lgt.ap[:, :TILE],
                  wait=[s_act], inc=[(lgt.st(p), 16)])
            lgt.mark(lgt.st(p))
        p.barrier("c3_end")



A_WEIGHT_NAMES = ("L0_attn_a_q_w", "L0_attn_a_k_w", "L0_attn_a_v_w",
                  "L0_attn_a_b_w", "L0_attn_a_a_w", "L0_attn_a_g_w",
                  "L0_attn_a_o_w", "L0_attn_a_gate_w", "L0_attn_a_up_w",
                  "L0_attn_a_down_w")
A_VEC_NAMES = ("L0_attn_a_norm_w", "L0_attn_a_A_log", "L0_attn_a_dt_bias",
               "L0_attn_a_conv_q_w", "L0_attn_a_conv_k_w", "L0_attn_a_conv_v_w",
               "L0_attn_a_o_norm_w", "L0_attn_a_post_norm_w")
A_WSHAPES = {"L0_attn_a_q_w": (D, D), "L0_attn_a_k_w": (D, D),
             "L0_attn_a_v_w": (D, VD), "L0_attn_a_b_w": (D, NH),
             "L0_attn_a_a_w": (D, NH), "L0_attn_a_g_w": (D, VD),
             "L0_attn_a_o_w": (VD, D), "L0_attn_a_gate_w": (D, FF),
             "L0_attn_a_up_w": (D, FF), "L0_attn_a_down_w": (FF, D)}
A_VSHAPES = {"L0_attn_a_norm_w": (D,), "L0_attn_a_A_log": (NH,),
             "L0_attn_a_dt_bias": (NH,), "L0_attn_a_conv_q_w": (D, 4),
             "L0_attn_a_conv_k_w": (D, 4), "L0_attn_a_conv_v_w": (VD, 4),
             "L0_attn_a_o_norm_w": (DV,), "L0_attn_a_post_norm_w": (D,)}


def build_l1(cfg, stages=("A", "B", "C"), reuse=True):
    from contextlib import ExitStack
    Tl = cfg["Tl"]
    nc = bass.Bass("TRN2", target_bir_lowering=False, debug=False)
    io = {}
    io["x_fm"] = nc.dram_tensor("x_fm", [D, Tl], F32, kind="ExternalInput").ap()
    for n in A_WEIGHT_NAMES:
        io[n] = nc.dram_tensor(n, list(A_WSHAPES[n]), F32, kind="ExternalInput").ap()
    for n in A_VEC_NAMES:
        io[n] = nc.dram_tensor(n, list(A_VSHAPES[n]), F32, kind="ExternalInput").ap()
    io["gate_wT"] = nc.dram_tensor("gate_wT", [D, NE], F32, kind="ExternalInput").ap()
    io["pre_normw"] = nc.dram_tensor("pre_normw", [128, 8], F32, kind="ExternalInput").ap()
    io["pre_convw"] = nc.dram_tensor("pre_convw", [128, 128], F32, kind="ExternalInput").ap()
    io["pre_postw"] = nc.dram_tensor("pre_postw", [128, 8], F32, kind="ExternalInput").ap()
    consts = host_consts()
    for k in CONST_NAMES:
        arr = consts[k]
        dt = BF16 if arr.dtype == BF else F32
        io[k] = nc.dram_tensor(k, list(arr.shape), dt, kind="ExternalInput").ap()
    io["ln0_s"] = nc.dram_tensor("ln0_s", [D, Tl], F32).ap()
    io["q_s"] = nc.dram_tensor("q_s", [D, Tl], BF16).ap()
    io["k_s"] = nc.dram_tensor("k_s", [D, Tl], BF16).ap()
    io["v_s"] = nc.dram_tensor("v_s", [VD, Tl], BF16).ap()
    io["gsil_s"] = nc.dram_tensor("gsil_s", [VD, Tl], BF16).ap()
    io["beta_s"] = nc.dram_tensor("beta_s", [NH, Tl], F32).ap()
    io["g_s"] = nc.dram_tensor("g_s", [NH, Tl], F32).ap()
    io["ofm_s"] = nc.dram_tensor("ofm_s", [VD, Tl], BF16).ap()
    io["x1_s"] = nc.dram_tensor("x1_s", [D, Tl], F32).ap()
    io["h2_s"] = nc.dram_tensor("h2_s", [D, Tl], BF16).ap()
    io["mm_s"] = nc.dram_tensor("mm_s", [FF, Tl], BF16).ap()
    io["xo"] = nc.dram_tensor("xo", [D, Tl], F32, kind="ExternalOutput").ap()
    io["lno"] = nc.dram_tensor("lno", [D, Tl], BF16, kind="ExternalOutput").ap()
    io["lgo"] = nc.dram_tensor("lgo", [NE, Tl], F32, kind="ExternalOutput").ap()

    ctx = ExitStack()
    p = Prog(nc, ctx)
    s_cld = p.sem("cld")
    C = Consts(p, ctx, io, s_cld)
    p.do("sync", lambda e: e.nop(), wait=[(s_cld, s_cld.v)])
    p.barrier("consts")
    from contextlib import ExitStack as ES
    with ctx:
        if "A" in stages:
            with ES() as sctx:
                emit_stage_a(p, (sctx if reuse else ctx), C, cfg, io)
        if "B" in stages:
            with ES() as sctx:
                emit_stage_b2(p, (sctx if reuse else ctx), C, cfg, io)
        if "C" in stages:
            with ES() as sctx:
                emit_stage_c(p, (sctx if reuse else ctx), C, cfg, io)
        p.build_block()
    return nc


def l1_inputs(core_x_fm, weights, cfg):
    m = {"x_fm": np.ascontiguousarray(core_x_fm)}
    for n in A_WEIGHT_NAMES + A_VEC_NAMES:
        m[n] = np.ascontiguousarray(weights[n], dtype=np.float32)
    m["gate_wT"] = np.ascontiguousarray(weights["L0_ffn_gate_weight"].T,
                                        dtype=np.float32)
    pcw = np.zeros((128, 128), np.float32)
    for i, n in enumerate(("conv_q_w", "conv_k_w")):
        w = np.asarray(weights["L0_attn_a_" + n], np.float32)
        for c in range(8):
            pcw[:, i * 32 + c * 4: i * 32 + (c + 1) * 4] = w[c * 128:(c + 1) * 128]
    wv = np.asarray(weights["L0_attn_a_conv_v_w"], np.float32)
    for c in range(16):
        pcw[:, 64 + c * 4: 64 + (c + 1) * 4] = wv[c * 128:(c + 1) * 128]
    m["pre_convw"] = pcw
    m["pre_normw"] = np.ascontiguousarray(
        np.asarray(weights["L0_attn_a_norm_w"], np.float32).reshape(-1, 128).T)
    m["pre_postw"] = np.ascontiguousarray(
        np.asarray(weights["L0_attn_a_post_norm_w"], np.float32).reshape(-1, 128).T)
    m.update(host_consts())
    return m



def emit_kth(p, ctx, C, cfg, io, glg_name, olg_name, gates_dram):
    from contextlib import ExitStack
    nc = p.nc
    Tl = cfg["Tl"]
    NGC = cfg["NGC"]
    NITER = 50

    s_ld = p.sem("k_ld")
    s_mm = p.sem("k_mm")
    s_dve = p.sem("k_dve")
    s_act = p.sem("k_act")

    with ExitStack() as kctx:
        pctx = kctx
        glg = kctx.enter_context(nc.sbuf_tensor("k_glg", [128, NGC], F32))
        dump = kctx.enter_context(nc.sbuf_tensor("k_dump", [128, NGC], F32))
        olg = kctx.enter_context(nc.sbuf_tensor("k_olg", [NE, Tl], F32))
        kth = kctx.enter_context(nc.sbuf_tensor("k_kth", [128, 1], F32))
        rowm = kctx.enter_context(nc.sbuf_tensor("k_rowm", [128, 1], F32))
        rowc = kctx.enter_context(nc.sbuf_tensor("k_rowc", [128, 1], F32))
        tcur = kctx.enter_context(nc.sbuf_tensor("k_t", [1, 1], F32))
        tstep = kctx.enter_context(nc.sbuf_tensor("k_step", [1, 1], F32))
        sgn = kctx.enter_context(nc.sbuf_tensor("k_sgn", [1, 1], F32))
        st1 = kctx.enter_context(nc.sbuf_tensor("k_st1", [1, 1], F32))
        t64 = kctx.enter_context(nc.sbuf_tensor("k_t64", [128, 1], F32))
        mask = kctx.enter_context(nc.sbuf_tensor("k_mask", [NE, Tl], F32))
        gout = kctx.enter_context(nc.sbuf_tensor("k_gout", [NE, Tl], BF16))
        P_a = Slot(pctx.enter_context(nc.psum_tensor("k_pa", [128, 512], F32)))
        P_b = Slot(pctx.enter_context(nc.psum_tensor("k_pb", [1, 512], F32)))

        p.dma("sync", glg[:, :], io[glg_name][:, :], inc=[(s_ld, 16)])
        p.dma("sync", olg[:, :], io[olg_name][:, :], inc=[(s_ld, 16)])
        p.dma("sync", kth[:, :], io["c_kth"][:, :], inc=[(s_ld, 16)])

        p.do("vector", lambda e: e.tensor_reduce(
            rowm[:, :], glg[:, :], AX, ALU.max, apply_absolute_value=True),
            wait=[s_ld], inc=[s_dve])
        p.do("tensor", lambda e: e.matmul(
            P_b.ap[:, 0:1], rowm[:, :], C.ones_col_f[:, :],
            start=True, stop=True), wait=[s_dve], inc=[s_mm])
        p.do("scalar", lambda e: e.activation(
            tstep[:, :], P_b.ap[0:1, 0:1], AF.Copy, scale=1.0),
            wait=[s_mm], inc=[s_act])
        P_b.mark(s_act)
        p.do("vector", lambda e: e.memset(tcur[:, :], 0.0), inc=[s_dve])
        p.do("vector", lambda e: e.memset(t64[:, :], 0.0), inc=[s_dve])

        for it in range(NITER):
            p.do("vector", lambda e: e.tensor_scalar(
                dump[:, :], glg[:, :], t64[:, 0:1], None, ALU.is_lt, ALU.add,
                accum_out=rowc[:, :]), wait=[(s_dve, s_dve.v), (s_act, s_act.v)],
                inc=[s_dve])
            p.do("tensor", lambda e: e.matmul(
                P_b.ap[:, 0:1], rowc[:, :], C.ones_col_f[:, :],
                start=True, stop=True), wait=[s_dve] + P_b.take(), inc=[s_mm])
            p.do("scalar", lambda e: e.activation(
                sgn[:, :], P_b.ap[0:1, 0:1], AF.Sign, scale=-1.0,
                bias=kth[0:1, 0:1]), wait=[s_mm], inc=[s_act])
            P_b.mark(s_act)
            p.do("vector", lambda e: e.tensor_mul(st1[:, :], sgn[:, :],
                                                  tstep[:, :]),
                 wait=[s_act], inc=[s_dve])
            p.do("vector", lambda e: e.tensor_add(tcur[:, :], tcur[:, :],
                                                  st1[:, :]),
                 wait=[s_dve], inc=[s_dve])
            p.do("vector", lambda e: e.tensor_scalar_mul(
                tstep[:, :], tstep[:, :], 0.5), wait=[(s_dve, s_dve.v)],
                inc=[s_dve])
            p.do("tensor", lambda e: e.matmul(
                P_a.ap[:, 0:1], C.ones_row_f[:, :], tcur[:, :],
                start=True, stop=True), wait=[(s_dve, s_dve.v)] + P_a.take(),
                inc=[s_mm])
            p.do("scalar", lambda e: e.activation(
                t64[:, :], P_a.ap[:, 0:1], AF.Copy), wait=[s_mm], inc=[s_act])
            P_a.mark(s_act)

        p.do("vector", lambda e: e.tensor_scalar(
            mask[:, :], olg[:, :], t64[0:NE, 0:1], None, ALU.is_ge),
            wait=[(s_act, s_act.v)], inc=[s_dve])
        p.do("vector", lambda e: e.tensor_mul(gout[:, :], olg[:, :],
                                              mask[:, :]),
             wait=[s_dve], inc=[s_dve])
        s_gst = p.sem("k_gst")
        p.dma("gpsimd", io[gates_dram][:, :], gout[:, :],
              wait=[(s_dve, s_dve.v)], inc=[(s_gst, 16)])
        p.do("gpsimd", lambda e: e.nop(), wait=[(s_gst, s_gst.v)])
        p.barrier("kth_end")



def emit_ffn(p, ctx, C, cfg, io, ln_in, acc_out, gates_dram, tag):
    from contextlib import ExitStack
    nc = p.nc
    Tl, TILE = cfg["Tl"], cfg["TILE"]
    ntiles = Tl // TILE

    s_ld = p.sem(f"f_ld_{tag}")
    s_mm = p.sem(f"f_mm_{tag}")
    s_dve = p.sem(f"f_dve_{tag}")
    s_act = p.sem(f"f_act_{tag}")
    s_gb = p.sem(f"f_gb_{tag}")
    s_cv = p.sem(f"f_cv_{tag}")

    with ExitStack() as fctx:
        pctx = fctx
        W1S = Slot(fctx.enter_context(nc.sbuf_tensor(f"f_w1_{tag}", [128, 8 * FH], BF16)))
        W2S = Slot(fctx.enter_context(nc.sbuf_tensor(f"f_w2_{tag}", [128, 32 * D], BF16)))
        w1b, w2b = W1S.ap, W2S.ap
        hsb = fctx.enter_context(nc.sbuf_tensor(f"f_h_{tag}", [128, 32 * TILE // 1], BF16))
        lnt = Slot(fctx.enter_context(nc.sbuf_tensor(f"f_ln_{tag}", [128, 8 * TILE], BF16)))
        b1t = fctx.enter_context(nc.sbuf_tensor(f"f_b1_{tag}", [128, NE * 32], F32))
        b2r = fctx.enter_context(nc.sbuf_tensor(f"f_b2_{tag}", [NE, D], F32))
        b2rb = fctx.enter_context(nc.sbuf_tensor(f"f_b2b_{tag}", [NE, D], BF16))
        gbt = Slot(fctx.enter_context(nc.sbuf_tensor(f"f_gb_{tag}", [128, TILE], F32)))
        gtl = Slot(fctx.enter_context(nc.sbuf_tensor(f"f_gt_{tag}", [NE, TILE], BF16)))
        b2e = Slot(fctx.enter_context(nc.sbuf_tensor(f"f_b2e_{tag}", [128, TILE], F32)))
        R_t = Rot([fctx.enter_context(nc.sbuf_tensor(f"f_t{i}_{tag}", [128, TILE], F32))
                   for i in range(3)])
        stg = Rot([fctx.enter_context(nc.sbuf_tensor(f"f_stg{i}_{tag}", [128, 1024], F32))
                   for i in range(2)])
        R_pm = Rot([pctx.enter_context(nc.psum_tensor(f"f_pm{i}_{tag}", [128, 512], F32))
                    for i in range(4)])
        P_b2 = Slot(pctx.enter_context(nc.psum_tensor(f"f_pb2_{tag}", [128, 512], F32)))
        P_gb = Slot(pctx.enter_context(nc.psum_tensor(f"f_pgb_{tag}", [128, 512], F32)))

        p.dma("sync", b1t[:, :], io["pre_b1"][:, :], inc=[(s_ld, 16)])
        p.dma("sync", b2r[:, :], io["ffn_b2"][:, :], inc=[(s_ld, 16)])
        p.do("gpsimd", lambda e_: e_.tensor_copy(b2rb[:, :], b2r[:, :]),
             wait=[s_ld], inc=[s_cv])

        wld = [p.sem(f"f_wld{i}_{tag}") for i in range(2)]
        last_acc = {}

        for e in range(NE):
            w1w = W1S.take()
            w2w = W2S.take()
            for kc in range(8):
                for half in range(4):
                    sl = stg.next()
                    ws = wld[(stg.i - 1) % 2]
                    p.dma("sync", sl.ap[:, :],
                          io["ffn_w1T"][e, kc * 128:(kc + 1) * 128,
                                        half * 1024:(half + 1) * 1024],
                          wait=sl.take(), inc=[(ws, 16)])
                    p.do("gpsimd", lambda e_, sl=sl, kc=kc, half=half:
                         e_.tensor_copy(
                             w1b[:, kc * FH + half * 1024: kc * FH + (half + 1) * 1024],
                             sl.ap[:, :]),
                         wait=[ws] + (w1w if kc == 0 and half == 0 else []),
                         inc=[s_cv])
                    sl.mark(s_cv)
            for hc in range(32):
                sl = stg.next()
                ws = wld[(stg.i - 1) % 2]
                p.dma("sync", sl.ap[:, 0:D],
                      io["ffn_w2T"][e, hc * 128:(hc + 1) * 128, :],
                      wait=sl.take(), inc=[(ws, 16)])
                p.do("gpsimd", lambda e_, sl=sl, hc=hc: e_.tensor_copy(
                    w2b[:, hc * D:(hc + 1) * D], sl.ap[:, 0:D]),
                    wait=[ws] + (w2w if hc == 0 else []), inc=[s_cv])
                sl.mark(s_cv)
            wconv_done = s_cv.v

            for tt in range(ntiles):
                tok0 = tt * TILE
                lw = lnt.take()
                for c in range(8):
                    p.dma("sync", lnt.ap[:, c * TILE:(c + 1) * TILE],
                          io[ln_in][c * 128:(c + 1) * 128, tok0:tok0 + TILE],
                          wait=(lw if c == 0 else []), inc=[(s_ld, 16)])
                gtw = gtl.take()
                p.dma("sync", gtl.ap[:, :TILE],
                      io[gates_dram][:, tok0:tok0 + TILE],
                      wait=gtw, inc=[(s_ld, 16)])
                ld_done = s_ld.v
                p.do("tensor", lambda e_, e=e, tok0=tok0: e_.matmul(
                    P_gb.ap[:, :TILE], C.sel[:, e * 128:(e + 1) * 128],
                    gtl.ap[:, :TILE], start=True, stop=True),
                    wait=[(s_ld, ld_done)] + P_gb.take(), inc=[s_mm])
                gtl.mark(s_mm)
                gw = gbt.take()
                p.do("scalar", lambda e_: e_.activation(
                    gbt.ap[:, :TILE], P_gb.ap[:, :TILE], AF.Copy),
                    wait=[s_mm] + gw, inc=[s_act])
                P_gb.mark(s_act)
                for hc in range(32):
                    pm = R_pm.next()
                    for kc in range(8):
                        p.do("tensor", lambda e_, hc=hc, kc=kc, pm=pm, e=e:
                             e_.matmul(
                                 pm.ap[:, :TILE],
                                 w1b[:, kc * FH + hc * 128: kc * FH + hc * 128 + 128],
                                 lnt.ap[:, kc * TILE:(kc + 1) * TILE],
                                 start=(kc == 0), stop=(kc == 7)),
                             wait=[(s_ld, ld_done), (s_cv, wconv_done)] +
                                  (pm.take() if kc == 0 else []),
                             inc=([s_mm] if kc == 7 else []))
                    p.do("scalar", lambda e_, hc=hc, pm=pm, e=e: e_.activation(
                        hsb[:, hc * TILE:(hc + 1) * TILE], pm.ap[:, :TILE],
                        AF.Gelu_apprx_tanh, bias=b1t[:, e * 32 + hc:e * 32 + hc + 1]),
                        wait=[s_mm], inc=[s_act])
                    pm.mark(s_act)
                for oc in range(8):
                    pm = R_pm.next()
                    for hc in range(32):
                        p.do("tensor", lambda e_, oc=oc, hc=hc, pm=pm: e_.matmul(
                            pm.ap[:, :TILE],
                            w2b[:, hc * D + oc * 128: hc * D + oc * 128 + 128],
                            hsb[:, hc * TILE:(hc + 1) * TILE],
                            start=(hc == 0), stop=(hc == 31)),
                            wait=[(s_act, s_act.v)] + (pm.take() if hc == 0 else []),
                            inc=([s_mm] if hc == 31 else []))
                    tb = R_t.next()
                    if e == 0:
                        p.do("tensor", lambda e_, oc=oc, tok0=tok0: e_.matmul(
                            P_b2.ap[:, :TILE], b2rb[:, oc * 128:(oc + 1) * 128],
                            gtl.ap[:, :TILE], start=True, stop=True),
                            wait=[(s_ld, ld_done)] + P_b2.take(), inc=[s_mm])
                        gtl.mark(s_mm)
                        bw = b2e.take()
                        p.do("scalar", lambda e_: e_.activation(
                            b2e.ap[:, :TILE], P_b2.ap[:, :TILE], AF.Copy),
                            wait=[(s_mm, s_mm.v)] + bw, inc=[s_act])
                        P_b2.mark(s_act)
                        p.do("vector", lambda e_, pm=pm, tb=tb: e_.tensor_mul(
                            tb.ap[:, :TILE], pm.ap[:, :TILE], gbt.ap[:, :TILE]),
                            wait=[(s_mm, s_mm.v), (s_act, s_act.v)] + tb.take(),
                            inc=[s_dve])
                        pm.mark(s_dve)
                        p.do("vector", lambda e_, tb=tb: e_.tensor_add(
                            tb.ap[:, :TILE], tb.ap[:, :TILE], b2e.ap[:, :TILE]),
                            wait=[s_dve], inc=[s_dve])
                        b2e.mark(s_dve)
                        p.dma("gpsimd",
                              io[acc_out][oc * 128:(oc + 1) * 128, tok0:tok0 + TILE],
                              tb.ap[:, :TILE], wait=[(s_dve, s_dve.v)],
                              inc=[(tb.st(p), 16)])
                        tb.mark(tb.st(p))
                        last_acc[(tt, oc)] = (tb.st(p), tb.st(p).v)
                    else:
                        p.do("vector", lambda e_, pm=pm, tb=tb: e_.tensor_mul(
                            tb.ap[:, :TILE], pm.ap[:, :TILE], gbt.ap[:, :TILE]),
                            wait=[(s_mm, s_mm.v)] + tb.take(), inc=[s_dve])
                        pm.mark(s_dve)
                        p.dma("gpsimd",
                              io[acc_out][oc * 128:(oc + 1) * 128, tok0:tok0 + TILE],
                              tb.ap[:, :TILE],
                              wait=[(s_dve, s_dve.v), last_acc[(tt, oc)]],
                              accum_op=ALU.add, inc=[(tb.st(p), 16)])
                        tb.mark(tb.st(p))
                        last_acc[(tt, oc)] = (tb.st(p), tb.st(p).v)
                gbt.mark(s_dve)
                lnt.mark(s_mm)
            W1S.mark(s_mm)
            W2S.mark(s_mm)
        p.barrier(f"ffn_end_{tag}")



B_WNAMES = ("q_w1", "q_w2", "k_w1", "k_w2", "v_w", "g_w", "p_w")
B_BNAMES = ("q_b1", "q_b2", "k_b1", "k_b2", "v_b", "g_b", "p_b")


def emit_attn_b(p, ctx, C, cfg, io):
    from contextlib import ExitStack
    nc = p.nc
    Tl, TB, TILE = cfg["Tl"], cfg["TB"], cfg["TILE"]
    NB = Tl // TB
    ntiles = Tl // TILE
    JR = TB // 16
    NCH2 = TB // 128

    s_ld = p.sem("e_ld")
    s_mm = p.sem("e_mm")
    s_dve = p.sem("e_dve")
    s_act = p.sem("e_act")

    with ExitStack() as e1:
        pctx = e1
        wq1 = e1.enter_context(nc.sbuf_tensor("e_wq1", [128, 8 * D], BF16))
        wk1 = e1.enter_context(nc.sbuf_tensor("e_wk1", [128, 8 * D], BF16))
        wq2 = e1.enter_context(nc.sbuf_tensor("e_wq2", [128, 8 * D], BF16))
        wk2 = e1.enter_context(nc.sbuf_tensor("e_wk2", [128, 8 * D], BF16))
        wv = e1.enter_context(nc.sbuf_tensor("e_wv", [128, 8 * D], BF16))
        wg = e1.enter_context(nc.sbuf_tensor("e_wg", [128, 8 * D], BF16))
        emit_load_weights(p, e1, [(io["b_q_w1"], wq1, D, D), (io["b_k_w1"], wk1, D, D),
                                  (io["b_q_w2"], wq2, D, D), (io["b_k_w2"], wk2, D, D),
                                  (io["b_v_w"], wv, D, D), (io["b_g_w"], wg, D, D)],
                          "e1")
        bvec = e1.enter_context(nc.sbuf_tensor("e_bvec", [128, 6 * 8], F32))
        brow_q = e1.enter_context(nc.sbuf_tensor("e_browq", [1, D], F32))
        brow_k = e1.enter_context(nc.sbuf_tensor("e_browk", [1, D], F32))
        p.dma("sync", bvec[:, 0:32], io["pre_bvec"][:, :], inc=[(s_ld, 16)])
        p.dma("sync", brow_q[:, :], io["b_q_b2"].rearrange("(o d) -> o d", o=1),
              inc=[(s_ld, 16)])
        p.dma("sync", brow_k[:, :], io["b_k_b2"].rearrange("(o d) -> o d", o=1),
              inc=[(s_ld, 16)])
        p.barrier("e1_w")

        S_ac = Slot(e1.enter_context(nc.sbuf_tensor("e_ac", [128, 8 * TILE], F32)))
        S_x2 = Slot(e1.enter_context(nc.sbuf_tensor("e_x2", [128, 8 * TILE], F32)))
        S_xb = Slot(e1.enter_context(nc.sbuf_tensor("e_xb", [128, 8 * TILE], BF16)))
        S_q1 = Slot(e1.enter_context(nc.sbuf_tensor("e_q1", [128, 8 * TILE], BF16)))
        S_k1 = Slot(e1.enter_context(nc.sbuf_tensor("e_k1", [128, 8 * TILE], BF16)))
        R_sq = Rot([e1.enter_context(nc.sbuf_tensor(f"e_sq{i}", [128, TILE], BF16))
                    for i in range(8)])
        R_cv = Rot([e1.enter_context(nc.sbuf_tensor(f"e_cv{i}",
                                                    [128, max(TILE, 512)], BF16))
                    for i in range(3)])
        sp1 = Slot(e1.enter_context(nc.sbuf_tensor("e_sp1", [128, 512], F32)))
        sp2 = Slot(e1.enter_context(nc.sbuf_tensor("e_sp2", [128, 512], F32)))
        stat = e1.enter_context(nc.sbuf_tensor("e_stat", [1, 4 * TILE], F32))
        R_pm = Rot([pctx.enter_context(nc.psum_tensor(f"e_pm{i}", [128, 512], F32))
                    for i in range(2)])
        P_s1 = Slot(pctx.enter_context(nc.psum_tensor("e_ps1", [1, 512], F32)))
        P_s2 = Slot(pctx.enter_context(nc.psum_tensor("e_ps2", [1, 512], F32)))
        P_bc = Slot(pctx.enter_context(nc.psum_tensor("e_pb", [128, 512], F32)))
        ST = TILE

        for t in range(ntiles):
            tok0 = t * TILE
            acw = S_ac.take()
            for c in range(8):
                p.dma("sync", S_ac.ap[:, c * TILE:(c + 1) * TILE],
                      io["acc1_s"][c * 128:(c + 1) * 128, tok0:tok0 + TILE],
                      wait=(acw if c == 0 else []), inc=[(s_ld, 16)])
            x2w = S_x2.take()
            for c in range(8):
                p.dma("sync", S_x2.ap[:, c * TILE:(c + 1) * TILE],
                      io["xi"][c * 128:(c + 1) * 128, tok0:tok0 + TILE],
                      wait=(x2w if c == 0 else []), inc=[(s_ld, 16)])
            ld_done = s_ld.v
            for c in range(8):
                p.do("vector", lambda e, c=c: e.tensor_add(
                    S_x2.ap[:, c * TILE:(c + 1) * TILE],
                    S_x2.ap[:, c * TILE:(c + 1) * TILE],
                    S_ac.ap[:, c * TILE:(c + 1) * TILE]),
                    wait=[(s_ld, ld_done)], inc=[s_dve])
            S_ac.mark(s_dve)
            for c in range(8):
                p.dma("gpsimd", io["x2_s"][c * 128:(c + 1) * 128, tok0:tok0 + TILE],
                      S_x2.ap[:, c * TILE:(c + 1) * TILE],
                      wait=[(s_dve, s_dve.v)], inc=[(S_x2.st(p), 16)])
            S_x2.mark(S_x2.st(p))
            for c in range(8):
                p.do("tensor", lambda e, c=c: e.matmul(
                    P_s1.ap[:, :TILE], C.ones_col_f[:, :],
                    S_x2.ap[:, c * TILE:(c + 1) * TILE],
                    start=(c == 0), stop=(c == 7)),
                    wait=[(s_dve, s_dve.v)] + (P_s1.take() if c == 0 else []),
                    inc=([s_mm] if c == 7 else []))
            for c in range(8):
                sq = R_sq.next()
                p.do("scalar", lambda e, c=c, sq=sq: e.activation(
                    sq.ap[:, :], S_x2.ap[:, c * TILE:(c + 1) * TILE], AF.Square),
                    wait=[(s_dve, s_dve.v)] + sq.take(), inc=[s_act])
                p.do("tensor", lambda e, c=c, sq=sq: e.matmul(
                    P_s2.ap[:, :TILE], C.ones_col_b[:, :], sq.ap[:, :],
                    start=(c == 0), stop=(c == 7)),
                    wait=[s_act] + (P_s2.take() if c == 0 else []),
                    inc=([s_mm] if c == 7 else []))
            R_sq.mark_all(s_mm)
            p.do("vector", lambda e: e.tensor_scalar_mul(
                stat[:, 0 * ST:0 * ST + TILE], P_s1.ap[:, :TILE], 1.0 / D),
                wait=[s_mm], inc=[s_dve])
            P_s1.mark(s_dve)
            p.do("scalar", lambda e: e.activation(
                stat[:, 1 * ST:1 * ST + TILE], stat[:, 0 * ST:0 * ST + TILE],
                AF.Square), wait=[s_dve], inc=[s_act])
            p.do("vector", lambda e: e.scalar_tensor_tensor(
                stat[:, 2 * ST:2 * ST + TILE], P_s2.ap[:, :TILE], 1.0 / D,
                stat[:, 1 * ST:1 * ST + TILE], ALU.mult, ALU.subtract),
                wait=[s_mm, s_act], inc=[s_dve])
            P_s2.mark(s_dve)
            p.do("scalar", lambda e: e.activation(
                stat[:, 2 * ST:2 * ST + TILE], stat[:, 2 * ST:2 * ST + TILE],
                AF.Ln, bias=LN_EPS), wait=[s_dve], inc=[s_act])
            p.do("scalar", lambda e: e.activation(
                stat[:, 2 * ST:2 * ST + TILE], stat[:, 2 * ST:2 * ST + TILE],
                AF.Exp, scale=-0.5), wait=[s_act], inc=[s_act])
            p.do("tensor", lambda e: e.matmul(
                P_bc.ap[:, :TILE], C.ones_row_f[:, :], stat[:, 0 * ST:0 * ST + TILE],
                start=True, stop=True), wait=[(s_act, s_act.v)] + P_bc.take(),
                inc=[s_mm])
            xbw = S_xb.take()
            for c in range(8):
                p.do("vector", lambda e, c=c: e.tensor_sub(
                    S_xb.ap[:, c * TILE:(c + 1) * TILE],
                    S_x2.ap[:, c * TILE:(c + 1) * TILE], P_bc.ap[:, :TILE]),
                    wait=[s_mm] + (xbw if c == 0 else []), inc=[s_dve])
            P_bc.mark(s_dve)
            S_x2.mark(s_dve)
            p.do("tensor", lambda e: e.matmul(
                P_bc.ap[:, :TILE], C.ones_row_f[:, :], stat[:, 2 * ST:2 * ST + TILE],
                start=True, stop=True), wait=[(s_dve, s_dve.v)] + P_bc.take(),
                inc=[s_mm])
            for c in range(8):
                p.do("vector", lambda e, c=c: e.tensor_mul(
                    S_xb.ap[:, c * TILE:(c + 1) * TILE],
                    S_xb.ap[:, c * TILE:(c + 1) * TILE], P_bc.ap[:, :TILE]),
                    wait=[s_mm], inc=[s_dve])
            P_bc.mark(s_dve)
            xb_done = s_dve.v

            def proj_b(w_sb, bidx, keep_slot, dram, tag2):
                if keep_slot is not None:
                    kw = keep_slot.take()
                for mc in range(8):
                    pm = R_pm.next()
                    for kc in range(8):
                        p.do("tensor", lambda e, mc=mc, kc=kc, pm=pm, w_sb=w_sb:
                             e.matmul(
                                 pm.ap[:, :TILE],
                                 w_sb[:, kc * D + mc * 128: kc * D + mc * 128 + 128],
                                 S_xb.ap[:, kc * TILE:(kc + 1) * TILE],
                                 start=(kc == 0), stop=(kc == 7)),
                             wait=[(s_dve, xb_done)] + (pm.take() if kc == 0 else []),
                             inc=([s_mm] if kc == 7 else []))
                    if keep_slot is not None:
                        p.do("scalar", lambda e, mc=mc, pm=pm, bidx=bidx: e.activation(
                            keep_slot.ap[:, mc * TILE:(mc + 1) * TILE],
                            pm.ap[:, :TILE], AF.Gelu,
                            bias=bvec[:, bidx * 8 + mc: bidx * 8 + mc + 1]),
                            wait=[s_mm] + (kw if mc == 0 else []), inc=[s_act])
                        pm.mark(s_act)
                    else:
                        cv = R_cv.next()
                        p.do("scalar", lambda e, mc=mc, pm=pm, bidx=bidx, cv=cv:
                             e.activation(
                                 cv.ap[:, :TILE], pm.ap[:, :TILE], AF.Gelu,
                                 bias=bvec[:, bidx * 8 + mc: bidx * 8 + mc + 1]),
                             wait=[s_mm] + cv.take(), inc=[s_act])
                        pm.mark(s_act)
                        p.dma("gpsimd",
                              dram[mc * 128:(mc + 1) * 128, tok0:tok0 + TILE],
                              cv.ap[:, :TILE], wait=[s_act], inc=[(cv.st(p), 16)])
                        cv.mark(cv.st(p))

            proj_b(wq1, 0, S_q1, None, "q1")
            proj_b(wk1, 1, S_k1, None, "k1")
            proj_b(wv, 2, None, io["vb_s"], "v")
            proj_b(wg, 3, None, io["gb_s"], "g")
            q1_done = s_act.v

            def proj2(src_slot, w_sb, browx, dram):
                for tk in range(TILE // 128):
                    for nh in range(2):
                        pm = R_pm.next()
                        for kc in range(8):
                            p.do("tensor", lambda e, tk=tk, nh=nh, kc=kc, pm=pm,
                                 w_sb=w_sb, src_slot=src_slot: e.matmul(
                                     pm.ap[:, :512],
                                     src_slot.ap[:, kc * TILE + tk * 128:
                                                 kc * TILE + tk * 128 + 128],
                                     w_sb[:, kc * D + nh * 512: kc * D + (nh + 1) * 512],
                                     start=(kc == 0), stop=False),
                                 wait=[(s_act, q1_done)] + (pm.take() if kc == 0 else []),
                                 inc=[])
                        p.do("tensor", lambda e, nh=nh, pm=pm, browx=browx:
                             e.matmul(
                                 pm.ap[:, :512], C.ones_row_f[:, :],
                                 browx[0:1, nh * 512:(nh + 1) * 512],
                                 start=False, stop=True), inc=[s_mm])
                        spw = sp1.take()
                        p.do("scalar", lambda e, pm=pm: e.activation(
                            sp1.ap[:, :], pm.ap[:, :512], AF.Relu),
                            wait=[s_mm] + spw, inc=[s_act])
                        spw2 = sp2.take()
                        p.do("scalar", lambda e, pm=pm: e.activation(
                            sp2.ap[:, :], pm.ap[:, :512], AF.Abs),
                            wait=[(s_mm, s_mm.v)] + spw2, inc=[s_act])
                        pm.mark(s_act)
                        p.do("scalar", lambda e: e.activation(
                            sp2.ap[:, :], sp2.ap[:, :], AF.Exp, scale=-1.0),
                            wait=[s_act], inc=[s_act])
                        p.do("scalar", lambda e: e.activation(
                            sp2.ap[:, :], sp2.ap[:, :], AF.Ln, bias=1.0),
                            wait=[s_act], inc=[s_act])
                        cv = R_cv.next()
                        p.do("vector", lambda e, cv=cv: e.tensor_add(
                            cv.ap[:, :512], sp1.ap[:, :], sp2.ap[:, :]),
                            wait=[s_act] + cv.take(), inc=[s_dve])
                        sp1.mark(s_dve)
                        sp2.mark(s_dve)
                        p.dma("gpsimd",
                              dram[tok0 + tk * 128: tok0 + tk * 128 + 128,
                                   nh * 512:(nh + 1) * 512],
                              cv.ap[:, :512], wait=[(s_dve, s_dve.v)],
                              inc=[(cv.st(p), 16)])
                        cv.mark(cv.st(p))

            proj2(S_q1, wq2, brow_q, io["q2_s"])
            proj2(S_k1, wk2, brow_k, io["k2_s"])
            S_q1.mark(s_mm)
            S_k1.mark(s_mm)
            S_xb.mark(s_mm)
        p.barrier("e1_end")

    with ExitStack() as e2:
        pctx = e2
        wp = e2.enter_context(nc.sbuf_tensor("e_wp", [128, 8 * D], BF16))
        gwt = e2.enter_context(nc.sbuf_tensor("e_gwt", [128, 8 * NE], BF16))
        emit_load_weights(p, e2, [(io["b_p_w"], wp, D, D)], "e2")
        emit_load_weights(p, e2, [(io["gate_wT"], gwt, D, NE)], "e2b")
        pbv = e2.enter_context(nc.sbuf_tensor("e_pbv", [128, 8], F32))
        p.dma("sync", pbv[:, :], io["pre_pbv"][:, :], inc=[(s_ld, 16)])
        p.barrier("e2_w")

        S_q2 = Slot(e2.enter_context(nc.sbuf_tensor("e2_q2", [128, NCH2 * 64], BF16)))
        S_k2 = Slot(e2.enter_context(nc.sbuf_tensor("e2_k2", [128, NCH2 * 64], BF16)))
        S_vh = Slot(e2.enter_context(nc.sbuf_tensor("e2_vh", [64, TB], BF16)))
        S_gh = Slot(e2.enter_context(nc.sbuf_tensor("e2_gh", [64, TB], BF16)))
        S_ms = Slot(e2.enter_context(nc.sbuf_tensor("e2_ms", [64, TB], BF16)))
        S_mr = Slot(e2.enter_context(nc.sbuf_tensor("e2_mr", [128, 8 * JR], BF16)))
        A_sb = Slot(e2.enter_context(nc.sbuf_tensor("e2_A", [64, 64], BF16)))
        zv = e2.enter_context(nc.sbuf_tensor("e2_z", [64, 1], F32))
        S_xs = Slot(e2.enter_context(nc.sbuf_tensor("e2_xs", [128, 8 * JR], F32)))
        S_x3 = Slot(e2.enter_context(nc.sbuf_tensor("e2_x3", [128, 8 * JR], F32)))
        S_l3 = Slot(e2.enter_context(nc.sbuf_tensor("e2_l3", [128, 8 * JR], BF16)))
        R_sqE = Rot([e2.enter_context(nc.sbuf_tensor(f"e2_sq{i}", [128, JR], BF16))
                    for i in range(8)])
        statE = e2.enter_context(nc.sbuf_tensor("e2_statE", [1, 4 * JR], F32))
        lgtE = Slot(e2.enter_context(nc.sbuf_tensor("e2_lg", [NE, JR], F32)))
        R_pmE = Rot([pctx.enter_context(nc.psum_tensor(f"e2_pm{i}", [128, 512], F32))
                    for i in range(2)])
        P_A = Slot(pctx.enter_context(nc.psum_tensor("e2_pA", [64, 512], F32)))
        P_s1E = Slot(pctx.enter_context(nc.psum_tensor("e2_ps1", [1, 512], F32)))
        P_s2E = Slot(pctx.enter_context(nc.psum_tensor("e2_ps2", [1, 512], F32)))
        P_bcE = Slot(pctx.enter_context(nc.psum_tensor("e2_pb", [128, 512], F32)))
        P_lg = Slot(pctx.enter_context(nc.psum_tensor("e2_plg", [NE, 512], F32)))
        STE = JR

        for b in range(NB):
            for h in range(NH):
                tokb = b * TB
                q2w = S_q2.take()
                p.dma("sync",
                      S_q2.ap[:, :].rearrange("p (c f) -> p c f", c=NCH2),
                      io["q2_s"][tokb:tokb + TB, 64 * h:64 * (h + 1)]
                      .rearrange("(c p) f -> p c f", p=128),
                      wait=q2w, inc=[(s_ld, 16)])
                k2w = S_k2.take()
                p.dma("sync",
                      S_k2.ap[:, :].rearrange("p (c f) -> p c f", c=NCH2),
                      io["k2_s"][tokb:tokb + TB, 64 * h:64 * (h + 1)]
                      .rearrange("(c p) f -> p c f", p=128),
                      wait=k2w, inc=[(s_ld, 16)])
                p.dma("sync", S_vh.ap[:, :],
                      io["vb_s"][64 * h:64 * (h + 1), tokb:tokb + TB],
                      wait=S_vh.take(), inc=[(s_ld, 16)])
                p.dma("sync", S_gh.ap[:, :],
                      io["gb_s"][64 * h:64 * (h + 1), tokb:tokb + TB],
                      wait=S_gh.take(), inc=[(s_ld, 16)])
                xsw = S_xs.take()
                for c in range(8):
                    p.dma("sync", S_xs.ap[:, c * JR:(c + 1) * JR],
                          io["x2_s"][c * 128:(c + 1) * 128,
                                     tokb + h * JR: tokb + (h + 1) * JR],
                          wait=(xsw if c == 0 else []), inc=[(s_ld, 16)])
                ld_done = s_ld.v

                for cc in range(NCH2):
                    p.do("tensor", lambda e, cc=cc: e.matmul(
                        P_A.ap[:, 0:64], S_k2.ap[:, cc * 64:(cc + 1) * 64],
                        S_q2.ap[:, cc * 64:(cc + 1) * 64],
                        start=(cc == 0), stop=(cc == NCH2 - 1)),
                        wait=[(s_ld, ld_done)] + (P_A.take() if cc == 0 else []),
                        inc=([s_mm] if cc == NCH2 - 1 else []))
                S_q2.mark(s_mm)
                S_k2.mark(s_mm)
                aw = A_sb.take()
                p.do("vector", lambda e: e.tensor_copy(A_sb.ap[:, :],
                                                       P_A.ap[:, 0:64]),
                     wait=[s_mm] + aw, inc=[s_dve])
                p.do("tensor", lambda e: e.matmul(
                    P_A.ap[:, 64:65], A_sb.ap[:, :], C.ones_col_b[0:64, :],
                    start=True, stop=True), wait=[s_dve] + P_A.take(),
                    inc=[s_mm])
                p.do("vector", lambda e: e.tensor_scalar(
                    zv[:, :], P_A.ap[:, 64:65], BSCALE, float(D),
                    ALU.mult, ALU.add), wait=[s_mm], inc=[s_dve])
                P_A.mark(s_dve)
                p.do("vector", lambda e: e.reciprocal(zv[:, :], zv[:, :]),
                     wait=[s_dve], inc=[s_dve])
                msw = S_ms.take()
                for cc in range(NCH2):
                    pm = R_pmE.next()
                    p.do("tensor", lambda e, cc=cc, pm=pm: e.matmul(
                        pm.ap[0:64, 0:128], A_sb.ap[:, :],
                        S_vh.ap[:, cc * 128:(cc + 1) * 128],
                        start=True, stop=True),
                        wait=[(s_dve, s_dve.v)] + pm.take(), inc=[s_mm])
                    p.do("vector", lambda e, cc=cc, pm=pm: e.scalar_tensor_tensor(
                        S_ms.ap[:, cc * 128:(cc + 1) * 128], pm.ap[0:64, 0:128],
                        BSCALE, S_vh.ap[:, cc * 128:(cc + 1) * 128],
                        ALU.mult, ALU.add),
                        wait=[s_mm] + (msw if cc == 0 else []), inc=[s_dve])
                    p.do("vector", lambda e, cc=cc: e.tensor_scalar(
                        S_ms.ap[:, cc * 128:(cc + 1) * 128],
                        S_ms.ap[:, cc * 128:(cc + 1) * 128], zv[:, 0:1], None,
                        ALU.mult), wait=[s_dve], inc=[s_dve])
                    p.do("vector", lambda e, cc=cc: e.tensor_mul(
                        S_ms.ap[:, cc * 128:(cc + 1) * 128],
                        S_ms.ap[:, cc * 128:(cc + 1) * 128],
                        S_gh.ap[:, cc * 128:(cc + 1) * 128]),
                        wait=[s_dve], inc=[s_dve])
                    pm.mark(s_dve)
                S_vh.mark(s_dve)
                S_vh.mark(s_mm)
                S_gh.mark(s_dve)
                A_sb.mark(s_mm)
                mrw = S_mr.take()
                for dt in range(16):
                    p.do("vector", lambda e, dt=dt: e.tensor_copy(
                        S_mr.ap[64 * (dt % 2):64 * (dt % 2) + 64,
                                (dt // 2) * JR:(dt // 2) * JR + JR],
                        S_ms.ap[:, dt:dt + 16 * (JR - 1) + 1:16]),
                        wait=[(s_dve, s_dve.v)] + (mrw if dt == 0 else []),
                        inc=[s_dve])
                S_ms.mark(s_dve)
                x3w = S_x3.take()
                for mc in range(8):
                    pm = R_pmE.next()
                    for kc in range(8):
                        p.do("tensor", lambda e, mc=mc, kc=kc, pm=pm: e.matmul(
                            pm.ap[:, :JR],
                            wp[:, kc * D + mc * 128: kc * D + mc * 128 + 128],
                            S_mr.ap[:, kc * JR:(kc + 1) * JR],
                            start=(kc == 0), stop=(kc == 7)),
                            wait=[(s_dve, s_dve.v)] + (pm.take() if kc == 0 else []),
                            inc=([s_mm] if kc == 7 else []))
                    p.do("vector", lambda e, mc=mc, pm=pm: e.scalar_tensor_tensor(
                        S_x3.ap[:, mc * JR:(mc + 1) * JR], pm.ap[:, :JR],
                        pbv[:, mc:mc + 1], S_xs.ap[:, mc * JR:(mc + 1) * JR],
                        ALU.add, ALU.add),
                        wait=[s_mm] + (x3w if mc == 0 else []), inc=[s_dve])
                    pm.mark(s_dve)
                S_mr.mark(s_mm)
                S_xs.mark(s_dve)
                for c in range(8):
                    p.dma("gpsimd",
                          io["x3o"][c * 128:(c + 1) * 128,
                                    tokb + h * JR: tokb + (h + 1) * JR],
                          S_x3.ap[:, c * JR:(c + 1) * JR],
                          wait=[(s_dve, s_dve.v)], inc=[(S_x3.st(p), 16)])
                S_x3.mark(S_x3.st(p))
                for c in range(8):
                    p.do("tensor", lambda e, c=c: e.matmul(
                        P_s1E.ap[:, :JR], C.ones_col_f[:, :],
                        S_x3.ap[:, c * JR:(c + 1) * JR],
                        start=(c == 0), stop=(c == 7)),
                        wait=[(s_dve, s_dve.v)] + (P_s1E.take() if c == 0 else []),
                        inc=([s_mm] if c == 7 else []))
                for c in range(8):
                    sq = R_sqE.next()
                    p.do("scalar", lambda e, c=c, sq=sq: e.activation(
                        sq.ap[:, :], S_x3.ap[:, c * JR:(c + 1) * JR], AF.Square),
                        wait=[(s_dve, s_dve.v)] + sq.take(), inc=[s_act])
                    p.do("tensor", lambda e, c=c, sq=sq: e.matmul(
                        P_s2E.ap[:, :JR], C.ones_col_b[:, :], sq.ap[:, :],
                        start=(c == 0), stop=(c == 7)),
                        wait=[s_act] + (P_s2E.take() if c == 0 else []),
                        inc=([s_mm] if c == 7 else []))
                R_sqE.mark_all(s_mm)
                p.do("vector", lambda e: e.tensor_scalar_mul(
                    statE[:, 0 * STE:0 * STE + JR], P_s1E.ap[:, :JR], 1.0 / D),
                    wait=[s_mm], inc=[s_dve])
                P_s1E.mark(s_dve)
                p.do("scalar", lambda e: e.activation(
                    statE[:, 1 * STE:1 * STE + JR], statE[:, 0 * STE:0 * STE + JR],
                    AF.Square), wait=[s_dve], inc=[s_act])
                p.do("vector", lambda e: e.scalar_tensor_tensor(
                    statE[:, 2 * STE:2 * STE + JR], P_s2E.ap[:, :JR], 1.0 / D,
                    statE[:, 1 * STE:1 * STE + JR], ALU.mult, ALU.subtract),
                    wait=[s_mm, s_act], inc=[s_dve])
                P_s2E.mark(s_dve)
                p.do("scalar", lambda e: e.activation(
                    statE[:, 2 * STE:2 * STE + JR], statE[:, 2 * STE:2 * STE + JR],
                    AF.Ln, bias=LN_EPS), wait=[s_dve], inc=[s_act])
                p.do("scalar", lambda e: e.activation(
                    statE[:, 2 * STE:2 * STE + JR], statE[:, 2 * STE:2 * STE + JR],
                    AF.Exp, scale=-0.5), wait=[s_act], inc=[s_act])
                p.do("tensor", lambda e: e.matmul(
                    P_bcE.ap[:, :JR], C.ones_row_f[:, :], statE[:, 0 * STE:0 * STE + JR],
                    start=True, stop=True), wait=[(s_act, s_act.v)] + P_bcE.take(),
                    inc=[s_mm])
                l3w = S_l3.take()
                for c in range(8):
                    p.do("vector", lambda e, c=c: e.tensor_sub(
                        S_l3.ap[:, c * JR:(c + 1) * JR],
                        S_x3.ap[:, c * JR:(c + 1) * JR], P_bcE.ap[:, :JR]),
                        wait=[s_mm] + (l3w if c == 0 else []), inc=[s_dve])
                P_bcE.mark(s_dve)
                S_x3.mark(s_dve)
                p.do("tensor", lambda e: e.matmul(
                    P_bcE.ap[:, :JR], C.ones_row_f[:, :], statE[:, 2 * STE:2 * STE + JR],
                    start=True, stop=True), wait=[(s_dve, s_dve.v)] + P_bcE.take(),
                    inc=[s_mm])
                for c in range(8):
                    p.do("vector", lambda e, c=c: e.tensor_mul(
                        S_l3.ap[:, c * JR:(c + 1) * JR],
                        S_l3.ap[:, c * JR:(c + 1) * JR], P_bcE.ap[:, :JR]),
                        wait=[s_mm], inc=[s_dve])
                P_bcE.mark(s_dve)
                for c in range(8):
                    p.dma("gpsimd",
                          io["ln3o"][c * 128:(c + 1) * 128,
                                     tokb + h * JR: tokb + (h + 1) * JR],
                          S_l3.ap[:, c * JR:(c + 1) * JR],
                          wait=[(s_dve, s_dve.v)], inc=[(S_l3.st(p), 16)])
                S_l3.mark(S_l3.st(p))
                for kc in range(8):
                    p.do("tensor", lambda e, kc=kc: e.matmul(
                        P_lg.ap[:, :JR], gwt[:, kc * NE:(kc + 1) * NE],
                        S_l3.ap[:, kc * JR:(kc + 1) * JR],
                        start=(kc == 0), stop=(kc == 7)),
                        wait=[(s_dve, s_dve.v)] + (P_lg.take() if kc == 0 else []),
                        inc=([s_mm] if kc == 7 else []))
                S_l3.mark(s_mm)
                lgw = lgtE.take()
                p.do("scalar", lambda e: e.activation(
                    lgtE.ap[:, :JR], P_lg.ap[:, :JR], AF.Copy),
                    wait=[s_mm] + lgw, inc=[s_act])
                P_lg.mark(s_act)
                p.dma("gpsimd",
                      io["lgo2"][:, tokb + h * JR: tokb + (h + 1) * JR],
                      lgtE.ap[:, :JR], wait=[s_act], inc=[(lgtE.st(p), 16)])
                lgtE.mark(lgtE.st(p))
        p.barrier("e2_end")



def emit_final_add(p, ctx, C, cfg, io):
    from contextlib import ExitStack
    nc = p.nc
    Tl, TILE = cfg["Tl"], cfg["TILE"]
    s_ld = p.sem("z_ld")
    s_dve = p.sem("z_dve")
    with ExitStack() as zctx:
        S_a = Slot(zctx.enter_context(nc.sbuf_tensor("z_a", [128, 8 * TILE], F32)))
        S_x = Slot(zctx.enter_context(nc.sbuf_tensor("z_x", [128, 8 * TILE], F32)))
        for t in range(Tl // TILE):
            tok0 = t * TILE
            aw = S_a.take()
            for c in range(8):
                p.dma("sync", S_a.ap[:, c * TILE:(c + 1) * TILE],
                      io["acc2_s"][c * 128:(c + 1) * 128, tok0:tok0 + TILE],
                      wait=(aw if c == 0 else []), inc=[(s_ld, 16)])
            xw = S_x.take()
            for c in range(8):
                p.dma("sync", S_x.ap[:, c * TILE:(c + 1) * TILE],
                      io["xi"][c * 128:(c + 1) * 128, tok0:tok0 + TILE],
                      wait=(xw if c == 0 else []), inc=[(s_ld, 16)])
            ld_done = s_ld.v
            for c in range(8):
                p.do("vector", lambda e, c=c: e.tensor_add(
                    S_x.ap[:, c * TILE:(c + 1) * TILE],
                    S_x.ap[:, c * TILE:(c + 1) * TILE],
                    S_a.ap[:, c * TILE:(c + 1) * TILE]),
                    wait=[(s_ld, ld_done)], inc=[s_dve])
            S_a.mark(s_dve)
            for c in range(8):
                p.dma("gpsimd", io["out_fm"][c * 128:(c + 1) * 128, tok0:tok0 + TILE],
                      S_x.ap[:, c * TILE:(c + 1) * TILE],
                      wait=[(s_dve, s_dve.v)], inc=[(S_x.st(p), 16)])
            S_x.mark(S_x.st(p))
        p.barrier("z_end")



def _declare_common(nc, cfg, io):
    consts = host_consts()
    for k in CONST_NAMES:
        arr = consts[k]
        dt = BF16 if arr.dtype == BF else F32
        io[k] = nc.dram_tensor(k, list(arr.shape), dt, kind="ExternalInput").ap()
    io["c_kth"] = nc.dram_tensor("c_kth", [128, 1], F32, kind="ExternalInput").ap()


def build_l2(cfg):
    from contextlib import ExitStack
    Tl, NGC = cfg["Tl"], cfg["NGC"]
    nc = bass.Bass("TRN2", target_bir_lowering=False, debug=False)
    io = {}
    io["xi"] = nc.dram_tensor("xi", [D, Tl], F32, kind="ExternalInput").ap()
    io["lni"] = nc.dram_tensor("lni", [D, Tl], BF16, kind="ExternalInput").ap()
    io["glg"] = nc.dram_tensor("glg", [128, NGC], F32, kind="ExternalInput").ap()
    io["olg"] = nc.dram_tensor("olg", [NE, Tl], F32, kind="ExternalInput").ap()
    for n in B_WNAMES:
        io["b_" + n] = nc.dram_tensor("b_" + n, [D, D], F32, kind="ExternalInput").ap()
    for n in B_BNAMES:
        io["b_" + n] = nc.dram_tensor("b_" + n, [D], F32, kind="ExternalInput").ap()
    io["ffn_w1T"] = nc.dram_tensor("ffn_w1T", [NE, D, FH], F32, kind="ExternalInput").ap()
    io["ffn_w2T"] = nc.dram_tensor("ffn_w2T", [NE, FH, D], F32, kind="ExternalInput").ap()
    io["ffn_b1"] = nc.dram_tensor("ffn_b1", [NE, FH], F32, kind="ExternalInput").ap()
    io["ffn_b2"] = nc.dram_tensor("ffn_b2", [NE, D], F32, kind="ExternalInput").ap()
    io["gate_wT"] = nc.dram_tensor("gate_wT", [D, NE], F32, kind="ExternalInput").ap()
    io["pre_b1"] = nc.dram_tensor("pre_b1", [128, 256], F32, kind="ExternalInput").ap()
    io["pre_bvec"] = nc.dram_tensor("pre_bvec", [128, 32], F32, kind="ExternalInput").ap()
    io["pre_pbv"] = nc.dram_tensor("pre_pbv", [128, 8], F32, kind="ExternalInput").ap()
    _declare_common(nc, cfg, io)
    io["acc1_s"] = nc.dram_tensor("acc1_s", [D, Tl], F32).ap()
    io["x2_s"] = nc.dram_tensor("x2_s", [D, Tl], F32).ap()
    io["q2_s"] = nc.dram_tensor("q2_s", [Tl, D], BF16).ap()
    io["k2_s"] = nc.dram_tensor("k2_s", [Tl, D], BF16).ap()
    io["vb_s"] = nc.dram_tensor("vb_s", [D, Tl], BF16).ap()
    io["gb_s"] = nc.dram_tensor("gb_s", [D, Tl], BF16).ap()
    io["x3o"] = nc.dram_tensor("x3o", [D, Tl], F32, kind="ExternalOutput").ap()
    io["ln3o"] = nc.dram_tensor("ln3o", [D, Tl], BF16, kind="ExternalOutput").ap()
    io["lgo2"] = nc.dram_tensor("lgo2", [NE, Tl], F32, kind="ExternalOutput").ap()

    io["gates_s"] = nc.dram_tensor("gates_s", [NE, Tl], BF16).ap()
    ctx = ExitStack()
    p = Prog(nc, ctx)
    s_cld = p.sem("cld")
    C = Consts(p, ctx, io, s_cld)
    p.do("sync", lambda e: e.nop(), wait=[(s_cld, s_cld.v)])
    p.barrier("consts")
    with ctx:
        emit_kth(p, ctx, C, cfg, io, "glg", "olg", "gates_s")
        emit_ffn(p, ctx, C, cfg, io, "lni", "acc1_s", "gates_s", "f1")
        emit_attn_b(p, ctx, C, cfg, io)
        p.build_block()
    return nc


def build_l3(cfg):
    Tl, NGC = cfg["Tl"], cfg["NGC"]
    nc = bass.Bass("TRN2", target_bir_lowering=False, debug=False)
    io = {}
    io["xi"] = nc.dram_tensor("xi", [D, Tl], F32, kind="ExternalInput").ap()
    io["lni"] = nc.dram_tensor("lni", [D, Tl], BF16, kind="ExternalInput").ap()
    io["glg"] = nc.dram_tensor("glg", [128, NGC], F32, kind="ExternalInput").ap()
    io["olg"] = nc.dram_tensor("olg", [NE, Tl], F32, kind="ExternalInput").ap()
    io["ffn_w1T"] = nc.dram_tensor("ffn_w1T", [NE, D, FH], F32, kind="ExternalInput").ap()
    io["ffn_w2T"] = nc.dram_tensor("ffn_w2T", [NE, FH, D], F32, kind="ExternalInput").ap()
    io["ffn_b1"] = nc.dram_tensor("ffn_b1", [NE, FH], F32, kind="ExternalInput").ap()
    io["ffn_b2"] = nc.dram_tensor("ffn_b2", [NE, D], F32, kind="ExternalInput").ap()
    io["pre_b1"] = nc.dram_tensor("pre_b1", [128, 256], F32, kind="ExternalInput").ap()
    _declare_common(nc, cfg, io)
    io["acc2_s"] = nc.dram_tensor("acc2_s", [D, Tl], F32).ap()
    io["out_fm"] = nc.dram_tensor("out_fm", [D, Tl], F32, kind="ExternalOutput").ap()

    io["gates_s"] = nc.dram_tensor("gates_s", [NE, Tl], BF16).ap()
    ctx = ExitStack()
    p = Prog(nc, ctx)
    s_cld = p.sem("cld")
    C = Consts(p, ctx, io, s_cld)
    p.do("sync", lambda e: e.nop(), wait=[(s_cld, s_cld.v)])
    p.barrier("consts")
    with ctx:
        emit_kth(p, ctx, C, cfg, io, "glg", "olg", "gates_s")
        emit_ffn(p, ctx, C, cfg, io, "lni", "acc2_s", "gates_s", "f2")
        emit_final_add(p, ctx, C, cfg, io)
        p.build_block()
    return nc



FULL_CFG = {"Tl": 8192, "TB": 4096, "TB2": 2048, "TILE": 512, "NGC": 4096,
            "kth": 24576}


def _pc(v):
    v = np.asarray(v, np.float32)
    return np.ascontiguousarray(v.reshape(-1, 128).T)


def _prearranged(inputs):
    pre = {}
    pre["pre_normw"] = _pc(inputs["L0_attn_a_norm_w"])
    cw = np.zeros((128, 128), np.float32)
    for i, n in enumerate(("conv_q_w", "conv_k_w")):
        w = np.asarray(inputs["L0_attn_a_" + n], np.float32)
        for c in range(8):
            cw[:, i * 32 + c * 4: i * 32 + (c + 1) * 4] = w[c * 128:(c + 1) * 128]
    wv = np.asarray(inputs["L0_attn_a_conv_v_w"], np.float32)
    for c in range(16):
        cw[:, 64 + c * 4: 64 + (c + 1) * 4] = wv[c * 128:(c + 1) * 128]
    pre["pre_convw"] = cw
    pre["pre_postw"] = _pc(inputs["L0_attn_a_post_norm_w"])
    b1 = np.asarray(inputs["L0_ffn_b1"], np.float32)
    pre["pre_b1"] = np.ascontiguousarray(
        b1.reshape(8, 32, 128).transpose(2, 0, 1).reshape(128, 256))
    pre["pre_bvec"] = np.ascontiguousarray(np.concatenate(
        [_pc(inputs["L0_attn_b_" + n]) for n in ("q_b1", "k_b1", "v_b", "g_b")],
        axis=1))
    pre["pre_pbv"] = _pc(inputs["L0_attn_b_p_b"])
    return pre


def kernel(**inputs):
    cfg = FULL_CFG
    Tl = cfg["Tl"]
    x = np.asarray(inputs["L0_attn_a_hidden_states"])
    consts = host_consts()
    ckth = np.full((128, 1), cfg["kth"] - 0.5, np.float32)

    xs = [np.ascontiguousarray(
        x[2 * c:2 * c + 2].reshape(Tl, D).T) for c in range(NCORES)]

    nc1 = build_l1(cfg)
    m_common = {}
    for n in A_WEIGHT_NAMES + A_VEC_NAMES:
        m_common[n] = np.ascontiguousarray(np.asarray(inputs[n], np.float32))
    m_common["gate_wT"] = np.ascontiguousarray(
        np.asarray(inputs["L0_ffn_gate_weight"]).T.astype(np.float32, copy=False))
    pre = _prearranged(inputs)
    m_common["pre_normw"] = pre["pre_normw"]
    m_common["pre_convw"] = pre["pre_convw"]
    m_common["pre_postw"] = pre["pre_postw"]
    m_common.update(consts)
    in_maps = [dict(m_common, x_fm=xs[c]) for c in range(NCORES)]
    r1 = run_bass_kernel_spmd(nc1, in_maps, core_ids=list(range(NCORES))).results

    glg1 = np.ascontiguousarray(
        np.concatenate([r["lgo"].reshape(-1) for r in r1]).reshape(128, cfg["NGC"]))

    nc2 = build_l2(cfg)
    m2 = {}
    for wn, bn in (("q_w1", "q_b1"), ("q_w2", "q_b2"), ("k_w1", "k_b1"),
                   ("k_w2", "k_b2"), ("v_w", "v_b"), ("g_w", "g_b"),
                   ("p_w", "p_b")):
        m2["b_" + wn] = np.ascontiguousarray(
            np.asarray(inputs["L0_attn_b_" + wn], np.float32).T)
        m2["b_" + bn] = np.ascontiguousarray(
            np.asarray(inputs["L0_attn_b_" + bn], np.float32))
    m2["ffn_w1T"] = np.ascontiguousarray(
        np.asarray(inputs["L0_ffn_w1"], np.float32).transpose(0, 2, 1))
    m2["ffn_w2T"] = np.ascontiguousarray(
        np.asarray(inputs["L0_ffn_w2"], np.float32).transpose(0, 2, 1))
    m2["ffn_b1"] = np.ascontiguousarray(np.asarray(inputs["L0_ffn_b1"], np.float32))
    m2["ffn_b2"] = np.ascontiguousarray(np.asarray(inputs["L0_ffn_b2"], np.float32))
    m2["gate_wT"] = m_common["gate_wT"]
    m2["pre_b1"] = pre["pre_b1"]
    m2["pre_bvec"] = pre["pre_bvec"]
    m2["pre_pbv"] = pre["pre_pbv"]
    m2["glg"] = glg1
    m2["c_kth"] = ckth
    m2.update(consts)
    in_maps2 = [dict(m2, xi=r1[c]["xo"], lni=r1[c]["lno"], olg=r1[c]["lgo"])
                for c in range(NCORES)]
    r2 = run_bass_kernel_spmd(nc2, in_maps2, core_ids=list(range(NCORES))).results

    glg2 = np.ascontiguousarray(
        np.concatenate([r["lgo2"].reshape(-1) for r in r2]).reshape(128, cfg["NGC"]))

    nc3 = build_l3(cfg)
    m3 = {k: m2[k] for k in ("ffn_w1T", "ffn_w2T", "ffn_b1", "ffn_b2", "c_kth",
                             "pre_b1")}
    m3["glg"] = glg2
    m3.update(consts)
    in_maps3 = [dict(m3, xi=r2[c]["x3o"], lni=r2[c]["ln3o"], olg=r2[c]["lgo2"])
                for c in range(NCORES)]
    r3 = run_bass_kernel_spmd(nc3, in_maps3, core_ids=list(range(NCORES))).results

    out = np.empty((16, 4096, D), np.float32)
    for c in range(NCORES):
        out[2 * c:2 * c + 2] = r3[c]["out_fm"].T.reshape(2, 4096, D)
    return out



class ScanStream:
    def __init__(self, p, ctx, pctx, C, cfg, sid, grall, brall):
        nc = p.nc
        TB = cfg["TB"]
        t = f"s{sid}"
        self.ld = p.sem(f"b{t}_ld")
        self.mm = p.sem(f"b{t}_mm")
        self.dv = p.sem(f"b{t}_dv")
        self.ac = p.sem(f"b{t}_ac")
        self.st_ = p.sem(f"b{t}_st")
        S = lambda name, shape, dt: Slot(ctx.enter_context(
            nc.sbuf_tensor(f"b{t}_{name}", shape, dt)))
        self.q = S("q", [64, TB], BF16)
        self.k = S("k", [64, TB], BF16)
        self.v = S("v", [128, TB], BF16)
        self.gs = S("gs", [128, TB], BF16)
        self.r0 = 32 * sid
        self.gr = Slot(grall)
        self.br = Slot(brall)
        self.Sst = [ctx.enter_context(nc.sbuf_tensor(f"b{t}_S{i}", [64, 128], F32))
                    for i in range(2)]
        self.S0b = ctx.enter_context(nc.sbuf_tensor(f"b{t}_S0b", [64, 128], BF16))
        self.gc = ctx.enter_context(nc.sbuf_tensor(f"b{t}_gc", [1, 128], F32))
        self.tcol = ctx.enter_context(nc.sbuf_tensor(f"b{t}_tcol", [128, 1], F32))
        self.btm = ctx.enter_context(nc.sbuf_tensor(f"b{t}_btm", [128, 1], F32))
        self.grb = ctx.enter_context(nc.sbuf_tensor(f"b{t}_grb", [128, 128], F32))
        self.gl64 = ctx.enter_context(nc.sbuf_tensor(f"b{t}_gl", [64, 1], F32))
        self.bex = ctx.enter_context(nc.sbuf_tensor(f"b{t}_bex", [64, 128], F32))
        self.dex = ctx.enter_context(nc.sbuf_tensor(f"b{t}_dex", [64, 128], F32))
        self.bc64 = ctx.enter_context(nc.sbuf_tensor(f"b{t}_bc", [64, 1], F32))
        self.f0m = ctx.enter_context(nc.sbuf_tensor(f"b{t}_f0m", [128, 128], F32))
        self.g2 = ctx.enter_context(nc.sbuf_tensor(f"b{t}_g2", [128, 128], F32))
        self.d1 = ctx.enter_context(nc.sbuf_tensor(f"b{t}_d1", [128, 128], F32))
        self.d2 = ctx.enter_context(nc.sbuf_tensor(f"b{t}_d2", [128, 128], F32))
        self.m1a = ctx.enter_context(nc.sbuf_tensor(f"b{t}_m1a", [128, 128], F32))
        self.Mb = ctx.enter_context(nc.sbuf_tensor(f"b{t}_M", [128, 128], BF16))
        self.Acur = [ctx.enter_context(nc.sbuf_tensor(f"b{t}_A{i}", [128, 128], BF16))
                     for i in range(2)]
        self.Bfac = [ctx.enter_context(nc.sbuf_tensor(f"b{t}_B{j}", [128, 128], BF16))
                     for j in range(7)]
        self.Yb = [ctx.enter_context(nc.sbuf_tensor(f"b{t}_Y{i}", [128, 128], BF16))
                   for i in range(2)]
        self.Kb = ctx.enter_context(nc.sbuf_tensor(f"b{t}_Kb", [64, 128], BF16))
        self.Qb = ctx.enter_context(nc.sbuf_tensor(f"b{t}_Qb", [64, 128], BF16))
        self.Kd = ctx.enter_context(nc.sbuf_tensor(f"b{t}_Kd", [64, 128], BF16))
        self.Kdt = ctx.enter_context(nc.sbuf_tensor(f"b{t}_Kdt", [128, 64], BF16))
        self.Vsb = ctx.enter_context(nc.sbuf_tensor(f"b{t}_Vsb", [128, 128], BF16))
        self.w2b = ctx.enter_context(nc.sbuf_tensor(f"b{t}_w2", [128, 128], BF16))
        self.r1 = ctx.enter_context(nc.sbuf_tensor(f"b{t}_r1", [128, 128], F32))
        self.sqo = ctx.enter_context(nc.sbuf_tensor(f"b{t}_sqo", [128, 128], F32))
        self.ms = ctx.enter_context(nc.sbuf_tensor(f"b{t}_ms", [128, 1], F32))
        self.o1 = ctx.enter_context(nc.sbuf_tensor(f"b{t}_o1", [128, 128], F32))
        self.o2 = ctx.enter_context(nc.sbuf_tensor(f"b{t}_o2", [128, 128], BF16))
        self.oo = S("oout", [128, 128], BF16)
        self.pA = Slot(pctx.enter_context(nc.psum_tensor(f"b{t}_pA", [128, 512], F32)))
        self.pAP = Slot(pctx.enter_context(nc.psum_tensor(f"b{t}_pAP", [128, 512], F32)))


def emit_scan_loads(p, C, cfg, io, s, b, h):
    TB = cfg["TB"]
    for (sl_, dram, r0, nr) in (
            (s.q, io["q_s"], 64 * h, 64), (s.k, io["k_s"], 64 * h, 64),
            (s.v, io["v_s"], 128 * h, 128), (s.gs, io["gsil_s"], 128 * h, 128)):
        p.dma("sync", sl_.ap[:, :], dram[r0:r0 + nr, b * TB:(b + 1) * TB],
              wait=sl_.take(), inc=[(s.ld, 16)])
    p.dma("sync", s.gr.ap[s.r0:s.r0 + 1, :], io["g_s"][h:h + 1, b * TB:(b + 1) * TB],
          wait=s.gr.take(), inc=[(s.ld, 16)])
    p.dma("sync", s.br.ap[s.r0:s.r0 + 1, :], io["beta_s"][h:h + 1, b * TB:(b + 1) * TB],
          wait=s.br.take(), inc=[(s.ld, 16)])
    s.ld_done = s.ld.v
    p.do("vector", lambda e: e.memset(s.Sst[0][:, :], 0.0),
         wait=[(s.mm, s.mm.v), (s.ac, s.ac.v), (s.dv, s.dv.v)], inc=[s.dv])


def emit_scan_chunk_steps(p, C, cfg, io, s, b, h, cc):
    TB = cfg["TB"]
    A_ = lambda fn, wait=(), inc=(): p.do("scalar", fn, wait=wait, inc=inc)
    V_ = lambda fn, wait=(), inc=(): p.do("vector", fn, wait=wait, inc=inc)
    T_ = lambda fn, wait=(), inc=(): p.do("tensor", fn, wait=wait, inc=inc)
    sl = slice(cc * CHUNK, (cc + 1) * CHUNK)
    cur, nxt = s.Sst[cc % 2], s.Sst[(cc + 1) % 2]
    st = []

    def S_scan():
        V_(lambda e: e.tensor_tensor_scan(
            s.gc[:, :], s.gr.ap[s.r0:s.r0 + 1, sl],
            io["zrow_sb"][s.r0:s.r0 + 1, :], 0.0, ALU.add, ALU.add),
            wait=[(s.ld, s.ld_done), (s.ac, s.ac.v)], inc=[s.dv])
    st.append(S_scan)

    def S_gcb():
        T_(lambda e: e.matmul(s.pA.ap[:, 384:512], s.gc[:, :],
                              C.ones_row_f[0:1, :], start=True, stop=True),
           wait=[s.dv] + s.pA.take(), inc=[s.mm])
        T_(lambda e: e.matmul(s.pA.ap[:, 256:384], C.ones_row_f[:, :], s.gc[:, :],
                              start=True, stop=True), inc=[s.mm])
        T_(lambda e: e.matmul(s.pAP.ap[:, 384:512], s.br.ap[s.r0:s.r0 + 1, sl],
                              C.ones_f128[s.r0:s.r0 + 1, :], start=True,
                              stop=True),
           wait=[(s.ld, s.ld_done)] + s.pAP.take(), inc=[s.mm])
    st.append(S_gcb)

    def S_evac():
        A_(lambda e: e.activation(s.tcol[:, :], s.pA.ap[:, 384:385], AF.Copy),
           wait=[s.mm], inc=[s.ac])
        A_(lambda e: e.activation(s.grb[:, :], s.pA.ap[:, 256:384], AF.Copy),
           wait=[(s.mm, s.mm.v)], inc=[s.ac])
        A_(lambda e: e.activation(s.gl64[:, :], s.pA.ap[0:64, 383:384], AF.Copy),
           wait=[(s.mm, s.mm.v)], inc=[s.ac])
        A_(lambda e: e.activation(s.bex[:, :], s.pA.ap[0:64, 256:384], AF.Exp),
           wait=[(s.mm, s.mm.v)], inc=[s.ac])
        A_(lambda e: e.activation(s.dex[:, :], s.pA.ap[0:64, 256:384], AF.Exp,
                                  scale=-1.0, bias=s.gl64[:, 0:1]),
           wait=[(s.ac, s.ac.v)], inc=[s.ac])
        A_(lambda e: e.activation(s.bc64[:, :], s.gl64[:, :], AF.Exp),
           wait=[(s.ac, s.ac.v)], inc=[s.ac])
        A_(lambda e: e.activation(s.btm[:, :], s.pAP.ap[:, 384:385], AF.Copy),
           wait=[(s.mm, s.mm.v)], inc=[s.ac])
        s.pAP.mark(s.ac)
        s.pA.mark(s.ac)
        s.evd = s.ac.v
    st.append(S_evac)

    def S_decay():
        V_(lambda e: e.tensor_scalar(s.f0m[:, :], s.grb[:, :], s.tcol[:, 0:1],
                                     0.0, ALU.subtract, ALU.max),
           wait=[(s.ac, s.evd)], inc=[s.dv])
        A_(lambda e: e.activation(s.d1[:, :], s.f0m[:, :], AF.Exp, scale=-1.0),
           wait=[s.dv], inc=[s.ac])
        V_(lambda e: e.scalar_tensor_tensor(s.g2[:, :], s.grb[:, :],
                                            s.tcol[:, 0:1], C.negmask[:, :],
                                            ALU.subtract, ALU.add),
           wait=[(s.ac, s.evd)], inc=[s.dv])
        A_(lambda e: e.activation(s.d2[:, :], s.g2[:, :], AF.Exp),
           wait=[(s.dv, s.dv.v)], inc=[s.ac])
    st.append(S_decay)

    def S_kkq():
        T_(lambda e: e.matmul(s.pA.ap[:, 0:128], s.k.ap[:, sl], s.k.ap[:, sl],
                              start=True, stop=True),
           wait=[(s.ac, s.evd), (s.ld, s.ld_done)], inc=[s.mm])
        T_(lambda e: e.matmul(s.pA.ap[:, 128:256], s.k.ap[:, sl], s.q.ap[:, sl],
                              start=True, stop=True), inc=[s.mm])
        s.kkq = s.mm.v
    st.append(S_kkq)

    def S_m():
        V_(lambda e: e.tensor_mul(s.m1a[:, :], s.pA.ap[:, 0:128], s.d1[:, :]),
           wait=[(s.mm, s.kkq), (s.ac, s.ac.v)], inc=[s.dv])
        V_(lambda e: e.scalar_tensor_tensor(s.Mb[:, :], s.m1a[:, :],
                                            s.btm[:, 0:1], C.slmask[:, :],
                                            ALU.mult, ALU.mult),
           wait=[s.dv], inc=[s.dv])
        T_(lambda e: e.matmul(s.pA.ap[:, 256:384], s.Mb[:, :], C.id128[:, :],
                              start=True, stop=True),
           wait=[s.dv, (s.ac, s.evd)], inc=[s.mm])
        V_(lambda e: e.tensor_copy(s.Bfac[0][:, :], s.pA.ap[:, 256:384]),
           wait=[s.mm], inc=[s.dv])
    st.append(S_m)

    for j_ in range(1, 7):
        def S_sq(j=j_):
            ca, na = s.Acur[(j + 1) % 2], s.Acur[j % 2]
            cb = s.Bfac[j - 1]
            a_src = s.Mb if j == 1 else ca
            T_(lambda e: e.matmul(s.pA.ap[:, 256:384], cb[:, :], a_src[:, :],
                                  start=True, stop=True),
               wait=[(s.dv, s.dv.v)], inc=[s.mm])
            T_(lambda e: e.matmul(s.pA.ap[:, 384:512], a_src[:, :], cb[:, :],
                                  start=True, stop=True), inc=[s.mm])
            V_(lambda e: e.tensor_copy(na[:, :], s.pA.ap[:, 256:384]),
               wait=[s.mm], inc=[s.dv])
            V_(lambda e: e.tensor_copy(s.Bfac[j][:, :], s.pA.ap[:, 384:512]),
               inc=[s.dv])
            if j == 6:
                s.pA.mark(s.dv)
        st.append(S_sq)

    def S_kb():
        V_(lambda e: e.tensor_mul(s.Kb[:, :], s.k.ap[:, sl], s.bex[:, :]),
           wait=[(s.ac, s.evd)], inc=[s.dv])
        V_(lambda e: e.tensor_mul(s.Qb[:, :], s.q.ap[:, sl], s.bex[:, :]),
           inc=[s.dv])
        A_(lambda e: e.activation(s.S0b[:, :], cur[:, :], AF.Copy),
           wait=[(s.dv, s.dv.v)], inc=[s.ac])
        T_(lambda e: e.matmul(s.pAP.ap[:, 0:128], s.Kb[:, :], s.S0b[:, :],
                              start=True, stop=True),
           wait=[s.dv, s.ac] + s.pAP.take(), inc=[s.mm])
    st.append(S_kb)

    def S_vt():
        T_(lambda e: e.matmul(s.pAP.ap[:, 128:256], s.v.ap[:, sl],
                              C.id128[:, :], start=True, stop=True),
           wait=[(s.ld, s.ld_done)], inc=[s.mm])
        V_(lambda e: e.tensor_copy(s.Vsb[:, :], s.pAP.ap[:, 128:256]),
           wait=[s.mm], inc=[s.dv])
        V_(lambda e: e.tensor_sub(s.r1[:, :], s.Vsb[:, :], s.pAP.ap[:, 0:128]),
           wait=[(s.mm, s.mm.v), s.dv], inc=[s.dv])
        s.pAP.mark(s.dv)
        V_(lambda e: e.tensor_scalar(s.Yb[0][:, :], s.r1[:, :], s.btm[:, 0:1],
                                     None, ALU.mult), wait=[s.dv], inc=[s.dv])
    st.append(S_vt)

    for ji_, j_ in enumerate(range(6, -1, -1)):
        def S_ap(ji=ji_, j=j_):
            yc, yn = s.Yb[ji % 2], s.Yb[(ji + 1) % 2]
            T_(lambda e: e.matmul(s.pAP.ap[:, 0:128], s.Bfac[j][:, :], yc[:, :],
                                  start=True, stop=True),
               wait=[(s.dv, s.dv.v)] + s.pAP.take(), inc=[s.mm])
            V_(lambda e: e.tensor_sub(yn[:, :], yc[:, :], s.pAP.ap[:, 0:128]),
               wait=[s.mm], inc=[s.dv])
            s.pAP.mark(s.dv)
        st.append(S_ap)

    def S_kd():
        U = s.Yb[1]
        V_(lambda e: e.tensor_mul(s.Kd[:, :], s.k.ap[:, sl], s.dex[:, :]),
           wait=[(s.ac, s.evd)], inc=[s.dv])
        T_(lambda e: e.matmul(s.pAP.ap[:, 0:64], s.Kd[:, :],
                              C.id128[0:64, 0:64], start=True, stop=True),
           wait=[(s.dv, s.dv.v)] + s.pAP.take(), inc=[s.mm])
        V_(lambda e: e.tensor_copy(s.Kdt[:, :], s.pAP.ap[:, 0:64]),
           wait=[s.mm], inc=[s.dv])
        s.pAP.mark(s.dv)
    st.append(S_kd)

    def S_su():
        U = s.Yb[1]
        T_(lambda e: e.matmul(s.pAP.ap[0:64, 256:384], s.Kdt[:, :], U[:, :],
                              start=True, stop=True),
           wait=[(s.dv, s.dv.v)] + s.pAP.take(), inc=[s.mm])
        V_(lambda e: e.scalar_tensor_tensor(
            nxt[:, :], cur[:, :], s.bc64[:, 0:1], s.pAP.ap[0:64, 256:384],
            ALU.mult, ALU.add), wait=[s.mm, (s.ac, s.ac.v)], inc=[s.dv])
        s.pAP.mark(s.dv)
    st.append(S_su)

    def S_omm():
        U = s.Yb[1]
        V_(lambda e: e.tensor_mul(s.w2b[:, :], s.pA.ap[:, 128:256], s.d2[:, :]),
           wait=[(s.ac, s.ac.v), (s.mm, s.kkq), (s.dv, s.dv.v)], inc=[s.dv])
        s.pA.mark(s.dv)
        T_(lambda e: e.matmul(s.pAP.ap[:, 128:256], s.w2b[:, :], U[:, :],
                              start=True, stop=False),
           wait=[(s.dv, s.dv.v)] + s.pAP.take(), inc=[s.mm])
        T_(lambda e: e.matmul(s.pAP.ap[:, 128:256], s.Qb[:, :], s.S0b[:, :],
                              start=False, stop=True), inc=[s.mm])
    st.append(S_omm)

    def S_onorm():
        A_(lambda e: e.activation(s.sqo[:, :], s.pAP.ap[:, 128:256], AF.Square),
           wait=[s.mm], inc=[s.ac])
        V_(lambda e: e.reduce_sum(s.ms[:, :], s.sqo[:, :], axis=AX),
           wait=[s.ac], inc=[s.dv])
        A_(lambda e: e.activation(s.ms[:, :], s.ms[:, :], AF.Ln, scale=1.0 / DV,
                                  bias=EPS), wait=[s.dv], inc=[s.ac])
        A_(lambda e: e.activation(s.ms[:, :], s.ms[:, :], AF.Exp, scale=-0.5),
           wait=[s.ac], inc=[s.ac])
    st.append(S_onorm)

    def S_oout():
        V_(lambda e: e.tensor_scalar(s.o1[:, :], s.pAP.ap[:, 128:256],
                                     s.ms[:, 0:1], None, ALU.mult),
           wait=[(s.ac, s.ac.v)], inc=[s.dv])
        s.pAP.mark(s.dv)
        V_(lambda e: e.tensor_mul(s.o2[:, :], s.o1[:, :], io["onwB_sb"][:, :]),
           wait=[s.dv], inc=[s.dv])
        T_(lambda e: e.matmul(s.pA.ap[:, 128:256], s.o2[:, :], C.id128[:, :],
                              start=True, stop=True),
           wait=[(s.dv, s.dv.v)], inc=[s.mm])
        oow = s.oo.take()
        V_(lambda e: e.tensor_mul(s.oo.ap[:, :], s.pA.ap[:, 128:256],
                                  s.gs.ap[:, sl]),
           wait=[s.mm] + oow, inc=[s.dv])
        s.pA.mark(s.dv)
        p.dma("gpsimd",
              io["ofm_s"][128 * h:128 * (h + 1),
                          b * TB + cc * CHUNK: b * TB + (cc + 1) * CHUNK],
              s.oo.ap[:, :], wait=[(s.dv, s.dv.v)], inc=[(s.oo.st(p), 16)])
        s.oo.mark(s.oo.st(p))
    st.append(S_oout)
    return st


def emit_scan_chunk(p, C, cfg, io, s, b, h, cc):
    for f in emit_scan_chunk_steps(p, C, cfg, io, s, b, h, cc):
        f()


def emit_stage_b2(p, ctx, C, cfg, io):
    from contextlib import ExitStack
    pctx = ExitStack()
    nc = p.nc
    Tl, TB = cfg["Tl"], cfg["TB"]
    NB = Tl // TB
    NCH = TB // CHUNK

    s_ld0 = p.sem("b2_ld0")
    s_mm0 = p.sem("b2_mm0")
    s_ac0 = p.sem("b2_ac0")
    s_dv0 = p.sem("b2_dv0")
    onw_row = ctx.enter_context(nc.sbuf_tensor("b2_onwr", [1, 128], F32))
    onwB = ctx.enter_context(nc.sbuf_tensor("b2_onwB", [128, 128], F32))
    zrow = ctx.enter_context(nc.sbuf_tensor("b2_zrow", [65, 128], F32))
    grall = ctx.enter_context(nc.sbuf_tensor("b2_grall", [65, TB], F32))
    brall = ctx.enter_context(nc.sbuf_tensor("b2_brall", [65, TB], F32))
    io["zrow_sb"] = zrow
    io["onwB_sb"] = onwB
    P_tmp = Slot(pctx.enter_context(nc.psum_tensor("b2_ptmp", [128, 512], F32)))
    p.dma("sync", onw_row[:, :], io["L0_attn_a_o_norm_w"].rearrange("(o d) -> o d", o=1),
          inc=[(s_ld0, 16)])
    p.do("vector", lambda e: e.memset(zrow[:, :], 0.0), inc=[s_dv0])
    p.do("tensor", lambda e: e.matmul(
        P_tmp.ap[:, 0:128], C.ones_row_f[:, :], onw_row[:, :],
        start=True, stop=True), wait=[s_ld0], inc=[s_mm0])
    p.do("scalar", lambda e: e.activation(onwB[:, :], P_tmp.ap[:, 0:128], AF.Copy),
         wait=[s_mm0], inc=[s_ac0])
    P_tmp.mark(s_ac0)
    p.barrier("b2_init")
    pctx.close()

    from contextlib import ExitStack as ES
    pctx2 = ES()
    NS = 3
    streams = [ScanStream(p, ctx, pctx2, C, cfg, i, grall, brall)
               for i in range(NS)]
    pairs = [(b, h) for b in range(NB) for h in range(NH)]
    for i in range(0, len(pairs), NS):
        duo = pairs[i:i + NS]
        for k, (b, h) in enumerate(duo):
            emit_scan_loads(p, C, cfg, io, streams[k], b, h)
        for cc in range(NCH):
            lists = [emit_scan_chunk_steps(p, C, cfg, io, streams[k], b, h, cc)
                     for k, (b, h) in enumerate(duo)]
            for i in range(max(len(l) for l in lists)):
                for l in lists:
                    if i < len(l):
                        l[i]()
        for k in range(len(duo)):
            s = streams[k]
            for sl_ in (s.q, s.k, s.v, s.gs, s.gr, s.br):
                sl_.mark(s.dv)
                sl_.mark(s.mm)
    p.barrier("b2_end")
    pctx2.close()



class ScanStream3:
    def __init__(self, p, ctx, pctx, C, cfg, sid):
        nc = p.nc
        TB = cfg["TB"]
        t = f"v{sid}"
        self.ld = p.sem(f"b{t}_ld")
        self.mm = p.sem(f"b{t}_mm")
        self.dv = p.sem(f"b{t}_dv")
        self.ac = p.sem(f"b{t}_ac")
        S = lambda name, shape, dt: Slot(ctx.enter_context(
            nc.sbuf_tensor(f"b{t}_{name}", shape, dt)))
        T = lambda name, shape, dt: ctx.enter_context(
            nc.sbuf_tensor(f"b{t}_{name}", shape, dt))
        TB2 = cfg["TB2"]
        self.qk = [S("q", [128, TB2], BF16), S("k", [128, TB2], BF16)]
        self.v = [S("v0", [128, TB2], BF16), S("v1", [128, TB2], BF16)]
        self.gs = [S("gs0", [128, TB2], BF16), S("gs1", [128, TB2], BF16)]
        self.gbg = S("gbg", [64, TB2], F32)
        self.gbb = S("gbb", [64, TB2], F32)
        self.Sst = [T(f"S{i}", [128, 128], F32) for i in range(2)]
        self.S0b = T("S0b", [128, 128], BF16)
        self.gc = [T("gc0", [1, 128], F32), T("gc1", [1, 128], F32)]
        self.bt = [T("bt0", [1, 128], F32), T("bt1", [1, 128], F32)]
        self.grb = T("grb", [128, 256], F32)
        self.gcol = T("gcol", [128, 256], F32)
        self.bcol = T("bcol", [128, 256], F32)
        self.glP = T("glP", [128, 1], F32)
        self.bexP = T("bexP", [128, 128], F32)
        self.dexP = T("dexP", [128, 128], F32)
        self.bcP = T("bcP", [128, 1], F32)
        self.t1 = T("t1", [128, 256], F32)
        self.md = T("md", [128, 512], F32)
        self.dd = T("dd", [128, 512], F32)
        self.m1a = T("m1a", [128, 256], F32)
        self.m1b = T("m1b", [128, 256], F32)
        self.Mb = T("Mb", [128, 256], BF16)
        self.Bf0 = T("Bf0", [128, 256], BF16)
        self.LVL = [T(f"L{j}", [128, 512], BF16) for j in range(1, 7)]
        self.Yb = [T(f"Y{i}", [128, 256], BF16) for i in range(2)]
        self.KbP = T("KbP", [128, 128], BF16)
        self.QbP = T("QbP", [128, 128], BF16)
        self.KdP = T("KdP", [128, 128], BF16)
        self.KdtP = T("KdtP", [128, 128], BF16)
        self.VsbP = T("VsbP", [128, 256], BF16)
        self.w2bP = T("w2bP", [128, 256], BF16)
        self.r1 = T("r1", [128, 256], F32)
        self.sqo = T("sqo", [128, 256], F32)
        self.ms = T("ms", [128, 2], F32)
        self.o1 = T("o1", [128, 256], F32)
        self.o2 = T("o2", [128, 256], BF16)
        self.oo = [S("oo0", [128, 128], BF16), S("oo1", [128, 128], BF16)]
        self.pKK = Slot(pctx.enter_context(nc.psum_tensor(f"b{t}_pKK", [128, 512], F32)))
        self.pW = Slot(pctx.enter_context(nc.psum_tensor(f"b{t}_pW", [128, 512], F32)))
        self.pAP = Slot(pctx.enter_context(nc.psum_tensor(f"b{t}_pAP", [128, 512], F32)))
        self.pF = Slot(pctx.enter_context(nc.psum_tensor(f"b{t}_pF", [128, 1024], BF16)))


def emit_scan3_loads(p, C, cfg, io, s, b, hp, seg, first):
    TB, TB2 = cfg["TB"], cfg["TB2"]
    h0 = 2 * hp
    c0 = b * TB + seg * TB2
    for i, (sl_, dram, r0) in enumerate((
            (s.qk[0], io["q_s"], 128 * hp), (s.qk[1], io["k_s"], 128 * hp),
            (s.v[0], io["v_s"], 128 * h0), (s.v[1], io["v_s"], 128 * (h0 + 1)),
            (s.gs[0], io["gsil_s"], 128 * h0), (s.gs[1], io["gsil_s"], 128 * (h0 + 1)))):
        p.dma("sync", sl_.ap[:, :], dram[r0:r0 + 128, c0:c0 + TB2],
              wait=sl_.take(), inc=[(s.ld, 16)])
    gw = s.gbg.take()
    bw = s.gbb.take()
    for i in range(2):
        p.dma("sync", s.gbg.ap[32 * i:32 * i + 1, :],
              io["g_s"][h0 + i:h0 + i + 1, c0:c0 + TB2],
              wait=(gw if i == 0 else []), inc=[(s.ld, 16)])
        p.dma("sync", s.gbb.ap[32 * i:32 * i + 1, :],
              io["beta_s"][h0 + i:h0 + i + 1, c0:c0 + TB2],
              wait=(bw if i == 0 else []), inc=[(s.ld, 16)])
    s.ld_done = s.ld.v
    if first:
        p.do("vector", lambda e: e.memset(s.Sst[0][:, :], 0.0),
             wait=[(s.mm, s.mm.v), (s.ac, s.ac.v), (s.dv, s.dv.v)], inc=[s.dv])


def emit_scan3_chunk(p, C, cfg, io, s, b, hp, cc):
    TB = cfg["TB"]
    h0 = 2 * hp
    A_ = lambda fn, wait=(), inc=(): p.do("scalar", fn, wait=wait, inc=inc)
    V_ = lambda fn, wait=(), inc=(): p.do("vector", fn, wait=wait, inc=inc)
    T_ = lambda fn, wait=(), inc=(): p.do("tensor", fn, wait=wait, inc=inc)
    import os as _os
    BCUT3 = float(_os.environ.get("BCUT3", "99"))
    NCHS = cfg["TB2"] // CHUNK
    sl = slice((cc % NCHS) * CHUNK, (cc % NCHS + 1) * CHUNK)
    cur, nxt = s.Sst[cc % 2], s.Sst[(cc + 1) % 2]
    H = [slice(0, 128), slice(128, 256)]
    R_ = [slice(0, 64), slice(64, 128)]

    for i in range(2):
        V_(lambda e, i=i: e.tensor_tensor_scan(
            s.gc[i][:, :], s.gbg.ap[32 * i:32 * i + 1, sl],
            io["zrow_sb"][32 * i:32 * i + 1, :],
            0.0, ALU.add, ALU.add),
            wait=[(s.ld, s.ld_done), (s.ac, s.ac.v)], inc=[s.dv])
    if BCUT3 < 1.2:
        return
    for i in range(2):
        T_(lambda e, i=i: e.matmul(s.pW.ap[:, i * 128:(i + 1) * 128],
                                   C.ones_row_f[:, :], s.gc[i][:, :],
                                   start=True, stop=True),
           wait=[(s.dv, s.dv.v)] + (s.pW.take() if i == 0 else []), inc=[s.mm])
    A_(lambda e: e.activation(s.grb[:, :], s.pW.ap[:, 0:256], AF.Copy),
       wait=[s.mm], inc=[s.ac])
    s.pW.mark(s.ac)
    if BCUT3 < 1.4:
        return
    for i in range(2):
        T_(lambda e, i=i: e.matmul(s.pW.ap[:, 256 + i * 128:256 + (i + 1) * 128],
                                   s.gc[i][:, :], C.ones_row_f[0:1, :],
                                   start=True, stop=True),
           wait=(s.pW.take() if i == 0 else []), inc=[s.mm])
    A_(lambda e: e.activation(s.gcol[:, :], s.pW.ap[:, 256:512], AF.Copy),
       wait=[s.mm], inc=[s.ac])
    s.pW.mark(s.ac)
    if BCUT3 < 1.6:
        return
    for i in range(2):
        V_(lambda e, i=i, sl=sl: e.tensor_copy(
            s.bt[i][:, :], s.gbb.ap[32 * i:32 * i + 1, sl]),
           wait=[(s.ld, s.ld_done)], inc=[s.dv])
    for i in range(2):
        T_(lambda e, i=i: e.matmul(
            s.pW.ap[:, 256 + i * 128:256 + (i + 1) * 128],
            s.bt[i][:, :], C.ones_row_f[0:1, :],
            start=True, stop=True),
           wait=[(s.dv, s.dv.v)] + (s.pW.take() if i == 0 else []), inc=[s.mm])
    A_(lambda e: e.activation(s.bcol[:, :], s.pW.ap[:, 256:512], AF.Copy),
       wait=[s.mm], inc=[s.ac])
    s.pW.mark(s.ac)
    if BCUT3 < 2:
        return
    for i in range(2):
        A_(lambda e, i=i: e.activation(s.glP[R_[i], :],
                                       s.grb[R_[i], H[i].start + 127:H[i].start + 128],
                                       AF.Copy),
           wait=[(s.ac, s.ac.v)], inc=[s.ac])
        if BCUT3 < 2.2:
            continue
        A_(lambda e, i=i: e.activation(s.bexP[R_[i], :], s.grb[R_[i], H[i]], AF.Exp),
           wait=[(s.ac, s.ac.v)], inc=[s.ac])
        if BCUT3 < 2.4:
            continue
        A_(lambda e, i=i: e.activation(s.dexP[R_[i], :], s.grb[R_[i], H[i]], AF.Exp,
                                       scale=-1.0, bias=s.glP[R_[i], 0:1]),
           wait=[(s.ac, s.ac.v)], inc=[s.ac])
    if BCUT3 < 2.6:
        return
    A_(lambda e: e.activation(s.bcP[:, :], s.glP[:, :], AF.Exp),
       wait=[(s.ac, s.ac.v)], inc=[s.ac])
    if BCUT3 < 2.8:
        return
    V_(lambda e: e.tensor_sub(s.t1[:, :], s.grb[:, :], s.gcol[:, :]),
       wait=[(s.ac, s.ac.v)], inc=[s.dv])
    if BCUT3 < 2.85:
        return
    V_(lambda e: e.tensor_scalar(s.md[:, 0:256], s.t1[:, :], -1.0, 0.0,
                                 ALU.mult, ALU.min),
       wait=[s.dv], inc=[s.dv])
    if BCUT3 < 2.9:
        return
    V_(lambda e: e.tensor_add(s.md[:, 256:512], s.t1[:, :], C.negmask2[:, :]),
       wait=[(s.dv, s.dv.v)], inc=[s.dv])
    if BCUT3 < 2.95:
        return
    A_(lambda e: e.activation(s.dd[:, :], s.md[:, :], AF.Exp),
       wait=[(s.dv, s.dv.v)], inc=[s.ac])
    if BCUT3 < 3:
        return
    for i in range(2):
        T_(lambda e, i=i, sl=sl: e.matmul(
            s.pKK.ap[:, i * 128:(i + 1) * 128],
            s.qk[1].ap[R_[i], sl], s.qk[1].ap[R_[i], sl],
            start=True, stop=True),
           wait=[(s.ld, s.ld_done)] + (s.pKK.take() if i == 0 else []), inc=[s.mm])
        T_(lambda e, i=i, sl=sl: e.matmul(
            s.pKK.ap[:, 256 + i * 128:256 + (i + 1) * 128],
            s.qk[1].ap[R_[i], sl], s.qk[0].ap[R_[i], sl],
            start=True, stop=True), inc=[s.mm])
    kkq = s.mm.v
    if BCUT3 < 3.3:
        return
    V_(lambda e: e.tensor_mul(s.m1a[:, :], s.pKK.ap[:, 0:256], s.dd[:, 0:256]),
       wait=[(s.mm, kkq), (s.ac, s.ac.v)], inc=[s.dv])
    V_(lambda e: e.tensor_mul(s.m1b[:, :], s.m1a[:, :], s.bcol[:, :]),
       wait=[s.dv], inc=[s.dv])
    V_(lambda e: e.tensor_mul(s.Mb[:, :], s.m1b[:, :], C.slmask2[:, :]),
       wait=[s.dv], inc=[s.dv])
    for i in range(2):
        T_(lambda e, i=i: e.transpose(s.pF.ap[:, i * 128:(i + 1) * 128],
                                      s.Mb[:, H[i]], C.id128[:, :]),
           wait=[(s.dv, s.dv.v)] + (s.pF.take() if i == 0 else []), inc=[s.mm])
    V_(lambda e: e.tensor_copy(s.Bf0[:, :], s.pF.ap[:, 0:256]),
       wait=[s.mm], inc=[s.dv])
    s.pF.mark(s.dv)
    if BCUT3 < 3.6:
        return
    for j in range(1, 7):
        Aprev = s.Mb if j == 1 else s.LVL[j - 2]
        aof = 0 if j == 1 else 0
        Bprev = s.Bf0 if j == 1 else None
        for i in range(2):
            if j == 1:
                asrc = s.Mb[:, H[i]]
                bsrc = s.Bf0[:, H[i]]
            else:
                asrc = s.LVL[j - 2][:, i * 128:(i + 1) * 128]
                bsrc = s.LVL[j - 2][:, 256 + i * 128:256 + (i + 1) * 128]
            T_(lambda e, i=i, asrc=asrc, bsrc=bsrc: e.matmul(
                s.pW.ap[:, i * 128:(i + 1) * 128], bsrc, asrc,
                start=True, stop=True),
               wait=[(s.dv, s.dv.v)] + (s.pW.take() if i == 0 else []), inc=[s.mm])
            T_(lambda e, i=i, asrc=asrc, bsrc=bsrc: e.matmul(
                s.pW.ap[:, 256 + i * 128:256 + (i + 1) * 128], asrc, bsrc,
                start=True, stop=True), inc=[s.mm])
        V_(lambda e, j=j: e.tensor_copy(s.LVL[j - 1][:, :], s.pW.ap[:, :]),
           wait=[s.mm], inc=[s.dv])
        s.pW.mark(s.dv)
    if BCUT3 < 4:
        return
    V_(lambda e, sl=sl: e.tensor_mul(s.KbP[:, :], s.qk[1].ap[:, sl], s.bexP[:, :]),
       wait=[(s.ac, s.ac.v)], inc=[s.dv])
    V_(lambda e, sl=sl: e.tensor_mul(s.QbP[:, :], s.qk[0].ap[:, sl], s.bexP[:, :]),
       inc=[s.dv])
    A_(lambda e, cur=cur: e.activation(s.S0b[:, :], cur[:, :], AF.Copy),
       wait=[(s.dv, s.dv.v)], inc=[s.ac])
    for i in range(2):
        T_(lambda e, i=i: e.matmul(s.pAP.ap[:, i * 128:(i + 1) * 128],
                                   s.KbP[R_[i], :], s.S0b[R_[i], :],
                                   start=True, stop=True),
           wait=[s.dv, s.ac] + (s.pAP.take() if i == 0 else []), inc=[s.mm])
    for i in range(2):
        T_(lambda e, i=i, sl=sl: e.transpose(
            s.pF.ap[:, 256 + i * 128:256 + (i + 1) * 128], s.v[i].ap[:, sl],
            C.id128[:, :]),
           wait=s.pF.take() if i == 0 else [], inc=[s.mm])
    V_(lambda e: e.tensor_copy(s.VsbP[:, :], s.pF.ap[:, 256:512]),
       wait=[s.mm], inc=[s.dv])
    s.pF.mark(s.dv)
    V_(lambda e: e.tensor_sub(s.r1[:, :], s.VsbP[:, :], s.pAP.ap[:, 0:256]),
       wait=[(s.mm, s.mm.v), s.dv], inc=[s.dv])
    s.pAP.mark(s.dv)
    V_(lambda e: e.tensor_mul(s.Yb[0][:, :], s.r1[:, :], s.bcol[:, :]),
       wait=[s.dv], inc=[s.dv])
    if BCUT3 < 5:
        return
    for ji, j in enumerate(range(6, -1, -1)):
        yc, yn = s.Yb[ji % 2], s.Yb[(ji + 1) % 2]
        Bsl = (s.Bf0 if j == 0 else s.LVL[j - 1])
        bof = 0 if j == 0 else 256
        for i in range(2):
            T_(lambda e, i=i, yc=yc, Bsl=Bsl, bof=bof: e.matmul(
                s.pAP.ap[:, i * 128:(i + 1) * 128],
                Bsl[:, bof + i * 128:bof + (i + 1) * 128], yc[:, H[i]],
                start=True, stop=True),
               wait=[(s.dv, s.dv.v)] + (s.pAP.take() if i == 0 else []), inc=[s.mm])
        V_(lambda e, yc=yc, yn=yn: e.tensor_sub(yn[:, :], yc[:, :],
                                                s.pAP.ap[:, 0:256]),
           wait=[s.mm], inc=[s.dv])
        s.pAP.mark(s.dv)
    U = s.Yb[1]
    V_(lambda e: e.tensor_mul(s.w2bP[:, :], s.pKK.ap[:, 256:512], s.dd[:, 256:512]),
       wait=[(s.ac, s.ac.v), (s.mm, kkq)], inc=[s.dv])
    s.pKK.mark(s.dv)
    for i in range(2):
        T_(lambda e, i=i: e.matmul(s.pAP.ap[:, i * 128:(i + 1) * 128],
                                   s.w2bP[:, H[i]], U[:, H[i]],
                                   start=True, stop=False),
           wait=[(s.dv, s.dv.v)] + (s.pAP.take() if i == 0 else []), inc=[s.mm])
        T_(lambda e, i=i: e.matmul(s.pAP.ap[:, i * 128:(i + 1) * 128],
                                   s.QbP[R_[i], :], s.S0b[R_[i], :],
                                   start=False, stop=True), inc=[s.mm])
    A_(lambda e: e.activation(s.sqo[:, :], s.pAP.ap[:, 0:256], AF.Square),
       wait=[s.mm], inc=[s.ac])
    V_(lambda e: e.tensor_reduce(
        s.ms[:, :], s.sqo[:, :].rearrange("p (h f) -> p h f", h=2), AX, ALU.add),
       wait=[s.ac], inc=[s.dv])
    A_(lambda e: e.activation(s.ms[:, :], s.ms[:, :], AF.Ln, scale=1.0 / DV,
                              bias=EPS), wait=[s.dv], inc=[s.ac])
    A_(lambda e: e.activation(s.ms[:, :], s.ms[:, :], AF.Exp, scale=-0.5),
       wait=[s.ac], inc=[s.ac])
    for i in range(2):
        V_(lambda e, i=i: e.tensor_scalar(s.o1[:, H[i]], s.pAP.ap[:, i * 128:(i + 1) * 128],
                                          s.ms[:, i:i + 1], None, ALU.mult),
           wait=[(s.ac, s.ac.v)], inc=[s.dv])
    s.pAP.mark(s.dv)
    V_(lambda e: e.tensor_mul(s.o2[:, :], s.o1[:, :], io["onw2_sb"][:, :]),
       wait=[(s.dv, s.dv.v)], inc=[s.dv])
    for i in range(2):
        T_(lambda e, i=i: e.transpose(s.pF.ap[:, i * 128:(i + 1) * 128],
                                      s.o2[:, H[i]], C.id128[:, :]),
           wait=[(s.dv, s.dv.v)] + (s.pF.take() if i == 0 else []), inc=[s.mm])
    for i in range(2):
        oow = s.oo[i].take()
        V_(lambda e, i=i, sl=sl: e.tensor_mul(
            s.oo[i].ap[:, :], s.pF.ap[:, i * 128:(i + 1) * 128], s.gs[i].ap[:, sl]),
           wait=[s.mm] + oow, inc=[s.dv])
    s.pF.mark(s.dv)
    for i in range(2):
        p.dma("gpsimd",
              io["ofm_s"][128 * (h0 + i):128 * (h0 + i + 1),
                          b * TB + cc * CHUNK: b * TB + (cc + 1) * CHUNK],
              s.oo[i].ap[:, :], wait=[(s.dv, s.dv.v)], inc=[(s.oo[i].st(p), 16)])
        s.oo[i].mark(s.oo[i].st(p))
    if BCUT3 < 6:
        return
    V_(lambda e, sl=sl: e.tensor_mul(s.KdP[:, :], s.qk[1].ap[:, sl], s.dexP[:, :]),
       wait=[(s.ac, s.ac.v)], inc=[s.dv])
    for i in range(2):
        T_(lambda e, i=i: e.transpose(s.pF.ap[:, 512 + i * 64:512 + (i + 1) * 64],
                                      s.KdP[R_[i], :], C.id128[R_[i], R_[i]]),
           wait=[(s.dv, s.dv.v)] + (s.pF.take() if i == 0 else []), inc=[s.mm])
    V_(lambda e: e.tensor_copy(s.KdtP[:, :], s.pF.ap[:, 512:640]),
       wait=[s.mm], inc=[s.dv])
    s.pF.mark(s.dv)
    for i in range(2):
        T_(lambda e, i=i: e.matmul(s.pAP.ap[0:64, i * 128:(i + 1) * 128],
                                   s.KdtP[:, i * 64:(i + 1) * 64], U[:, H[i]],
                                   start=True, stop=True),
           wait=[(s.dv, s.dv.v)] + (s.pAP.take() if i == 0 else []), inc=[s.mm])
    for i in range(2):
        V_(lambda e, i=i, cur=cur, nxt=nxt: e.scalar_tensor_tensor(
            nxt[R_[i], :], cur[R_[i], :], s.bcP[R_[i], 0:1],
            s.pAP.ap[0:64, i * 128:(i + 1) * 128], ALU.mult, ALU.add),
           wait=[s.mm, (s.ac, s.ac.v)], inc=[s.dv])
    s.pAP.mark(s.dv)


def emit_stage_b3(p, ctx, C, cfg, io):
    from contextlib import ExitStack
    pctx = ExitStack()
    nc = p.nc
    Tl, TB = cfg["Tl"], cfg["TB"]
    NB = Tl // TB
    NCH = TB // CHUNK

    s_ld0 = p.sem("b3_ld0")
    s_mm0 = p.sem("b3_mm0")
    s_ac0 = p.sem("b3_ac0")
    s_dv0 = p.sem("b3_dv0")
    onw_row = ctx.enter_context(nc.sbuf_tensor("b3_onwr", [1, 128], F32))
    onw2 = ctx.enter_context(nc.sbuf_tensor("b3_onw2", [128, 256], F32))
    zrow = ctx.enter_context(nc.sbuf_tensor("b3_zrow", [64, 128], F32))
    io["zrow_sb"] = zrow
    io["onw2_sb"] = onw2
    P_tmp = Slot(pctx.enter_context(nc.psum_tensor("b3_ptmp", [128, 512], F32)))
    p.dma("sync", onw_row[:, :], io["L0_attn_a_o_norm_w"].rearrange("(o d) -> o d", o=1),
          inc=[(s_ld0, 16)])
    p.do("vector", lambda e: e.memset(zrow[:, :], 0.0), inc=[s_dv0])
    p.do("tensor", lambda e: e.matmul(
        P_tmp.ap[:, 0:128], C.ones_row_f[:, :], onw_row[:, :],
        start=True, stop=True), wait=[s_ld0], inc=[s_mm0])
    p.do("scalar", lambda e: e.activation(onw2[:, 0:128], P_tmp.ap[:, 0:128], AF.Copy),
         wait=[s_mm0], inc=[s_ac0])
    p.do("scalar", lambda e: e.activation(onw2[:, 128:256], P_tmp.ap[:, 0:128], AF.Copy),
         wait=[(s_mm0, s_mm0.v)], inc=[s_ac0])
    P_tmp.mark(s_ac0)
    p.barrier("b3_init")
    pctx.close()

    from contextlib import ExitStack as ES
    pctx2 = ES()
    streams = [ScanStream3(p, ctx, pctx2, C, cfg, i) for i in range(2)]
    NCHS = cfg["TB2"] // CHUNK
    pairs = [(b, hp) for b in range(NB) for hp in range(NH // 2)]
    for i in range(0, len(pairs), 2):
        duo = pairs[i:i + 2]
        for seg in range(NCH // NCHS):
            for k, (b, hp) in enumerate(duo):
                emit_scan3_loads(p, C, cfg, io, streams[k], b, hp, seg, seg == 0)
            for cc in range(seg * NCHS, (seg + 1) * NCHS):
                for k, (b, hp) in enumerate(duo):
                    emit_scan3_chunk(p, C, cfg, io, streams[k], b, hp, cc)
            for k in range(len(duo)):
                s = streams[k]
                for group in (s.qk, s.v, s.gs, [s.gbg, s.gbb]):
                    for sl_ in group:
                        sl_.mark(s.dv)
                        sl_.mark(s.mm)
    p.barrier("b3_end")
    pctx2.close()


# version 07
# speedup vs baseline: 1.0508x; 1.0508x over previous
import numpy as np
import ml_dtypes
from contextlib import ExitStack
import concourse.bass as bass
import concourse.mybir as mybir
from concourse.bass_utils import run_bass_kernel_spmd

F32 = mybir.dt.float32
BF16 = mybir.dt.bfloat16
AF = mybir.ActivationFunctionType
ALU = mybir.AluOpType
AX = mybir.AxisListType.X

NCORES = 8
D = 1024
NH = 16
DK = 64
DV = 128
VD = 2048
FF = 2752
NE = 8
FH = 4096
HD = 64
CHUNK = 128
LN_EPS = 1e-5
EPS = 1e-6
QSCALE = DK ** (-0.5)
BSCALE = HD ** (-0.5)

ENGINES = ("tensor", "vector", "scalar", "gpsimd", "sync")

BF = ml_dtypes.bfloat16


class Sem:
    __slots__ = ("h", "v", "name")

    def __init__(self, h, name):
        self.h = h
        self.v = 0
        self.name = name


class Slot:
    __slots__ = ("ap", "marks", "stsem")

    def __init__(self, ap):
        self.ap = ap
        self.marks = []
        self.stsem = None

    def take(self):
        m, self.marks = self.marks, []
        return m

    def mark(self, sem):
        self.marks.append((sem, sem.v))

    def st(self, p):
        if self.stsem is None:
            self.stsem = p.sem()
        return self.stsem


class Rot:

    def __init__(self, aps):
        self.slots = [Slot(a) for a in aps]
        self.i = 0

    def next(self):
        s = self.slots[self.i % len(self.slots)]
        self.i += 1
        return s

    def mark_all(self, sem):
        for s in self.slots:
            s.marks = [(sem, sem.v)]


class Prog:

    def __init__(self, nc, ctx):
        self.nc = nc
        self.ctx = ctx
        self.q = {e: [] for e in ENGINES}
        self.wm = {e: {} for e in ENGINES}
        self.nsem = 0
        self.sems = []

    def sem(self, name=None):
        name = name or f"s{self.nsem}"
        self.nsem += 1
        h = self.ctx.enter_context(self.nc.semaphore(name))
        s = Sem(h, name)
        self.sems.append(s)
        return s

    def do(self, eng, fn, wait=(), inc=()):
        resolved = []
        for w in wait:
            s, val = (w if isinstance(w, tuple) else (w, None))
            val = s.v if val is None else val
            if val > 0 and self.wm[eng].get(s, 0) < val:
                resolved.append((s, val))
                self.wm[eng][s] = val
        incs = []
        for i in inc:
            s, n = (i if isinstance(i, tuple) else (i, 1))
            s.v += n
            incs.append((s, n))
        self.q[eng].append((resolved, fn, incs))

    def dma(self, eng, out, in_, wait=(), inc=(), **kw):
        self.do(eng, lambda e: e.dma_start(out=out, in_=in_, **kw),
                wait=wait, inc=inc)

    def barrier(self, name=None, engines=ENGINES):
        b = self.sem(name or f"bar{self.nsem}")
        n = len(engines)
        flush = [(s, s.v) for s in self.sems if s.v > 0 and s is not b]
        for e in engines:
            self.do(e, lambda eng: eng.nop(), wait=flush, inc=[b])
        for e in engines:
            self.do(e, lambda eng: eng.nop(), wait=[(b, n)])
        return b

    def build_block(self):
        with self.nc.Block() as block:
            for ename in ENGINES:
                ops = self.q[ename]

                def make(ops=ops):
                    def body(eng):
                        for (waits, fn, incs) in ops:
                            for (s, v) in waits:
                                eng.wait_ge(s.h, v)
                            instr = fn(eng) if fn is not None else None
                            for (s, n) in incs:
                                if instr is None:
                                    instr = eng.nop()
                                instr.then_inc(s.h, n)
                    return body

                getattr(block, ename)(make())


def cdiv(a, b):
    return (a + b - 1) // b



def host_consts():
    c = {}
    c["c_id128"] = np.eye(128, dtype=BF)
    c["c_id128_f"] = np.eye(128, dtype=np.float32)
    c["c_ones_row_f"] = np.ones((1, 128), np.float32)
    c["c_ones_row_b"] = np.ones((1, 128), BF)
    c["c_ones_col_b"] = np.ones((128, 1), BF)
    c["c_ones_col_f"] = np.ones((128, 1), np.float32)
    c["c_ones_f128"] = np.ones((128, 128), np.float32)
    c["c_slmask"] = np.tril(np.ones((128, 128), np.float32), -1)
    c["c_slmask2"] = np.tile(np.tril(np.ones((128, 128), np.float32), -1), (1, 2))
    c["c_negmask2"] = np.tile(np.tril(np.ones((128, 128), np.float32), -1) * -1e6,
                              (1, 2))
    c["c_negmask"] = (np.tril(np.ones((128, 128), np.float32), -1) * -1e6)
    ps = np.zeros((128, 2), BF)
    ps[:64, 0] = 1
    ps[64:, 1] = 1
    c["c_pairsum"] = ps
    pe = np.zeros((2, 128), BF)
    pe[0, :64] = 1
    pe[1, 64:] = 1
    c["c_pairexp"] = pe
    c["c_pairexp_f"] = pe.astype(np.float32)
    sel = np.zeros((8, 8 * 128), BF)
    for e in range(8):
        sel[e, e * 128:(e + 1) * 128] = 1
    c["c_sel"] = sel
    c["c_fconst"] = np.tile(np.array([[LN_EPS, EPS, 1.0, 0.5, -0.5, -1.0, 2.0,
                                        0.0]], np.float32), (128, 1))
    return c


CONST_NAMES = ("c_fconst", "c_slmask2", "c_negmask2",
               "c_id128", "c_id128_f", "c_ones_row_f", "c_ones_row_b", "c_ones_col_b", "c_ones_col_f", "c_ones_f128",
               "c_slmask", "c_negmask", "c_pairsum", "c_pairexp",
               "c_pairexp_f", "c_sel")


class Consts:

    def __init__(self, p, ctx, io, s_ld):
        nc = p.nc
        self.id128 = ctx.enter_context(nc.sbuf_tensor("k_id128", [128, 128], BF16))
        self.id128_f = ctx.enter_context(nc.sbuf_tensor("k_id128f", [128, 128], F32))
        self.ones_row_f = ctx.enter_context(nc.sbuf_tensor("k_or_f", [1, 128], F32))
        self.ones_row_b = ctx.enter_context(nc.sbuf_tensor("k_or_b", [1, 128], BF16))
        self.ones_col_b = ctx.enter_context(nc.sbuf_tensor("k_oc_b", [128, 1], BF16))
        self.ones_col_f = ctx.enter_context(nc.sbuf_tensor("k_oc_f", [128, 1], F32))
        self.ones_f128 = ctx.enter_context(nc.sbuf_tensor("k_of128", [128, 128], F32))
        self.slmask = ctx.enter_context(nc.sbuf_tensor("k_sl", [128, 128], F32))
        self.slmask2 = ctx.enter_context(nc.sbuf_tensor("k_sl2", [128, 256], F32))
        self.negmask2 = ctx.enter_context(nc.sbuf_tensor("k_neg2", [128, 256], F32))
        self.negmask = ctx.enter_context(nc.sbuf_tensor("k_neg", [128, 128], F32))
        self.pairsum = ctx.enter_context(nc.sbuf_tensor("k_psum", [128, 2], BF16))
        self.pairexp = ctx.enter_context(nc.sbuf_tensor("k_pexp", [2, 128], BF16))
        self.pairexp_f = ctx.enter_context(nc.sbuf_tensor("k_pexpf", [2, 128], F32))
        self.sel = ctx.enter_context(nc.sbuf_tensor("k_sel", [8, 8 * 128], BF16))
        self.fconst = ctx.enter_context(nc.sbuf_tensor("k_fc", [128, 8], F32))
        for i, v in enumerate((LN_EPS, EPS, 1.0, 0.5, -0.5, -1.0, 2.0)):
            nc.const_aps.register(F32, v, self.fconst[:, i:i + 1])
        for sb, key in ((self.fconst, "c_fconst"),
                        (self.id128, "c_id128"), (self.id128_f, "c_id128_f"),
                        (self.ones_row_f, "c_ones_row_f"),
                        (self.ones_row_b, "c_ones_row_b"), (self.ones_col_b, "c_ones_col_b"),
                        (self.ones_col_f, "c_ones_col_f"), (self.ones_f128, "c_ones_f128"),
                        (self.slmask, "c_slmask"), (self.negmask, "c_negmask"),
                        (self.slmask2, "c_slmask2"), (self.negmask2, "c_negmask2"),
                        (self.pairsum, "c_pairsum"), (self.pairexp, "c_pairexp"),
                        (self.pairexp_f, "c_pairexp_f"), (self.sel, "c_sel")):
            p.dma("sync", sb[:, :], io[key][:, :], inc=[(s_ld, 16)])



def emit_stage_a(p, ctx, C, cfg, io):
    from contextlib import ExitStack
    pctx = ExitStack()
    nc = p.nc
    Tl, TB, TILE = cfg["Tl"], cfg["TB"], cfg["TILE"]
    ntiles = Tl // TILE
    tiles_per_batch = TB // TILE

    s_ld = p.sem("a_ld")
    s_mm = p.sem("a_mm")
    s_dve = p.sem("a_dve")
    s_act = p.sem("a_act")
    s_st = p.sem("a_st")
    s_wld = p.sem("a_wld")
    s_wcv = p.sem("a_wcv")

    wstg = Rot([ctx.enter_context(nc.sbuf_tensor(f"a_wstg{i}", [128, 1024], F32))
                for i in range(2)])
    w_q = ctx.enter_context(nc.sbuf_tensor("w_q", [128, 8 * D], BF16))
    w_k = ctx.enter_context(nc.sbuf_tensor("w_k", [128, 8 * D], BF16))
    w_v = ctx.enter_context(nc.sbuf_tensor("w_v", [128, 8 * VD], BF16))
    w_g = ctx.enter_context(nc.sbuf_tensor("w_g", [128, 8 * VD], BF16))
    w_b = ctx.enter_context(nc.sbuf_tensor("w_b", [128, 8 * NH], BF16))
    w_a = ctx.enter_context(nc.sbuf_tensor("w_a", [128, 8 * NH], BF16))

    wld_sems = [p.sem("a_wld0"), p.sem("a_wld1")]

    def load_w(dram, sb, kdim, mdim):
        for kc in range(cdiv(kdim, 128)):
            rows = min(128, kdim - kc * 128)
            for m0 in range(0, mdim, 1024):
                mw = min(1024, mdim - m0)
                sl = wstg.next()
                ws = wld_sems[(wstg.i - 1) % 2]
                p.dma("sync", sl.ap[:rows, :mw],
                      dram[kc * 128:kc * 128 + rows, m0:m0 + mw],
                      wait=sl.take(), inc=[(ws, 16)])
                p.do("gpsimd", lambda e, sl=sl, kc=kc, rows=rows, sb=sb,
                     mdim=mdim, m0=m0, mw=mw:
                     e.tensor_copy(sb[:rows, kc * mdim + m0:kc * mdim + m0 + mw],
                                   sl.ap[:rows, :mw]),
                     wait=[ws], inc=[s_wcv])
                sl.mark(s_wcv)

    load_w(io["L0_attn_a_q_w"], w_q, D, D)
    load_w(io["L0_attn_a_k_w"], w_k, D, D)
    load_w(io["L0_attn_a_v_w"], w_v, D, VD)
    load_w(io["L0_attn_a_g_w"], w_g, D, VD)
    load_w(io["L0_attn_a_b_w"], w_b, D, NH)
    load_w(io["L0_attn_a_a_w"], w_a, D, NH)

    normw = ctx.enter_context(nc.sbuf_tensor("a_normw", [128, 8], F32))
    convw = ctx.enter_context(nc.sbuf_tensor("a_convw", [128, 128], F32))
    alog = ctx.enter_context(nc.sbuf_tensor("a_alog", [NH, 2], F32))
    nea = ctx.enter_context(nc.sbuf_tensor("a_nea", [NH, 1], F32))
    p.dma("sync", normw[:, :], io["pre_normw"][:, :], inc=[(s_ld, 16)])
    p.dma("sync", convw[:, :], io["pre_convw"][:, :], inc=[(s_ld, 16)])
    p.dma("sync", alog[:, 0:1], io["L0_attn_a_A_log"].rearrange("(h o) -> h o", o=1),
          inc=[(s_ld, 16)])
    p.dma("sync", alog[:, 1:2], io["L0_attn_a_dt_bias"].rearrange("(h o) -> h o", o=1),
          inc=[(s_ld, 16)])
    p.do("scalar", lambda e: e.activation(nea[:, :], alog[:, 0:1], AF.Exp),
         wait=[s_ld], inc=[s_act])
    p.do("vector", lambda e: e.tensor_scalar_mul(nea[:, :], nea[:, :], -1.0),
         wait=[s_act], inc=[s_dve])

    p.barrier("a_w")

    S_xt = Slot(ctx.enter_context(nc.sbuf_tensor("a_xt", [128, 8 * TILE], F32)))
    R_sq = Rot([ctx.enter_context(nc.sbuf_tensor(f"a_sq{i}", [128, TILE], BF16))
                for i in range(8)])
    S_hb = Slot(ctx.enter_context(nc.sbuf_tensor("a_hb", [128, 8 * TILE], BF16)))
    stat = ctx.enter_context(nc.sbuf_tensor("a_stat", [1, 5 * TILE], F32))
    qpre = [Slot(ctx.enter_context(nc.sbuf_tensor(f"a_qpre{m}", [128, TILE + 4], BF16)))
            for m in range(8)]
    kpre = [Slot(ctx.enter_context(nc.sbuf_tensor(f"a_kpre{m}", [128, TILE + 4], BF16)))
            for m in range(8)]
    vpre = [Slot(ctx.enter_context(nc.sbuf_tensor(f"a_vpre{m}", [128, TILE + 4], BF16)))
            for m in range(16)]
    R_cvt = Rot([ctx.enter_context(nc.sbuf_tensor(f"a_cvt{i}", [128, TILE], F32))
                 for i in range(2)])
    R_cvs = Rot([ctx.enter_context(nc.sbuf_tensor(f"a_cvs{i}", [128, TILE], BF16))
                 for i in range(2)])
    S_small = Slot(ctx.enter_context(nc.sbuf_tensor("a_small", [16, 4 * TILE], F32)))
    rst2 = Slot(ctx.enter_context(nc.sbuf_tensor("a_rst2", [2, TILE], F32)))

    P_st1 = Slot(pctx.enter_context(nc.psum_tensor("a_ps1", [1, 512], F32)))
    P_st2 = Slot(pctx.enter_context(nc.psum_tensor("a_ps2", [1, 512], F32)))
    P_bc = Slot(pctx.enter_context(nc.psum_tensor("a_pb", [128, 512], F32)))
    R_pm = Rot([pctx.enter_context(nc.psum_tensor(f"a_pm{i}", [128, 512], F32))
                for i in range(2)])
    P_sm = Slot(pctx.enter_context(nc.psum_tensor("a_psm", [16, 512], F32)))
    P_l2 = Slot(pctx.enter_context(nc.psum_tensor("a_pl2", [2, 512], F32)))
    P_l2b = Slot(pctx.enter_context(nc.psum_tensor("a_pl2b", [128, 512], F32)))

    ST = TILE
    for t in range(ntiles):
        tok0 = t * TILE
        first_in_batch = (t % tiles_per_batch) == 0

        xw = S_xt.take()
        for c in range(8):
            p.dma("sync", S_xt.ap[:, c * TILE:(c + 1) * TILE],
                  io["x_fm"][c * 128:(c + 1) * 128, tok0:tok0 + TILE],
                  wait=(xw if c == 0 else []), inc=[(s_ld, 16)])

        for c in range(8):
            p.do("tensor", lambda e, c=c: e.matmul(
                P_st1.ap[:, :TILE], C.ones_col_f[:, :],
                S_xt.ap[:, c * TILE:(c + 1) * TILE],
                start=(c == 0), stop=(c == 7)),
                wait=[s_ld] + (P_st1.take() if c == 0 else []),
                inc=([s_mm] if c == 7 else []))
        S_xt.mark(s_mm)
        for c in range(8):
            sq = R_sq.next()
            p.do("scalar", lambda e, c=c, sq=sq: e.activation(
                sq.ap[:, :], S_xt.ap[:, c * TILE:(c + 1) * TILE], AF.Square),
                wait=[s_ld] + sq.take(), inc=[s_act])
            p.do("tensor", lambda e, c=c, sq=sq: e.matmul(
                P_st2.ap[:, :TILE], C.ones_col_b[:, :], sq.ap[:, :],
                start=(c == 0), stop=(c == 7)),
                wait=[s_act] + (P_st2.take() if c == 0 else []),
                inc=([s_mm] if c == 7 else []))
        R_sq.mark_all(s_mm)
        S_xt.mark(s_act)
        p.do("vector", lambda e: e.tensor_scalar_mul(
            stat[:, 0 * ST:0 * ST + TILE], P_st1.ap[:, :TILE], 1.0 / D),
            wait=[s_mm], inc=[s_dve])
        P_st1.mark(s_dve)
        p.do("scalar", lambda e: e.activation(
            stat[:, 1 * ST:1 * ST + TILE], stat[:, 0 * ST:0 * ST + TILE],
            AF.Square), wait=[s_dve], inc=[s_act])
        p.do("vector", lambda e: e.scalar_tensor_tensor(
            stat[:, 4 * ST:4 * ST + TILE], P_st2.ap[:, :TILE], 1.0 / D,
            stat[:, 1 * ST:1 * ST + TILE], ALU.mult, ALU.subtract),
            wait=[s_mm, s_act], inc=[s_dve])
        P_st2.mark(s_dve)
        p.do("scalar", lambda e: e.activation(
            stat[:, 2 * ST:2 * ST + TILE], stat[:, 4 * ST:4 * ST + TILE],
            AF.Ln, bias=LN_EPS), wait=[s_dve], inc=[s_act])
        p.do("scalar", lambda e: e.activation(
            stat[:, 2 * ST:2 * ST + TILE], stat[:, 2 * ST:2 * ST + TILE],
            AF.Exp, scale=-0.5), wait=[s_act], inc=[s_act])
        p.do("tensor", lambda e: e.matmul(
            P_bc.ap[:, :TILE], C.ones_row_f[:, :], stat[:, 0 * ST:0 * ST + TILE],
            start=True, stop=True), wait=[(s_dve, s_dve.v)] + P_bc.take(),
            inc=[s_mm])
        for c in range(8):
            p.do("vector", lambda e, c=c: e.tensor_sub(
                S_xt.ap[:, c * TILE:(c + 1) * TILE],
                S_xt.ap[:, c * TILE:(c + 1) * TILE], P_bc.ap[:, :TILE]),
                wait=[s_mm], inc=[s_dve])
        P_bc.mark(s_dve)
        p.do("tensor", lambda e: e.matmul(
            P_bc.ap[:, :TILE], C.ones_row_f[:, :], stat[:, 2 * ST:2 * ST + TILE],
            start=True, stop=True), wait=[s_act] + P_bc.take(), inc=[s_mm])
        for c in range(8):
            p.do("vector", lambda e, c=c: e.tensor_mul(
                S_xt.ap[:, c * TILE:(c + 1) * TILE],
                S_xt.ap[:, c * TILE:(c + 1) * TILE], P_bc.ap[:, :TILE]),
                wait=[s_mm], inc=[s_dve])
        P_bc.mark(s_dve)
        for c in range(8):
            p.dma("gpsimd", io["ln0_s"][c * 128:(c + 1) * 128, tok0:tok0 + TILE],
                  S_xt.ap[:, c * TILE:(c + 1) * TILE],
                  wait=[(s_dve, s_dve.v)], inc=[(S_xt.st(p), 16)])
        S_xt.mark(S_xt.st(p))

        for c in range(8):
            sq = R_sq.next()
            p.do("scalar", lambda e, c=c, sq=sq: e.activation(
                sq.ap[:, :], S_xt.ap[:, c * TILE:(c + 1) * TILE], AF.Square),
                wait=[(s_dve, s_dve.v)] + sq.take(), inc=[s_act])
            p.do("tensor", lambda e, c=c, sq=sq: e.matmul(
                P_st1.ap[:, :TILE], C.ones_col_b[:, :], sq.ap[:, :],
                start=(c == 0), stop=(c == 7)),
                wait=[s_act] + (P_st1.take() if c == 0 else []),
                inc=([s_mm] if c == 7 else []))
        R_sq.mark_all(s_mm)
        p.do("scalar", lambda e: e.activation(
            stat[:, 3 * ST:3 * ST + TILE], P_st1.ap[:, :TILE], AF.Ln,
            scale=1.0 / D, bias=EPS), wait=[s_mm], inc=[s_act])
        P_st1.mark(s_act)
        p.do("scalar", lambda e: e.activation(
            stat[:, 3 * ST:3 * ST + TILE], stat[:, 3 * ST:3 * ST + TILE],
            AF.Exp, scale=-0.5), wait=[s_act], inc=[s_act])
        p.do("tensor", lambda e: e.matmul(
            P_bc.ap[:, :TILE], C.ones_row_f[:, :], stat[:, 3 * ST:3 * ST + TILE],
            start=True, stop=True), wait=[s_act] + P_bc.take(), inc=[s_mm])
        hbw = S_hb.take()
        for c in range(8):
            p.do("vector", lambda e, c=c: e.scalar_tensor_tensor(
                S_hb.ap[:, c * TILE:(c + 1) * TILE],
                S_xt.ap[:, c * TILE:(c + 1) * TILE],
                normw[:, c:c + 1], P_bc.ap[:, :TILE], ALU.mult, ALU.mult),
                wait=[s_mm] + (hbw if c == 0 else []), inc=[s_dve])
        P_bc.mark(s_dve)
        S_xt.mark(s_dve)
        hb_done = s_dve.v

        def proj_group(w_sb, mdim, mc, pm):
            for kc in range(8):
                p.do("tensor", lambda e, kc=kc, mc=mc, pm=pm, w_sb=w_sb, mdim=mdim:
                     e.matmul(pm.ap[:, :TILE],
                              w_sb[:, kc * mdim + mc * 128: kc * mdim + mc * 128 + 128],
                              S_hb.ap[:, kc * TILE:(kc + 1) * TILE],
                              start=(kc == 0), stop=(kc == 7)),
                     wait=[(s_dve, hb_done)] + (pm.take() if kc == 0 else []),
                     inc=([s_mm] if kc == 7 else []))

        for (w_sb, pre, nmc, mdim) in ((w_q, qpre, 8, D), (w_k, kpre, 8, D),
                                       (w_v, vpre, 16, VD)):
            for mc in range(nmc):
                pm = R_pm.next()
                proj_group(w_sb, mdim, mc, pm)
                pr = pre[mc]
                p.do("scalar", lambda e, pm=pm, pr=pr: e.activation(
                    pr.ap[:, 4:4 + TILE], pm.ap[:, :TILE], AF.Copy),
                    wait=[s_mm] + pr.take(), inc=[s_act])
                pm.mark(s_act)
                pr.mark(s_act)

        for kc in range(8):
            p.do("tensor", lambda e, kc=kc: e.matmul(
                P_sm.ap[:, :TILE], w_b[:, kc * NH:(kc + 1) * NH],
                S_hb.ap[:, kc * TILE:(kc + 1) * TILE],
                start=(kc == 0), stop=(kc == 7)),
                wait=[(s_dve, hb_done)] + (P_sm.take() if kc == 0 else []),
                inc=([s_mm] if kc == 7 else []))
        smw = S_small.take()
        p.do("scalar", lambda e: e.activation(
            S_small.ap[:, 0:TILE], P_sm.ap[:, :TILE], AF.Sigmoid),
            wait=[s_mm] + smw, inc=[s_act])
        P_sm.mark(s_act)
        p.dma("gpsimd", io["beta_s"][:, tok0:tok0 + TILE], S_small.ap[:, 0:TILE],
              wait=[s_act], inc=[(S_small.st(p), 16)])
        for kc in range(8):
            p.do("tensor", lambda e, kc=kc: e.matmul(
                P_sm.ap[:, :TILE], w_a[:, kc * NH:(kc + 1) * NH],
                S_hb.ap[:, kc * TILE:(kc + 1) * TILE],
                start=(kc == 0), stop=(kc == 7)),
                wait=[(s_dve, hb_done)] + (P_sm.take() if kc == 0 else []),
                inc=([s_mm] if kc == 7 else []))
        p.do("scalar", lambda e: e.activation(
            S_small.ap[:, ST:ST + TILE], P_sm.ap[:, :TILE], AF.Relu,
            bias=alog[:, 1:2]), wait=[s_mm], inc=[s_act])
        p.do("scalar", lambda e: e.activation(
            S_small.ap[:, 2 * ST:2 * ST + TILE], P_sm.ap[:, :TILE], AF.Abs,
            bias=alog[:, 1:2]), wait=[s_mm], inc=[s_act])
        P_sm.mark(s_act)
        p.do("scalar", lambda e: e.activation(
            S_small.ap[:, 2 * ST:2 * ST + TILE],
            S_small.ap[:, 2 * ST:2 * ST + TILE], AF.Exp, scale=-1.0),
            wait=[s_act], inc=[s_act])
        p.do("scalar", lambda e: e.activation(
            S_small.ap[:, 2 * ST:2 * ST + TILE],
            S_small.ap[:, 2 * ST:2 * ST + TILE], AF.Ln, bias=1.0),
            wait=[s_act], inc=[s_act])
        p.do("vector", lambda e: e.scalar_tensor_tensor(
            S_small.ap[:, 3 * ST:3 * ST + TILE], S_small.ap[:, ST:ST + TILE],
            1.0, S_small.ap[:, 2 * ST:2 * ST + TILE], ALU.mult, ALU.add),
            wait=[s_act], inc=[s_dve])
        p.do("vector", lambda e: e.tensor_scalar(
            S_small.ap[:, 3 * ST:3 * ST + TILE],
            S_small.ap[:, 3 * ST:3 * ST + TILE], nea[:, 0:1], None, ALU.mult),
            wait=[s_dve], inc=[s_dve])
        p.dma("gpsimd", io["g_s"][:, tok0:tok0 + TILE],
              S_small.ap[:, 3 * ST:3 * ST + TILE],
              wait=[(s_dve, s_dve.v)], inc=[(S_small.st(p), 16)])
        S_small.mark(S_small.st(p))

        for mc in range(16):
            pm = R_pm.next()
            for kc in range(8):
                p.do("tensor", lambda e, kc=kc, mc=mc, pm=pm: e.matmul(
                    pm.ap[:, :TILE],
                    w_g[:, kc * VD + mc * 128: kc * VD + mc * 128 + 128],
                    S_hb.ap[:, kc * TILE:(kc + 1) * TILE],
                    start=(kc == 0), stop=(kc == 7)),
                    wait=[(s_dve, hb_done)] + (pm.take() if kc == 0 else []),
                    inc=([s_mm] if kc == 7 else []))
            cv = R_cvs.next()
            p.do("scalar", lambda e, pm=pm, cv=cv: e.activation(
                cv.ap[:, :], pm.ap[:, :TILE], AF.Silu),
                wait=[s_mm] + cv.take(), inc=[s_act])
            pm.mark(s_act)
            p.dma("gpsimd", io["gsil_s"][mc * 128:(mc + 1) * 128, tok0:tok0 + TILE],
                  cv.ap[:, :], wait=[s_act], inc=[(cv.st(p), 16)])
            cv.mark(cv.st(p))
        S_hb.mark(s_mm)

        def conv_one(pre, nmc, wofs, out_dram, l2, tag):
            for mc in range(nmc):
                pr = pre[mc]
                if first_in_batch:
                    p.do("vector", lambda e, pr=pr: e.memset(pr.ap[:, 0:4], 0.0),
                         wait=[(s_act, s_act.v)], inc=[s_dve])
                cvt = R_cvt.next()
                p.do("vector", lambda e, pr=pr, mc=mc, cvt=cvt: e.tensor_scalar(
                    cvt.ap[:, :], pr.ap[:, 1:1 + TILE],
                    convw[:, wofs + mc * 4: wofs + mc * 4 + 1], None, ALU.mult),
                    wait=[(s_act, s_act.v), (s_dve, s_dve.v)] + cvt.take(),
                    inc=[s_dve])
                for i in (1, 2, 3):
                    p.do("vector", lambda e, pr=pr, mc=mc, i=i, cvt=cvt:
                         e.scalar_tensor_tensor(
                             cvt.ap[:, :], pr.ap[:, 1 + i:1 + i + TILE],
                             convw[:, wofs + mc * 4 + i: wofs + mc * 4 + i + 1],
                             cvt.ap[:, :], ALU.mult, ALU.add),
                         wait=[s_dve], inc=[s_dve])
                p.do("vector", lambda e, pr=pr: e.tensor_copy(
                    pr.ap[:, 0:4], pr.ap[:, TILE:TILE + 4]),
                    wait=[s_dve], inc=[s_dve])
                pr.marks = [(s_dve, s_dve.v)]
                if not l2:
                    cv = R_cvs.next()
                    p.do("scalar", lambda e, cvt=cvt, cv=cv: e.activation(
                        cv.ap[:, :], cvt.ap[:, :], AF.Silu),
                        wait=[(s_dve, s_dve.v)] + cv.take(), inc=[s_act])
                    cvt.mark(s_act)
                    p.dma("gpsimd",
                          out_dram[mc * 128:(mc + 1) * 128, tok0:tok0 + TILE],
                          cv.ap[:, :], wait=[s_act], inc=[(cv.st(p), 16)])
                    cv.mark(cv.st(p))
                else:
                    p.do("scalar", lambda e, cvt=cvt: e.activation(
                        cvt.ap[:, :], cvt.ap[:, :], AF.Silu),
                        wait=[(s_dve, s_dve.v)], inc=[s_act])
                    sq = R_sq.next()
                    p.do("scalar", lambda e, cvt=cvt, sq=sq: e.activation(
                        sq.ap[:, :], cvt.ap[:, :], AF.Square),
                        wait=[s_act] + sq.take(), inc=[s_act])
                    p.do("tensor", lambda e, sq=sq: e.matmul(
                        P_l2.ap[:, :TILE], C.pairsum[:, :], sq.ap[:, :],
                        start=True, stop=True),
                        wait=[s_act] + P_l2.take(), inc=[s_mm])
                    sq.mark(s_mm)
                    rw = rst2.take()
                    p.do("scalar", lambda e: e.activation(
                        rst2.ap[:, :], P_l2.ap[:, :TILE], AF.Ln, bias=EPS),
                        wait=[s_mm] + rw, inc=[s_act])
                    P_l2.mark(s_act)
                    p.do("scalar", lambda e: e.activation(
                        rst2.ap[:, :], rst2.ap[:, :], AF.Exp, scale=-0.5),
                        wait=[s_act], inc=[s_act])
                    p.do("tensor", lambda e: e.matmul(
                        P_l2b.ap[:, :TILE], C.pairexp_f[:, :], rst2.ap[:, :],
                        start=True, stop=True),
                        wait=[s_act] + P_l2b.take(), inc=[s_mm])
                    rst2.mark(s_mm)
                    cv = R_cvs.next()
                    scale = QSCALE if tag == "q" else 1.0
                    p.do("vector", lambda e, cvt=cvt, cv=cv, scale=scale:
                         e.scalar_tensor_tensor(
                             cv.ap[:, :], cvt.ap[:, :], scale,
                             P_l2b.ap[:, :TILE], ALU.mult, ALU.mult),
                         wait=[s_mm] + cv.take(), inc=[s_dve])
                    cvt.mark(s_dve)
                    P_l2b.mark(s_dve)
                    p.dma("gpsimd",
                          out_dram[mc * 128:(mc + 1) * 128, tok0:tok0 + TILE],
                          cv.ap[:, :], wait=[(s_dve, s_dve.v)],
                          inc=[(cv.st(p), 16)])
                    cv.mark(cv.st(p))

        conv_one(qpre, 8, 0, io["q_s"], True, "q")
        conv_one(kpre, 8, 32, io["k_s"], True, "k")
        conv_one(vpre, 16, 64, io["v_s"], False, "v")

    p.barrier("a_end")
    pctx.close()



def emit_stage_b(p, ctx, C, cfg, io):
    from contextlib import ExitStack
    pctx = ExitStack()
    nc = p.nc
    Tl, TB = cfg["Tl"], cfg["TB"]
    NB = Tl // TB
    NCH = TB // CHUNK

    s_ld = p.sem("b_ld")
    s_mm = p.sem("b_mm")
    s_dve = p.sem("b_dve")
    s_act = p.sem("b_act")
    s_st = p.sem("b_st")

    onw_row = ctx.enter_context(nc.sbuf_tensor("b_onwr", [1, 128], F32))
    onwB = ctx.enter_context(nc.sbuf_tensor("b_onwB", [128, 128], F32))
    zrow = ctx.enter_context(nc.sbuf_tensor("b_zrow", [1, 128], F32))
    P_tmp = Slot(pctx.enter_context(nc.psum_tensor("b_ptmp", [128, 512], F32)))
    s_ld0 = p.sem("b_ld0")
    p.dma("sync", onw_row[:, :], io["L0_attn_a_o_norm_w"].rearrange("(o d) -> o d", o=1),
          inc=[(s_ld0, 16)])
    p.do("vector", lambda e: e.memset(zrow[:, :], 0.0), inc=[s_dve])
    p.do("tensor", lambda e: e.matmul(
        P_tmp.ap[:, 0:128], C.ones_row_f[:, :], onw_row[:, :],
        start=True, stop=True), wait=[s_ld0], inc=[s_mm])
    p.do("scalar", lambda e: e.activation(onwB[:, :], P_tmp.ap[:, 0:128], AF.Copy),
         wait=[s_mm], inc=[s_act])
    P_tmp.mark(s_act)

    S_q = Slot(ctx.enter_context(nc.sbuf_tensor("b_q", [64, TB], BF16)))
    S_k = Slot(ctx.enter_context(nc.sbuf_tensor("b_k", [64, TB], BF16)))
    S_v = Slot(ctx.enter_context(nc.sbuf_tensor("b_v", [128, TB], BF16)))
    S_gs = Slot(ctx.enter_context(nc.sbuf_tensor("b_gs", [128, TB], BF16)))
    S_gr = Slot(ctx.enter_context(nc.sbuf_tensor("b_grow", [1, TB], F32)))
    S_br = Slot(ctx.enter_context(nc.sbuf_tensor("b_brow", [1, TB], F32)))

    Sst = [ctx.enter_context(nc.sbuf_tensor(f"b_S{i}", [64, 128], F32))
           for i in range(2)]
    S0b = ctx.enter_context(nc.sbuf_tensor("b_S0b", [64, 128], BF16))

    gc_row = ctx.enter_context(nc.sbuf_tensor("b_gc", [1, 128], F32))
    tcol = ctx.enter_context(nc.sbuf_tensor("b_tcol", [128, 2], F32))
    grb = ctx.enter_context(nc.sbuf_tensor("b_grb", [128, 128], F32))
    f0 = ctx.enter_context(nc.sbuf_tensor("b_f0", [128, 128], F32))
    fm_ = ctx.enter_context(nc.sbuf_tensor("b_fm", [128, 128], F32))
    d1 = ctx.enter_context(nc.sbuf_tensor("b_d1", [128, 128], F32))
    g2 = ctx.enter_context(nc.sbuf_tensor("b_g2", [128, 128], F32))
    d2 = ctx.enter_context(nc.sbuf_tensor("b_d2", [128, 128], F32))
    m1a = ctx.enter_context(nc.sbuf_tensor("b_m1a", [128, 128], F32))
    Mb = ctx.enter_context(nc.sbuf_tensor("b_M", [128, 128], BF16))
    MTb = ctx.enter_context(nc.sbuf_tensor("b_MT", [128, 128], BF16))
    Acur = [ctx.enter_context(nc.sbuf_tensor(f"b_A{i}", [128, 128], BF16))
            for i in range(2)]
    Bfac = [ctx.enter_context(nc.sbuf_tensor(f"b_B{j}", [128, 128], BF16))
            for j in range(7)]
    Yb = [ctx.enter_context(nc.sbuf_tensor(f"b_Y{i}", [128, 128], BF16))
          for i in range(2)]
    Kb = ctx.enter_context(nc.sbuf_tensor("b_Kb", [64, 128], BF16))
    Qb = ctx.enter_context(nc.sbuf_tensor("b_Qb", [64, 128], BF16))
    Kd = ctx.enter_context(nc.sbuf_tensor("b_Kd", [64, 128], BF16))
    Kdt = ctx.enter_context(nc.sbuf_tensor("b_Kdt", [128, 64], BF16))
    Vsb = ctx.enter_context(nc.sbuf_tensor("b_Vsb", [128, 128], BF16))
    w2b = ctx.enter_context(nc.sbuf_tensor("b_w2", [128, 128], BF16))
    r1 = ctx.enter_context(nc.sbuf_tensor("b_r1", [128, 128], F32))
    sqo = ctx.enter_context(nc.sbuf_tensor("b_sqo", [128, 128], F32))
    ms = ctx.enter_context(nc.sbuf_tensor("b_ms", [128, 1], F32))
    o1 = ctx.enter_context(nc.sbuf_tensor("b_o1", [128, 128], F32))
    o2 = ctx.enter_context(nc.sbuf_tensor("b_o2", [128, 128], BF16))
    S_oo = Slot(ctx.enter_context(nc.sbuf_tensor("b_oout", [128, 128], BF16)))
    drow = ctx.enter_context(nc.sbuf_tensor("b_drow", [1, 128], F32))
    brf = ctx.enter_context(nc.sbuf_tensor("b_brf", [1, 128], F32))
    bc1 = ctx.enter_context(nc.sbuf_tensor("b_bc1", [1, 1], F32))
    bc64 = ctx.enter_context(nc.sbuf_tensor("b_bc64", [64, 1], F32))

    pA = Slot(pctx.enter_context(nc.psum_tensor("b_pA", [128, 512], F32)))
    pG = Slot(pctx.enter_context(nc.psum_tensor("b_pG", [128, 512], F32)))
    pH = Slot(pctx.enter_context(nc.psum_tensor("b_pH", [128, 512], F32)))
    pC = Slot(pctx.enter_context(nc.psum_tensor("b_pC", [128, 512], F32)))
    pD = Slot(pctx.enter_context(nc.psum_tensor("b_pD", [128, 512], F32)))
    pE = Slot(pctx.enter_context(nc.psum_tensor("b_pE", [128, 512], F32)))
    pF = Slot(pctx.enter_context(nc.psum_tensor("b_pF", [128, 1024], BF16)))

    def A(fn, wait=(), inc=()):
        p.do("scalar", fn, wait=wait, inc=inc)

    for b in range(NB):
        for h in range(NH):
            for (sl_, dram, r0, nr) in (
                    (S_q, io["q_s"], 64 * h, 64), (S_k, io["k_s"], 64 * h, 64),
                    (S_v, io["v_s"], 128 * h, 128), (S_gs, io["gsil_s"], 128 * h, 128)):
                p.dma("sync", sl_.ap[:, :], dram[r0:r0 + nr, b * TB:(b + 1) * TB],
                      wait=sl_.take(), inc=[(s_ld, 16)])
            p.dma("sync", S_gr.ap[:, :], io["g_s"][h:h + 1, b * TB:(b + 1) * TB],
                  wait=S_gr.take(), inc=[(s_ld, 16)])
            p.dma("sync", S_br.ap[:, :], io["beta_s"][h:h + 1, b * TB:(b + 1) * TB],
                  wait=S_br.take(), inc=[(s_ld, 16)])
            ld_done = s_ld.v

            p.do("vector", lambda e: e.memset(Sst[0][:, :], 0.0),
                 wait=[(s_mm, s_mm.v), (s_act, s_act.v), (s_dve, s_dve.v)],
                 inc=[s_dve])

            for cc in range(NCH):
                BCUT = cfg.get("bcut", 99)
                sl = slice(cc * CHUNK, (cc + 1) * CHUNK)
                cur, nxt = Sst[cc % 2], Sst[(cc + 1) % 2]

                if BCUT < 1:
                    continue
                import os as _os
                BVAR = _os.environ.get("BVAR", "")
                if BVAR == "noscan":
                    p.do("vector", lambda e, sl=sl: e.tensor_copy(
                        gc_row[:, :], S_gr.ap[:, sl]),
                        wait=[(s_ld, ld_done), (s_act, s_act.v)], inc=[s_dve])
                else:
                    p.do("vector", lambda e, sl=sl: e.tensor_tensor_scan(
                        gc_row[:, :], S_gr.ap[:, sl], zrow[:, :], 0.0,
                        ALU.add, ALU.add), wait=[(s_ld, ld_done), (s_act, s_act.v)],
                        inc=[s_dve])
                p.do("tensor", lambda e: e.matmul(
                    pG.ap[:, 0:128], gc_row[:, :], C.ones_row_f[0:1, :],
                    start=True, stop=True),
                    wait=[s_dve] + pG.take(), inc=[s_mm])
                p.do("tensor", lambda e, sl=sl: e.matmul(
                    pG.ap[:, 128:256], S_br.ap[:, sl], C.ones_row_f[0:1, :],
                    start=True, stop=True),
                    wait=[(s_ld, ld_done)], inc=[s_mm])
                p.do("tensor", lambda e: e.matmul(
                    pG.ap[:, 256:384], C.ones_row_f[:, :], gc_row[:, :],
                    start=True, stop=True), inc=[s_mm])
                A(lambda e: e.activation(tcol[:, 0:1], pG.ap[:, 0:1], AF.Copy),
                  wait=[s_mm], inc=[s_act])
                A(lambda e: e.activation(tcol[:, 1:2], pG.ap[:, 128:129], AF.Copy),
                  wait=[(s_mm, s_mm.v)], inc=[s_act])
                A(lambda e: e.activation(grb[:, :], pG.ap[:, 256:384], AF.Copy),
                  wait=[(s_mm, s_mm.v)], inc=[s_act])
                pG.mark(s_act)
                if BCUT < 2:
                    continue
                p.do("vector", lambda e: e.tensor_scalar(
                    f0[:, :], grb[:, :], tcol[:, 0:1], None, ALU.subtract),
                    wait=[s_act], inc=[s_dve])
                p.do("vector", lambda e: e.tensor_scalar_max(fm_[:, :], f0[:, :], 0.0),
                     wait=[s_dve], inc=[s_dve])
                A(lambda e: e.activation(d1[:, :], fm_[:, :], AF.Exp, scale=-1.0),
                  wait=[s_dve], inc=[s_act])
                p.do("vector", lambda e: e.tensor_add(g2[:, :], f0[:, :],
                                                      C.negmask[:, :]),
                     wait=[(s_dve, s_dve.v)], inc=[s_dve])
                A(lambda e: e.activation(d2[:, :], g2[:, :], AF.Exp),
                  wait=[s_dve], inc=[s_act])
                if BCUT < 3:
                    continue
                p.do("tensor", lambda e, sl=sl: e.matmul(
                    pA.ap[:, 0:128], S_k.ap[:, sl], S_k.ap[:, sl],
                    start=True, stop=True),
                    wait=[(s_ld, ld_done)] + pA.take(), inc=[s_mm])
                p.do("tensor", lambda e, sl=sl: e.matmul(
                    pA.ap[:, 128:256], S_k.ap[:, sl], S_q.ap[:, sl],
                    start=True, stop=True), inc=[s_mm])
                p.do("vector", lambda e: e.tensor_mul(m1a[:, :], pA.ap[:, 0:128],
                                                      d1[:, :]),
                     wait=[s_mm, s_act], inc=[s_dve])
                p.do("vector", lambda e: e.scalar_tensor_tensor(
                    Mb[:, :], m1a[:, :], tcol[:, 1:2], C.slmask[:, :],
                    ALU.mult, ALU.mult), wait=[s_dve], inc=[s_dve])
                p.do("tensor", lambda e: e.transpose(
                    pF.ap[:, 0:128], Mb[:, :], C.id128[:, :]),
                    wait=[s_dve] + pF.take(), inc=[s_mm])
                p.do("vector", lambda e: e.tensor_copy(MTb[:, :], pF.ap[:, 0:128]),
                     wait=[s_mm], inc=[s_dve])
                pF.mark(s_dve)
                if BCUT < 4:
                    continue
                p.do("vector", lambda e: e.tensor_copy(Bfac[0][:, :], MTb[:, :]),
                     wait=[s_dve], inc=[s_dve])
                p.do("vector", lambda e: e.tensor_copy(Acur[0][:, :], Mb[:, :]),
                     wait=[(s_dve, s_dve.v)], inc=[s_dve])
                for j in range(1, 7):
                    ca, na = Acur[(j + 1) % 2], Acur[j % 2]
                    cb = Bfac[j - 1]
                    p.do("tensor", lambda e, ca=ca, cb=cb: e.matmul(
                        pC.ap[:, 0:128], cb[:, :], ca[:, :],
                        start=True, stop=True),
                        wait=[(s_dve, s_dve.v)] + pC.take(), inc=[s_mm])
                    p.do("tensor", lambda e, ca=ca, cb=cb: e.matmul(
                        pC.ap[:, 128:256], ca[:, :], cb[:, :],
                        start=True, stop=True), inc=[s_mm])
                    p.do("vector", lambda e, na=na: e.tensor_copy(
                        na[:, :], pC.ap[:, 0:128]), wait=[s_mm], inc=[s_dve])
                    p.do("vector", lambda e, j=j: e.tensor_copy(
                        Bfac[j][:, :], pC.ap[:, 128:256]), inc=[s_dve])
                    pC.mark(s_dve)
                if BCUT < 5:
                    continue
                A(lambda e: e.activation(brf[:, :], gc_row[:, :], AF.Exp),
                  wait=[(s_dve, s_dve.v)], inc=[s_act])
                p.do("tensor", lambda e: e.matmul(
                    pH.ap[0:64, 0:128], C.ones_row_f[:, 0:64], brf[:, :],
                    start=True, stop=True), wait=[s_act] + pH.take(), inc=[s_mm])
                p.do("vector", lambda e, sl=sl: e.tensor_mul(
                    Kb[:, :], S_k.ap[:, sl], pH.ap[0:64, 0:128]),
                    wait=[s_mm], inc=[s_dve])
                p.do("vector", lambda e, sl=sl: e.tensor_mul(
                    Qb[:, :], S_q.ap[:, sl], pH.ap[0:64, 0:128]),
                    wait=[(s_mm, s_mm.v)], inc=[s_dve])
                pH.mark(s_dve)
                if cc == 0:
                    A(lambda e, cur=cur: e.activation(S0b[:, :], cur[:, :], AF.Copy),
                      wait=[(s_dve, s_dve.v)], inc=[s_act])
                else:
                    A(lambda e, cur=cur: e.activation(S0b[:, :], cur[:, :], AF.Copy),
                      wait=[(s_dve, s_dve.v)], inc=[s_act])
                p.do("tensor", lambda e: e.matmul(
                    pE.ap[:, 0:128], Kb[:, :], S0b[:, :], start=True, stop=True),
                    wait=[s_dve, s_act] + pE.take(), inc=[s_mm])
                p.do("tensor", lambda e, sl=sl: e.transpose(
                    pF.ap[:, 192:320], S_v.ap[:, sl], C.id128[:, :]),
                    wait=pF.take(), inc=[s_mm])
                p.do("vector", lambda e: e.tensor_copy(Vsb[:, :], pF.ap[:, 192:320]),
                     wait=[s_mm], inc=[s_dve])
                pF.mark(s_dve)
                p.do("vector", lambda e: e.tensor_sub(r1[:, :], Vsb[:, :],
                                                      pE.ap[:, 0:128]),
                     wait=[(s_mm, s_mm.v), s_dve], inc=[s_dve])
                pE.mark(s_dve)
                p.do("vector", lambda e: e.tensor_scalar(
                    Yb[0][:, :], r1[:, :], tcol[:, 1:2], None, ALU.mult),
                    wait=[s_dve], inc=[s_dve])
                for ji, j in enumerate(range(6, -1, -1)):
                    yc, yn = Yb[ji % 2], Yb[(ji + 1) % 2]
                    p.do("tensor", lambda e, j=j, yc=yc: e.matmul(
                        pD.ap[:, 0:128], Bfac[j][:, :], yc[:, :],
                        start=True, stop=True),
                        wait=[(s_dve, s_dve.v)] + pD.take(), inc=[s_mm])
                    p.do("vector", lambda e, yc=yc, yn=yn: e.tensor_sub(
                        yn[:, :], yc[:, :], pD.ap[:, 0:128]),
                        wait=[s_mm], inc=[s_dve])
                    pD.mark(s_dve)
                U = Yb[7 % 2]
                if BCUT < 6:
                    continue
                p.do("vector", lambda e: e.tensor_mul(w2b[:, :], pA.ap[:, 128:256],
                                                      d2[:, :]),
                     wait=[(s_act, s_act.v), (s_mm, s_mm.v)], inc=[s_dve])
                pA.mark(s_dve)
                p.do("tensor", lambda e: e.matmul(
                    pE.ap[:, 128:256], w2b[:, :], U[:, :], start=True, stop=False),
                    wait=[(s_dve, s_dve.v)], inc=[s_mm])
                p.do("tensor", lambda e: e.matmul(
                    pE.ap[:, 128:256], Qb[:, :], S0b[:, :], start=False, stop=True),
                    inc=[s_mm])
                A(lambda e: e.activation(sqo[:, :], pE.ap[:, 128:256], AF.Square),
                  wait=[s_mm], inc=[s_act])
                p.do("vector", lambda e: e.reduce_sum(ms[:, :], sqo[:, :], axis=AX),
                     wait=[s_act], inc=[s_dve])
                A(lambda e: e.activation(ms[:, :], ms[:, :], AF.Ln,
                                         scale=1.0 / DV, bias=EPS),
                  wait=[s_dve], inc=[s_act])
                A(lambda e: e.activation(ms[:, :], ms[:, :], AF.Exp, scale=-0.5),
                  wait=[s_act], inc=[s_act])
                p.do("vector", lambda e: e.tensor_scalar(
                    o1[:, :], pE.ap[:, 128:256], ms[:, 0:1], None, ALU.mult),
                    wait=[s_act], inc=[s_dve])
                pE.mark(s_dve)
                p.do("vector", lambda e: e.tensor_mul(o2[:, :], o1[:, :], onwB[:, :]),
                     wait=[s_dve], inc=[s_dve])
                p.do("tensor", lambda e: e.transpose(
                    pF.ap[:, 0:128], o2[:, :], C.id128[:, :]),
                    wait=[s_dve] + pF.take(), inc=[s_mm])
                p.do("vector", lambda e, sl=sl: e.tensor_mul(
                    S_oo.ap[:, :], pF.ap[:, 0:128], S_gs.ap[:, sl]),
                    wait=[s_mm] + S_oo.take(), inc=[s_dve])
                pF.mark(s_dve)
                p.dma("gpsimd",
                      io["ofm_s"][128 * h:128 * (h + 1),
                                  b * TB + cc * CHUNK: b * TB + (cc + 1) * CHUNK],
                      S_oo.ap[:, :], wait=[(s_dve, s_dve.v)],
                      inc=[(S_oo.st(p), 16)])
                S_oo.mark(S_oo.st(p))
                if BCUT < 7:
                    continue
                A(lambda e: e.activation(drow[:, :], gc_row[:, :], AF.Exp,
                                         scale=-1.0, bias=gc_row[:, 127:128]),
                  wait=[(s_dve, s_dve.v)], inc=[s_act])
                if BCUT < 7.2:
                    continue
                BV2 = _os.environ.get("BVAR2", "").split(",")
                if "nodrow" not in BV2:
                    pass
                p.do("tensor", lambda e: e.matmul(
                    pH.ap[0:64, 128:256], C.ones_row_f[:, 0:64], drow[:, :],
                    start=True, stop=True), wait=[s_act] + pH.take(), inc=[s_mm])
                p.do("vector", lambda e, sl=sl: e.tensor_mul(
                    Kd[:, :], S_k.ap[:, sl], pH.ap[0:64, 128:256]),
                    wait=[s_mm], inc=[s_dve])
                pH.mark(s_dve)
                A(lambda e: e.activation(bc64[:, :],
                                         pG.ap[0:64, 256 + CHUNK - 1:256 + CHUNK],
                                         AF.Exp),
                  wait=[(s_mm, s_mm.v)], inc=[s_act])
                pG.mark(s_act)
                if BCUT < 7.4:
                    continue
                p.do("tensor", lambda e: e.transpose(
                    pF.ap[:, 128:192], Kd[:, :], C.id128[0:64, 0:64]),
                    wait=[s_dve] + pF.take(), inc=[s_mm])
                p.do("vector", lambda e: e.tensor_copy(Kdt[:, :], pF.ap[:, 128:192]),
                     wait=[s_mm], inc=[s_dve])
                pF.mark(s_dve)
                if BCUT < 7.6:
                    continue
                p.do("tensor", lambda e: e.matmul(
                    pE.ap[0:64, 256:384], Kdt[:, :], U[:, :],
                    start=True, stop=True),
                    wait=[(s_dve, s_dve.v)] + pE.take(), inc=[s_mm])
                if BCUT < 7.8:
                    continue
                p.do("vector", lambda e, cur=cur, nxt=nxt: e.scalar_tensor_tensor(
                    nxt[:, :], cur[:, :], bc64[:, 0:1], pE.ap[0:64, 256:384],
                    ALU.mult, ALU.add), wait=[s_mm, s_act], inc=[s_dve])
                pE.mark(s_dve)

            for sl_ in (S_q, S_k, S_v, S_gs, S_gr, S_br):
                sl_.mark(s_dve)
                sl_.mark(s_mm)

    p.barrier("b_end")
    pctx.close()



def emit_load_weights(p, ctx, specs, tag):
    nc = p.nc
    wld_sems = [p.sem(f"wld_{tag}0"), p.sem(f"wld_{tag}1")]
    s_wcv = p.sem(f"wcv_{tag}")
    with_ctx = [ctx.enter_context(nc.sbuf_tensor(f"wstg_{tag}{i}", [128, 1024], F32))
                for i in range(2)]
    stg = Rot(with_ctx)
    for (dram, sb, kdim, mdim) in specs:
        for kc in range(cdiv(kdim, 128)):
            rows = min(128, kdim - kc * 128)
            for m0 in range(0, mdim, 1024):
                mw = min(1024, mdim - m0)
                sl = stg.next()
                ws = wld_sems[(stg.i - 1) % 2]
                p.dma("sync", sl.ap[:rows, :mw],
                      dram[kc * 128:kc * 128 + rows, m0:m0 + mw],
                      wait=sl.take(), inc=[(ws, 16)])
                p.do("gpsimd", lambda e, sl=sl, kc=kc, rows=rows, sb=sb,
                     mdim=mdim, m0=m0, mw=mw:
                     e.tensor_copy(sb[:rows, kc * mdim + m0:kc * mdim + m0 + mw],
                                   sl.ap[:rows, :mw]),
                     wait=[ws], inc=[s_wcv])
                sl.mark(s_wcv)


def emit_norm_fm(p, C, sems, src_slots, src_stride, TILE, out_fn, kind,
                 stat, P_s1, P_bc, R_sq, wvec=None, ln_out=None):
    raise NotImplementedError



def emit_stage_c(p, ctx, C, cfg, io):
    nc = p.nc
    reuse = cfg.get("reuse", True)
    Tl, TILE = cfg["Tl"], cfg["TILE"]
    ntiles = Tl // TILE

    s_ld = p.sem("c_ld")
    s_mm = p.sem("c_mm")
    s_dve = p.sem("c_dve")
    s_act = p.sem("c_act")
    s_st = p.sem("c_st")

    from contextlib import ExitStack
    with ExitStack() as _c1:
        c1 = _c1 if reuse else ctx
        pctx = _c1
        w_o = c1.enter_context(nc.sbuf_tensor("w_o", [128, 16 * D], BF16))
        emit_load_weights(p, c1, [(io["L0_attn_a_o_w"], w_o, VD, D)], "c1")
        postw = c1.enter_context(nc.sbuf_tensor("c_postw", [128, 8], F32))
        p.dma("sync", postw[:, :], io["pre_postw"][:, :], inc=[(s_ld, 16)])
        p.barrier("c1_w")

        S_of = Slot(c1.enter_context(nc.sbuf_tensor("c_of", [128, 16 * TILE], BF16)))
        S_l0 = Slot(c1.enter_context(nc.sbuf_tensor("c_l0", [128, 8 * TILE], F32)))
        S_x1 = Slot(c1.enter_context(nc.sbuf_tensor("c_x1", [128, 8 * TILE], F32)))
        S_h2 = Slot(c1.enter_context(nc.sbuf_tensor("c_h2", [128, 8 * TILE], BF16)))
        R_sq = Rot([c1.enter_context(nc.sbuf_tensor(f"c_sq{i}", [128, TILE], BF16))
                    for i in range(8)])
        stat = c1.enter_context(nc.sbuf_tensor("c_stat", [1, TILE], F32))
        R_pm = Rot([pctx.enter_context(nc.psum_tensor(f"c_pm{i}", [128, 512], F32))
                    for i in range(2)])
        P_s1 = Slot(pctx.enter_context(nc.psum_tensor("c_ps1", [1, 512], F32)))
        P_bc = Slot(pctx.enter_context(nc.psum_tensor("c_pb", [128, 512], F32)))

        for t in range(ntiles):
            tok0 = t * TILE
            ofw = S_of.take()
            for c in range(16):
                p.dma("sync", S_of.ap[:, c * TILE:(c + 1) * TILE],
                      io["ofm_s"][c * 128:(c + 1) * 128, tok0:tok0 + TILE],
                      wait=(ofw if c == 0 else []), inc=[(s_ld, 16)])
            l0w = S_l0.take()
            for c in range(8):
                p.dma("sync", S_l0.ap[:, c * TILE:(c + 1) * TILE],
                      io["ln0_s"][c * 128:(c + 1) * 128, tok0:tok0 + TILE],
                      wait=(l0w if c == 0 else []), inc=[(s_ld, 16)])
            ld_done = s_ld.v
            x1w = S_x1.take()
            for mc in range(8):
                pm = R_pm.next()
                for kc in range(16):
                    p.do("tensor", lambda e, mc=mc, kc=kc, pm=pm: e.matmul(
                        pm.ap[:, :TILE],
                        w_o[:, kc * D + mc * 128: kc * D + mc * 128 + 128],
                        S_of.ap[:, kc * TILE:(kc + 1) * TILE],
                        start=(kc == 0), stop=(kc == 15)),
                        wait=[(s_ld, ld_done)] + (pm.take() if kc == 0 else []),
                        inc=([s_mm] if kc == 15 else []))
                p.do("vector", lambda e, mc=mc, pm=pm: e.tensor_add(
                    S_x1.ap[:, mc * TILE:(mc + 1) * TILE],
                    S_l0.ap[:, mc * TILE:(mc + 1) * TILE], pm.ap[:, :TILE]),
                    wait=[s_mm] + (x1w if mc == 0 else []), inc=[s_dve])
                pm.mark(s_dve)
            S_of.mark(s_mm)
            S_l0.mark(s_dve)
            for c in range(8):
                p.dma("gpsimd", io["x1_s"][c * 128:(c + 1) * 128, tok0:tok0 + TILE],
                      S_x1.ap[:, c * TILE:(c + 1) * TILE],
                      wait=[(s_dve, s_dve.v)], inc=[(S_x1.st(p), 16)])
            S_x1.mark(S_x1.st(p))
            for c in range(8):
                sq = R_sq.next()
                p.do("scalar", lambda e, c=c, sq=sq: e.activation(
                    sq.ap[:, :], S_x1.ap[:, c * TILE:(c + 1) * TILE], AF.Square),
                    wait=[(s_dve, s_dve.v)] + sq.take(), inc=[s_act])
                p.do("tensor", lambda e, c=c, sq=sq: e.matmul(
                    P_s1.ap[:, :TILE], C.ones_col_b[:, :], sq.ap[:, :],
                    start=(c == 0), stop=(c == 7)),
                    wait=[s_act] + (P_s1.take() if c == 0 else []),
                    inc=([s_mm] if c == 7 else []))
            R_sq.mark_all(s_mm)
            stw = stat
            p.do("scalar", lambda e: e.activation(
                stat[:, :TILE], P_s1.ap[:, :TILE], AF.Ln, scale=1.0 / D,
                bias=EPS), wait=[s_mm, (s_dve, s_dve.v)], inc=[s_act])
            P_s1.mark(s_act)
            p.do("scalar", lambda e: e.activation(
                stat[:, :TILE], stat[:, :TILE], AF.Exp, scale=-0.5),
                wait=[s_act], inc=[s_act])
            p.do("tensor", lambda e: e.matmul(
                P_bc.ap[:, :TILE], C.ones_row_f[:, :], stat[:, :TILE],
                start=True, stop=True), wait=[s_act] + P_bc.take(), inc=[s_mm])
            h2w = S_h2.take()
            for c in range(8):
                p.do("vector", lambda e, c=c: e.scalar_tensor_tensor(
                    S_h2.ap[:, c * TILE:(c + 1) * TILE],
                    S_x1.ap[:, c * TILE:(c + 1) * TILE],
                    postw[:, c:c + 1], P_bc.ap[:, :TILE], ALU.mult, ALU.mult),
                    wait=[s_mm] + (h2w if c == 0 else []), inc=[s_dve])
            P_bc.mark(s_dve)
            S_x1.mark(s_dve)
            for c in range(8):
                p.dma("gpsimd", io["h2_s"][c * 128:(c + 1) * 128, tok0:tok0 + TILE],
                      S_h2.ap[:, c * TILE:(c + 1) * TILE],
                      wait=[(s_dve, s_dve.v)], inc=[(S_h2.st(p), 16)])
            S_h2.mark(S_h2.st(p))
        p.barrier("c1_end")

    NFC = cdiv(FF, 128)
    with ExitStack() as _c2:
        c2 = _c2 if reuse else ctx
        pctx = _c2
        w_gt = c2.enter_context(nc.sbuf_tensor("w_gt", [128, 8 * FF], BF16))
        w_up = c2.enter_context(nc.sbuf_tensor("w_up", [128, 8 * FF], BF16))
        emit_load_weights(p, c2, [(io["L0_attn_a_gate_w"], w_gt, D, FF),
                                  (io["L0_attn_a_up_w"], w_up, D, FF)], "c2")
        p.barrier("c2_w")
        S_h2q = Slot(c2.enter_context(nc.sbuf_tensor("c2_h2", [128, 8 * TILE], BF16)))
        S_g1 = Slot(c2.enter_context(nc.sbuf_tensor("c2_g1", [128, NFC * TILE], BF16)))
        R_pmq = Rot([pctx.enter_context(nc.psum_tensor(f"c2_pm{i}", [128, 512], F32))
                    for i in range(2)])
        for t in range(ntiles):
            tok0 = t * TILE
            h2wq = S_h2q.take()
            for c in range(8):
                p.dma("sync", S_h2q.ap[:, c * TILE:(c + 1) * TILE],
                      io["h2_s"][c * 128:(c + 1) * 128, tok0:tok0 + TILE],
                      wait=(h2wq if c == 0 else []), inc=[(s_ld, 16)])
            ld_done = s_ld.v
            g1w = S_g1.take()
            for mc in range(NFC):
                mrows = min(128, FF - mc * 128)
                pm = R_pmq.next()
                for kc in range(8):
                    p.do("tensor", lambda e, mc=mc, kc=kc, pm=pm, mrows=mrows:
                         e.matmul(
                             pm.ap[:mrows, :TILE],
                             w_gt[:, kc * FF + mc * 128: kc * FF + mc * 128 + mrows],
                             S_h2q.ap[:, kc * TILE:(kc + 1) * TILE],
                             start=(kc == 0), stop=(kc == 7)),
                         wait=[(s_ld, ld_done)] + (pm.take() if kc == 0 else []),
                         inc=([s_mm] if kc == 7 else []))
                p.do("scalar", lambda e, mc=mc, pm=pm, mrows=mrows: e.activation(
                    S_g1.ap[:mrows, mc * TILE:mc * TILE + TILE], pm.ap[:mrows, :TILE],
                    AF.Silu), wait=[s_mm] + (g1w if mc == 0 else []), inc=[s_act])
                pm.mark(s_act)
            for mc in range(NFC):
                mrows = min(128, FF - mc * 128)
                pm = R_pmq.next()
                for kc in range(8):
                    p.do("tensor", lambda e, mc=mc, kc=kc, pm=pm, mrows=mrows:
                         e.matmul(
                             pm.ap[:mrows, :TILE],
                             w_up[:, kc * FF + mc * 128: kc * FF + mc * 128 + mrows],
                             S_h2q.ap[:, kc * TILE:(kc + 1) * TILE],
                             start=(kc == 0), stop=(kc == 7)),
                         wait=[(s_act, s_act.v)] + (pm.take() if kc == 0 else []),
                         inc=([s_mm] if kc == 7 else []))
                p.do("vector", lambda e, mc=mc, pm=pm, mrows=mrows: e.tensor_mul(
                    S_g1.ap[:mrows, mc * TILE:mc * TILE + TILE],
                    S_g1.ap[:mrows, mc * TILE:mc * TILE + TILE],
                    pm.ap[:mrows, :TILE]), wait=[s_mm], inc=[s_dve])
                pm.mark(s_dve)
                p.dma("gpsimd",
                      io["mm_s"][mc * 128:mc * 128 + mrows, tok0:tok0 + TILE],
                      S_g1.ap[:mrows, mc * TILE:mc * TILE + TILE],
                      wait=[(s_dve, s_dve.v)], inc=[(S_g1.st(p), 16)])
            S_g1.mark(S_g1.st(p))
            S_h2q.mark(s_mm)
        p.barrier("c2_end")

    with ExitStack() as _c3:
        c3 = _c3 if reuse else ctx
        pctx = _c3
        w_dn = c3.enter_context(nc.sbuf_tensor("w_dn", [128, NFC * D], BF16))
        emit_load_weights(p, c3, [(io["L0_attn_a_down_w"], w_dn, FF, D)], "c3")
        gwt = c3.enter_context(nc.sbuf_tensor("c3_gwt", [128, 8 * NE], BF16))
        emit_load_weights(p, c3, [(io["gate_wT"], gwt, D, NE)], "c3b")
        p.barrier("c3_w")
        S_mmz = Slot(c3.enter_context(nc.sbuf_tensor("c3_mm", [128, NFC * TILE], BF16)))
        S_x1z = Slot(c3.enter_context(nc.sbuf_tensor("c3_x1", [128, 8 * TILE], F32)))
        S_xtz = Slot(c3.enter_context(nc.sbuf_tensor("c3_xt", [128, 8 * TILE], F32)))
        S_lnz = Slot(c3.enter_context(nc.sbuf_tensor("c3_ln", [128, 8 * TILE], BF16)))
        R_sqz = Rot([c3.enter_context(nc.sbuf_tensor(f"c3_sq{i}", [128, TILE], BF16))
                    for i in range(8)])
        statz = c3.enter_context(nc.sbuf_tensor("c3_statz", [1, 4 * TILE], F32))
        lgt = Slot(c3.enter_context(nc.sbuf_tensor("c3_lg", [NE, TILE], F32)))
        R_pmz = Rot([pctx.enter_context(nc.psum_tensor(f"c3_pm{i}", [128, 512], F32))
                    for i in range(2)])
        P_s1z = Slot(pctx.enter_context(nc.psum_tensor("c3_ps1", [1, 512], F32)))
        P_s2z = Slot(pctx.enter_context(nc.psum_tensor("c3_ps2", [1, 512], F32)))
        P_bcz = Slot(pctx.enter_context(nc.psum_tensor("c3_pb", [128, 512], F32)))
        P_lg = Slot(pctx.enter_context(nc.psum_tensor("c3_plg", [NE, 512], F32)))
        ST = TILE
        for t in range(ntiles):
            tok0 = t * TILE
            mmwz = S_mmz.take()
            for c in range(NFC):
                rows = min(128, FF - c * 128)
                p.dma("sync", S_mmz.ap[:rows, c * TILE:(c + 1) * TILE],
                      io["mm_s"][c * 128:c * 128 + rows, tok0:tok0 + TILE],
                      wait=(mmwz if c == 0 else []), inc=[(s_ld, 16)])
            x1wz = S_x1z.take()
            for c in range(8):
                p.dma("sync", S_x1z.ap[:, c * TILE:(c + 1) * TILE],
                      io["x1_s"][c * 128:(c + 1) * 128, tok0:tok0 + TILE],
                      wait=(x1wz if c == 0 else []), inc=[(s_ld, 16)])
            xtwz = S_xtz.take()
            for c in range(8):
                p.dma("sync", S_xtz.ap[:, c * TILE:(c + 1) * TILE],
                      io["x_fm"][c * 128:(c + 1) * 128, tok0:tok0 + TILE],
                      wait=(xtwz if c == 0 else []), inc=[(s_ld, 16)])
            ld_done = s_ld.v
            for mc in range(8):
                pm = R_pmz.next()
                for kc in range(NFC):
                    krows = min(128, FF - kc * 128)
                    p.do("tensor", lambda e, mc=mc, kc=kc, pm=pm, krows=krows:
                         e.matmul(
                             pm.ap[:, :TILE],
                             w_dn[:krows, kc * D + mc * 128: kc * D + mc * 128 + 128],
                             S_mmz.ap[:krows, kc * TILE:(kc + 1) * TILE],
                             start=(kc == 0), stop=(kc == NFC - 1)),
                         wait=[(s_ld, ld_done)] + (pm.take() if kc == 0 else []),
                         inc=([s_mm] if kc == NFC - 1 else []))
                p.do("vector", lambda e, mc=mc, pm=pm: e.tensor_add(
                    S_x1z.ap[:, mc * TILE:(mc + 1) * TILE],
                    S_x1z.ap[:, mc * TILE:(mc + 1) * TILE], pm.ap[:, :TILE]),
                    wait=[s_mm], inc=[s_dve])
                pm.mark(s_dve)
                p.do("vector", lambda e, mc=mc: e.tensor_add(
                    S_xtz.ap[:, mc * TILE:(mc + 1) * TILE],
                    S_xtz.ap[:, mc * TILE:(mc + 1) * TILE],
                    S_x1z.ap[:, mc * TILE:(mc + 1) * TILE]),
                    wait=[s_dve], inc=[s_dve])
            S_mmz.mark(s_mm)
            S_x1z.mark(s_dve)
            for c in range(8):
                p.dma("gpsimd", io["xo"][c * 128:(c + 1) * 128, tok0:tok0 + TILE],
                      S_xtz.ap[:, c * TILE:(c + 1) * TILE],
                      wait=[(s_dve, s_dve.v)], inc=[(S_xtz.st(p), 16)])
            S_xtz.mark(S_xtz.st(p))
            for c in range(8):
                p.do("tensor", lambda e, c=c: e.matmul(
                    P_s1z.ap[:, :TILE], C.ones_col_f[:, :],
                    S_xtz.ap[:, c * TILE:(c + 1) * TILE],
                    start=(c == 0), stop=(c == 7)),
                    wait=[(s_dve, s_dve.v)] + (P_s1z.take() if c == 0 else []),
                    inc=([s_mm] if c == 7 else []))
            for c in range(8):
                sq = R_sqz.next()
                p.do("scalar", lambda e, c=c, sq=sq: e.activation(
                    sq.ap[:, :], S_xtz.ap[:, c * TILE:(c + 1) * TILE], AF.Square),
                    wait=[(s_dve, s_dve.v)] + sq.take(), inc=[s_act])
                p.do("tensor", lambda e, c=c, sq=sq: e.matmul(
                    P_s2z.ap[:, :TILE], C.ones_col_b[:, :], sq.ap[:, :],
                    start=(c == 0), stop=(c == 7)),
                    wait=[s_act] + (P_s2z.take() if c == 0 else []),
                    inc=([s_mm] if c == 7 else []))
            R_sqz.mark_all(s_mm)
            p.do("vector", lambda e: e.tensor_scalar_mul(
                statz[:, 0 * ST:0 * ST + TILE], P_s1z.ap[:, :TILE], 1.0 / D),
                wait=[s_mm], inc=[s_dve])
            P_s1z.mark(s_dve)
            p.do("scalar", lambda e: e.activation(
                statz[:, 1 * ST:1 * ST + TILE], statz[:, 0 * ST:0 * ST + TILE],
                AF.Square), wait=[s_dve], inc=[s_act])
            p.do("vector", lambda e: e.scalar_tensor_tensor(
                statz[:, 2 * ST:2 * ST + TILE], P_s2z.ap[:, :TILE], 1.0 / D,
                statz[:, 1 * ST:1 * ST + TILE], ALU.mult, ALU.subtract),
                wait=[s_mm, s_act], inc=[s_dve])
            P_s2z.mark(s_dve)
            p.do("scalar", lambda e: e.activation(
                statz[:, 2 * ST:2 * ST + TILE], statz[:, 2 * ST:2 * ST + TILE],
                AF.Ln, bias=LN_EPS), wait=[s_dve], inc=[s_act])
            p.do("scalar", lambda e: e.activation(
                statz[:, 2 * ST:2 * ST + TILE], statz[:, 2 * ST:2 * ST + TILE],
                AF.Exp, scale=-0.5), wait=[s_act], inc=[s_act])
            p.do("tensor", lambda e: e.matmul(
                P_bcz.ap[:, :TILE], C.ones_row_f[:, :], statz[:, 0 * ST:0 * ST + TILE],
                start=True, stop=True), wait=[(s_act, s_act.v)] + P_bcz.take(),
                inc=[s_mm])
            lnwz = S_lnz.take()
            for c in range(8):
                p.do("vector", lambda e, c=c: e.tensor_sub(
                    S_lnz.ap[:, c * TILE:(c + 1) * TILE],
                    S_xtz.ap[:, c * TILE:(c + 1) * TILE], P_bcz.ap[:, :TILE]),
                    wait=[s_mm] + (lnwz if c == 0 else []), inc=[s_dve])
            P_bcz.mark(s_dve)
            S_xtz.mark(s_dve)
            p.do("tensor", lambda e: e.matmul(
                P_bcz.ap[:, :TILE], C.ones_row_f[:, :], statz[:, 2 * ST:2 * ST + TILE],
                start=True, stop=True), wait=[(s_dve, s_dve.v)] + P_bcz.take(),
                inc=[s_mm])
            for c in range(8):
                p.do("vector", lambda e, c=c: e.tensor_mul(
                    S_lnz.ap[:, c * TILE:(c + 1) * TILE],
                    S_lnz.ap[:, c * TILE:(c + 1) * TILE], P_bcz.ap[:, :TILE]),
                    wait=[s_mm], inc=[s_dve])
            P_bcz.mark(s_dve)
            for c in range(8):
                p.dma("gpsimd", io["lno"][c * 128:(c + 1) * 128, tok0:tok0 + TILE],
                      S_lnz.ap[:, c * TILE:(c + 1) * TILE],
                      wait=[(s_dve, s_dve.v)], inc=[(S_lnz.st(p), 16)])
            S_lnz.mark(S_lnz.st(p))
            for kc in range(8):
                p.do("tensor", lambda e, kc=kc: e.matmul(
                    P_lg.ap[:, :TILE], gwt[:, kc * NE:(kc + 1) * NE],
                    S_lnz.ap[:, kc * TILE:(kc + 1) * TILE],
                    start=(kc == 0), stop=(kc == 7)),
                    wait=[(s_dve, s_dve.v)] + (P_lg.take() if kc == 0 else []),
                    inc=([s_mm] if kc == 7 else []))
            S_lnz.mark(s_mm)
            lgw = lgt.take()
            p.do("scalar", lambda e: e.activation(
                lgt.ap[:, :TILE], P_lg.ap[:, :TILE], AF.Copy),
                wait=[s_mm] + lgw, inc=[s_act])
            P_lg.mark(s_act)
            p.dma("gpsimd", io["lgo"][:, tok0:tok0 + TILE], lgt.ap[:, :TILE],
                  wait=[s_act], inc=[(lgt.st(p), 16)])
            lgt.mark(lgt.st(p))
        p.barrier("c3_end")



A_WEIGHT_NAMES = ("L0_attn_a_q_w", "L0_attn_a_k_w", "L0_attn_a_v_w",
                  "L0_attn_a_b_w", "L0_attn_a_a_w", "L0_attn_a_g_w",
                  "L0_attn_a_o_w", "L0_attn_a_gate_w", "L0_attn_a_up_w",
                  "L0_attn_a_down_w")
A_VEC_NAMES = ("L0_attn_a_norm_w", "L0_attn_a_A_log", "L0_attn_a_dt_bias",
               "L0_attn_a_conv_q_w", "L0_attn_a_conv_k_w", "L0_attn_a_conv_v_w",
               "L0_attn_a_o_norm_w", "L0_attn_a_post_norm_w")
A_WSHAPES = {"L0_attn_a_q_w": (D, D), "L0_attn_a_k_w": (D, D),
             "L0_attn_a_v_w": (D, VD), "L0_attn_a_b_w": (D, NH),
             "L0_attn_a_a_w": (D, NH), "L0_attn_a_g_w": (D, VD),
             "L0_attn_a_o_w": (VD, D), "L0_attn_a_gate_w": (D, FF),
             "L0_attn_a_up_w": (D, FF), "L0_attn_a_down_w": (FF, D)}
A_VSHAPES = {"L0_attn_a_norm_w": (D,), "L0_attn_a_A_log": (NH,),
             "L0_attn_a_dt_bias": (NH,), "L0_attn_a_conv_q_w": (D, 4),
             "L0_attn_a_conv_k_w": (D, 4), "L0_attn_a_conv_v_w": (VD, 4),
             "L0_attn_a_o_norm_w": (DV,), "L0_attn_a_post_norm_w": (D,)}


def build_l1(cfg, stages=("A", "B", "C"), reuse=True):
    from contextlib import ExitStack
    Tl = cfg["Tl"]
    nc = bass.Bass("TRN2", target_bir_lowering=False, debug=False)
    io = {}
    io["x_fm"] = nc.dram_tensor("x_fm", [D, Tl], F32, kind="ExternalInput").ap()
    for n in A_WEIGHT_NAMES:
        io[n] = nc.dram_tensor(n, list(A_WSHAPES[n]), F32, kind="ExternalInput").ap()
    for n in A_VEC_NAMES:
        io[n] = nc.dram_tensor(n, list(A_VSHAPES[n]), F32, kind="ExternalInput").ap()
    io["gate_wT"] = nc.dram_tensor("gate_wT", [D, NE], F32, kind="ExternalInput").ap()
    io["pre_normw"] = nc.dram_tensor("pre_normw", [128, 8], F32, kind="ExternalInput").ap()
    io["pre_convw"] = nc.dram_tensor("pre_convw", [128, 128], F32, kind="ExternalInput").ap()
    io["pre_postw"] = nc.dram_tensor("pre_postw", [128, 8], F32, kind="ExternalInput").ap()
    consts = host_consts()
    for k in CONST_NAMES:
        arr = consts[k]
        dt = BF16 if arr.dtype == BF else F32
        io[k] = nc.dram_tensor(k, list(arr.shape), dt, kind="ExternalInput").ap()
    io["ln0_s"] = nc.dram_tensor("ln0_s", [D, Tl], F32).ap()
    io["q_s"] = nc.dram_tensor("q_s", [D, Tl], BF16).ap()
    io["k_s"] = nc.dram_tensor("k_s", [D, Tl], BF16).ap()
    io["v_s"] = nc.dram_tensor("v_s", [VD, Tl], BF16).ap()
    io["gsil_s"] = nc.dram_tensor("gsil_s", [VD, Tl], BF16).ap()
    io["beta_s"] = nc.dram_tensor("beta_s", [NH, Tl], F32).ap()
    io["g_s"] = nc.dram_tensor("g_s", [NH, Tl], F32).ap()
    io["ofm_s"] = nc.dram_tensor("ofm_s", [VD, Tl], BF16).ap()
    io["x1_s"] = nc.dram_tensor("x1_s", [D, Tl], F32).ap()
    io["h2_s"] = nc.dram_tensor("h2_s", [D, Tl], BF16).ap()
    io["mm_s"] = nc.dram_tensor("mm_s", [FF, Tl], BF16).ap()
    io["xo"] = nc.dram_tensor("xo", [D, Tl], F32, kind="ExternalOutput").ap()
    io["lno"] = nc.dram_tensor("lno", [D, Tl], BF16, kind="ExternalOutput").ap()
    io["lgo"] = nc.dram_tensor("lgo", [NE, Tl], F32, kind="ExternalOutput").ap()

    ctx = ExitStack()
    p = Prog(nc, ctx)
    s_cld = p.sem("cld")
    C = Consts(p, ctx, io, s_cld)
    p.do("sync", lambda e: e.nop(), wait=[(s_cld, s_cld.v)])
    p.barrier("consts")
    from contextlib import ExitStack as ES
    with ctx:
        if "A" in stages:
            with ES() as sctx:
                emit_stage_a(p, (sctx if reuse else ctx), C, cfg, io)
        if "B" in stages:
            with ES() as sctx:
                emit_stage_b2(p, (sctx if reuse else ctx), C, cfg, io)
        if "C" in stages:
            with ES() as sctx:
                emit_stage_c(p, (sctx if reuse else ctx), C, cfg, io)
        p.build_block()
    return nc


def l1_inputs(core_x_fm, weights, cfg):
    m = {"x_fm": np.ascontiguousarray(core_x_fm)}
    for n in A_WEIGHT_NAMES + A_VEC_NAMES:
        m[n] = np.ascontiguousarray(weights[n], dtype=np.float32)
    m["gate_wT"] = np.ascontiguousarray(weights["L0_ffn_gate_weight"].T,
                                        dtype=np.float32)
    pcw = np.zeros((128, 128), np.float32)
    for i, n in enumerate(("conv_q_w", "conv_k_w")):
        w = np.asarray(weights["L0_attn_a_" + n], np.float32)
        for c in range(8):
            pcw[:, i * 32 + c * 4: i * 32 + (c + 1) * 4] = w[c * 128:(c + 1) * 128]
    wv = np.asarray(weights["L0_attn_a_conv_v_w"], np.float32)
    for c in range(16):
        pcw[:, 64 + c * 4: 64 + (c + 1) * 4] = wv[c * 128:(c + 1) * 128]
    m["pre_convw"] = pcw
    m["pre_normw"] = np.ascontiguousarray(
        np.asarray(weights["L0_attn_a_norm_w"], np.float32).reshape(-1, 128).T)
    m["pre_postw"] = np.ascontiguousarray(
        np.asarray(weights["L0_attn_a_post_norm_w"], np.float32).reshape(-1, 128).T)
    m.update(host_consts())
    return m



def emit_kth(p, ctx, C, cfg, io, glg_name, olg_name, gates_dram):
    from contextlib import ExitStack
    nc = p.nc
    Tl = cfg["Tl"]
    NGC = cfg["NGC"]
    NITER = 50

    s_ld = p.sem("k_ld")
    s_mm = p.sem("k_mm")
    s_dve = p.sem("k_dve")
    s_act = p.sem("k_act")

    with ExitStack() as kctx:
        pctx = kctx
        glg = kctx.enter_context(nc.sbuf_tensor("k_glg", [128, NGC], F32))
        dump = kctx.enter_context(nc.sbuf_tensor("k_dump", [128, NGC], F32))
        olg = kctx.enter_context(nc.sbuf_tensor("k_olg", [NE, Tl], F32))
        kth = kctx.enter_context(nc.sbuf_tensor("k_kth", [128, 1], F32))
        rowm = kctx.enter_context(nc.sbuf_tensor("k_rowm", [128, 1], F32))
        rowc = kctx.enter_context(nc.sbuf_tensor("k_rowc", [128, 1], F32))
        tcur = kctx.enter_context(nc.sbuf_tensor("k_t", [1, 1], F32))
        tstep = kctx.enter_context(nc.sbuf_tensor("k_step", [1, 1], F32))
        sgn = kctx.enter_context(nc.sbuf_tensor("k_sgn", [1, 1], F32))
        st1 = kctx.enter_context(nc.sbuf_tensor("k_st1", [1, 1], F32))
        t64 = kctx.enter_context(nc.sbuf_tensor("k_t64", [128, 1], F32))
        mask = kctx.enter_context(nc.sbuf_tensor("k_mask", [NE, Tl], F32))
        gout = kctx.enter_context(nc.sbuf_tensor("k_gout", [NE, Tl], BF16))
        P_a = Slot(pctx.enter_context(nc.psum_tensor("k_pa", [128, 512], F32)))
        P_b = Slot(pctx.enter_context(nc.psum_tensor("k_pb", [1, 512], F32)))

        p.dma("sync", glg[:, :], io[glg_name][:, :], inc=[(s_ld, 16)])
        p.dma("sync", olg[:, :], io[olg_name][:, :], inc=[(s_ld, 16)])
        p.dma("sync", kth[:, :], io["c_kth"][:, :], inc=[(s_ld, 16)])

        p.do("vector", lambda e: e.tensor_reduce(
            rowm[:, :], glg[:, :], AX, ALU.max, apply_absolute_value=True),
            wait=[s_ld], inc=[s_dve])
        p.do("tensor", lambda e: e.matmul(
            P_b.ap[:, 0:1], rowm[:, :], C.ones_col_f[:, :],
            start=True, stop=True), wait=[s_dve], inc=[s_mm])
        p.do("scalar", lambda e: e.activation(
            tstep[:, :], P_b.ap[0:1, 0:1], AF.Copy, scale=1.0),
            wait=[s_mm], inc=[s_act])
        P_b.mark(s_act)
        p.do("vector", lambda e: e.memset(tcur[:, :], 0.0), inc=[s_dve])
        p.do("vector", lambda e: e.memset(t64[:, :], 0.0), inc=[s_dve])

        for it in range(NITER):
            p.do("vector", lambda e: e.tensor_scalar(
                dump[:, :], glg[:, :], t64[:, 0:1], None, ALU.is_lt, ALU.add,
                accum_out=rowc[:, :]), wait=[(s_dve, s_dve.v), (s_act, s_act.v)],
                inc=[s_dve])
            p.do("tensor", lambda e: e.matmul(
                P_b.ap[:, 0:1], rowc[:, :], C.ones_col_f[:, :],
                start=True, stop=True), wait=[s_dve] + P_b.take(), inc=[s_mm])
            p.do("scalar", lambda e: e.activation(
                sgn[:, :], P_b.ap[0:1, 0:1], AF.Sign, scale=-1.0,
                bias=kth[0:1, 0:1]), wait=[s_mm], inc=[s_act])
            P_b.mark(s_act)
            p.do("vector", lambda e: e.tensor_mul(st1[:, :], sgn[:, :],
                                                  tstep[:, :]),
                 wait=[s_act], inc=[s_dve])
            p.do("vector", lambda e: e.tensor_add(tcur[:, :], tcur[:, :],
                                                  st1[:, :]),
                 wait=[s_dve], inc=[s_dve])
            p.do("vector", lambda e: e.tensor_scalar_mul(
                tstep[:, :], tstep[:, :], 0.5), wait=[(s_dve, s_dve.v)],
                inc=[s_dve])
            p.do("tensor", lambda e: e.matmul(
                P_a.ap[:, 0:1], C.ones_row_f[:, :], tcur[:, :],
                start=True, stop=True), wait=[(s_dve, s_dve.v)] + P_a.take(),
                inc=[s_mm])
            p.do("scalar", lambda e: e.activation(
                t64[:, :], P_a.ap[:, 0:1], AF.Copy), wait=[s_mm], inc=[s_act])
            P_a.mark(s_act)

        p.do("vector", lambda e: e.tensor_scalar(
            mask[:, :], olg[:, :], t64[0:NE, 0:1], None, ALU.is_ge),
            wait=[(s_act, s_act.v)], inc=[s_dve])
        p.do("vector", lambda e: e.tensor_mul(gout[:, :], olg[:, :],
                                              mask[:, :]),
             wait=[s_dve], inc=[s_dve])
        s_gst = p.sem("k_gst")
        p.dma("gpsimd", io[gates_dram][:, :], gout[:, :],
              wait=[(s_dve, s_dve.v)], inc=[(s_gst, 16)])
        p.do("gpsimd", lambda e: e.nop(), wait=[(s_gst, s_gst.v)])
        p.barrier("kth_end")



def emit_ffn(p, ctx, C, cfg, io, ln_in, acc_out, gates_dram, tag):
    from contextlib import ExitStack
    nc = p.nc
    Tl, TILE = cfg["Tl"], cfg["TILE"]
    ntiles = Tl // TILE

    s_ld = p.sem(f"f_ld_{tag}")
    s_mm = p.sem(f"f_mm_{tag}")
    s_dve = p.sem(f"f_dve_{tag}")
    s_act = p.sem(f"f_act_{tag}")
    s_gb = p.sem(f"f_gb_{tag}")
    s_cv = p.sem(f"f_cv_{tag}")

    with ExitStack() as fctx:
        pctx = fctx
        W1S = Slot(fctx.enter_context(nc.sbuf_tensor(f"f_w1_{tag}", [128, 8 * FH], BF16)))
        W2S = Slot(fctx.enter_context(nc.sbuf_tensor(f"f_w2_{tag}", [128, 32 * D], BF16)))
        w1b, w2b = W1S.ap, W2S.ap
        hsb = fctx.enter_context(nc.sbuf_tensor(f"f_h_{tag}", [128, 32 * TILE // 1], BF16))
        lnt = Slot(fctx.enter_context(nc.sbuf_tensor(f"f_ln_{tag}", [128, 8 * TILE], BF16)))
        b1t = fctx.enter_context(nc.sbuf_tensor(f"f_b1_{tag}", [128, NE * 32], F32))
        b2r = fctx.enter_context(nc.sbuf_tensor(f"f_b2_{tag}", [NE, D], F32))
        b2rb = fctx.enter_context(nc.sbuf_tensor(f"f_b2b_{tag}", [NE, D], BF16))
        gbt = Slot(fctx.enter_context(nc.sbuf_tensor(f"f_gb_{tag}", [128, TILE], F32)))
        gtl = Slot(fctx.enter_context(nc.sbuf_tensor(f"f_gt_{tag}", [NE, TILE], BF16)))
        b2e = Slot(fctx.enter_context(nc.sbuf_tensor(f"f_b2e_{tag}", [128, TILE], F32)))
        R_t = Rot([fctx.enter_context(nc.sbuf_tensor(f"f_t{i}_{tag}", [128, TILE], F32))
                   for i in range(3)])
        stg = Rot([fctx.enter_context(nc.sbuf_tensor(f"f_stg{i}_{tag}", [128, 1024], F32))
                   for i in range(2)])
        R_pm = Rot([pctx.enter_context(nc.psum_tensor(f"f_pm{i}_{tag}", [128, 512], F32))
                    for i in range(4)])
        P_b2 = Slot(pctx.enter_context(nc.psum_tensor(f"f_pb2_{tag}", [128, 512], F32)))
        P_gb = Slot(pctx.enter_context(nc.psum_tensor(f"f_pgb_{tag}", [128, 512], F32)))

        p.dma("sync", b1t[:, :], io["pre_b1"][:, :], inc=[(s_ld, 16)])
        p.dma("sync", b2r[:, :], io["ffn_b2"][:, :], inc=[(s_ld, 16)])
        p.do("gpsimd", lambda e_: e_.tensor_copy(b2rb[:, :], b2r[:, :]),
             wait=[s_ld], inc=[s_cv])

        wld = [p.sem(f"f_wld{i}_{tag}") for i in range(2)]
        last_acc = {}

        for e in range(NE):
            w1w = W1S.take()
            w2w = W2S.take()
            for kc in range(8):
                for half in range(4):
                    sl = stg.next()
                    ws = wld[(stg.i - 1) % 2]
                    p.dma("sync", sl.ap[:, :],
                          io["ffn_w1T"][e, kc * 128:(kc + 1) * 128,
                                        half * 1024:(half + 1) * 1024],
                          wait=sl.take(), inc=[(ws, 16)])
                    p.do("gpsimd", lambda e_, sl=sl, kc=kc, half=half:
                         e_.tensor_copy(
                             w1b[:, kc * FH + half * 1024: kc * FH + (half + 1) * 1024],
                             sl.ap[:, :]),
                         wait=[ws] + (w1w if kc == 0 and half == 0 else []),
                         inc=[s_cv])
                    sl.mark(s_cv)
            for hc in range(32):
                sl = stg.next()
                ws = wld[(stg.i - 1) % 2]
                p.dma("sync", sl.ap[:, 0:D],
                      io["ffn_w2T"][e, hc * 128:(hc + 1) * 128, :],
                      wait=sl.take(), inc=[(ws, 16)])
                p.do("gpsimd", lambda e_, sl=sl, hc=hc: e_.tensor_copy(
                    w2b[:, hc * D:(hc + 1) * D], sl.ap[:, 0:D]),
                    wait=[ws] + (w2w if hc == 0 else []), inc=[s_cv])
                sl.mark(s_cv)
            wconv_done = s_cv.v

            for tt in range(ntiles):
                tok0 = tt * TILE
                lw = lnt.take()
                for c in range(8):
                    p.dma("sync", lnt.ap[:, c * TILE:(c + 1) * TILE],
                          io[ln_in][c * 128:(c + 1) * 128, tok0:tok0 + TILE],
                          wait=(lw if c == 0 else []), inc=[(s_ld, 16)])
                gtw = gtl.take()
                p.dma("sync", gtl.ap[:, :TILE],
                      io[gates_dram][:, tok0:tok0 + TILE],
                      wait=gtw, inc=[(s_ld, 16)])
                ld_done = s_ld.v
                p.do("tensor", lambda e_, e=e, tok0=tok0: e_.matmul(
                    P_gb.ap[:, :TILE], C.sel[:, e * 128:(e + 1) * 128],
                    gtl.ap[:, :TILE], start=True, stop=True),
                    wait=[(s_ld, ld_done)] + P_gb.take(), inc=[s_mm])
                gtl.mark(s_mm)
                gw = gbt.take()
                p.do("scalar", lambda e_: e_.activation(
                    gbt.ap[:, :TILE], P_gb.ap[:, :TILE], AF.Copy),
                    wait=[s_mm] + gw, inc=[s_act])
                P_gb.mark(s_act)
                for hc in range(32):
                    pm = R_pm.next()
                    for kc in range(8):
                        p.do("tensor", lambda e_, hc=hc, kc=kc, pm=pm, e=e:
                             e_.matmul(
                                 pm.ap[:, :TILE],
                                 w1b[:, kc * FH + hc * 128: kc * FH + hc * 128 + 128],
                                 lnt.ap[:, kc * TILE:(kc + 1) * TILE],
                                 start=(kc == 0), stop=(kc == 7)),
                             wait=[(s_ld, ld_done), (s_cv, wconv_done)] +
                                  (pm.take() if kc == 0 else []),
                             inc=([s_mm] if kc == 7 else []))
                    p.do("scalar", lambda e_, hc=hc, pm=pm, e=e: e_.activation(
                        hsb[:, hc * TILE:(hc + 1) * TILE], pm.ap[:, :TILE],
                        AF.Gelu_apprx_tanh, bias=b1t[:, e * 32 + hc:e * 32 + hc + 1]),
                        wait=[s_mm], inc=[s_act])
                    pm.mark(s_act)
                for oc in range(8):
                    pm = R_pm.next()
                    for hc in range(32):
                        p.do("tensor", lambda e_, oc=oc, hc=hc, pm=pm: e_.matmul(
                            pm.ap[:, :TILE],
                            w2b[:, hc * D + oc * 128: hc * D + oc * 128 + 128],
                            hsb[:, hc * TILE:(hc + 1) * TILE],
                            start=(hc == 0), stop=(hc == 31)),
                            wait=[(s_act, s_act.v)] + (pm.take() if hc == 0 else []),
                            inc=([s_mm] if hc == 31 else []))
                    tb = R_t.next()
                    if e == 0:
                        p.do("tensor", lambda e_, oc=oc, tok0=tok0: e_.matmul(
                            P_b2.ap[:, :TILE], b2rb[:, oc * 128:(oc + 1) * 128],
                            gtl.ap[:, :TILE], start=True, stop=True),
                            wait=[(s_ld, ld_done)] + P_b2.take(), inc=[s_mm])
                        gtl.mark(s_mm)
                        bw = b2e.take()
                        p.do("scalar", lambda e_: e_.activation(
                            b2e.ap[:, :TILE], P_b2.ap[:, :TILE], AF.Copy),
                            wait=[(s_mm, s_mm.v)] + bw, inc=[s_act])
                        P_b2.mark(s_act)
                        p.do("vector", lambda e_, pm=pm, tb=tb: e_.tensor_mul(
                            tb.ap[:, :TILE], pm.ap[:, :TILE], gbt.ap[:, :TILE]),
                            wait=[(s_mm, s_mm.v), (s_act, s_act.v)] + tb.take(),
                            inc=[s_dve])
                        pm.mark(s_dve)
                        p.do("vector", lambda e_, tb=tb: e_.tensor_add(
                            tb.ap[:, :TILE], tb.ap[:, :TILE], b2e.ap[:, :TILE]),
                            wait=[s_dve], inc=[s_dve])
                        b2e.mark(s_dve)
                        p.dma("gpsimd",
                              io[acc_out][oc * 128:(oc + 1) * 128, tok0:tok0 + TILE],
                              tb.ap[:, :TILE], wait=[(s_dve, s_dve.v)],
                              inc=[(tb.st(p), 16)])
                        tb.mark(tb.st(p))
                        last_acc[(tt, oc)] = (tb.st(p), tb.st(p).v)
                    else:
                        p.do("vector", lambda e_, pm=pm, tb=tb: e_.tensor_mul(
                            tb.ap[:, :TILE], pm.ap[:, :TILE], gbt.ap[:, :TILE]),
                            wait=[(s_mm, s_mm.v)] + tb.take(), inc=[s_dve])
                        pm.mark(s_dve)
                        p.dma("gpsimd",
                              io[acc_out][oc * 128:(oc + 1) * 128, tok0:tok0 + TILE],
                              tb.ap[:, :TILE],
                              wait=[(s_dve, s_dve.v), last_acc[(tt, oc)]],
                              accum_op=ALU.add, inc=[(tb.st(p), 16)])
                        tb.mark(tb.st(p))
                        last_acc[(tt, oc)] = (tb.st(p), tb.st(p).v)
                gbt.mark(s_dve)
                lnt.mark(s_mm)
            W1S.mark(s_mm)
            W2S.mark(s_mm)
        p.barrier(f"ffn_end_{tag}")



B_WNAMES = ("q_w1", "q_w2", "k_w1", "k_w2", "v_w", "g_w", "p_w")
B_BNAMES = ("q_b1", "q_b2", "k_b1", "k_b2", "v_b", "g_b", "p_b")


def emit_attn_b(p, ctx, C, cfg, io):
    from contextlib import ExitStack
    nc = p.nc
    Tl, TB, TILE = cfg["Tl"], cfg["TB"], cfg["TILE"]
    NB = Tl // TB
    ntiles = Tl // TILE
    JR = TB // 16
    NCH2 = TB // 128

    s_ld = p.sem("e_ld")
    s_mm = p.sem("e_mm")
    s_dve = p.sem("e_dve")
    s_act = p.sem("e_act")

    with ExitStack() as e1:
        pctx = e1
        wq1 = e1.enter_context(nc.sbuf_tensor("e_wq1", [128, 8 * D], BF16))
        wk1 = e1.enter_context(nc.sbuf_tensor("e_wk1", [128, 8 * D], BF16))
        wq2 = e1.enter_context(nc.sbuf_tensor("e_wq2", [128, 8 * D], BF16))
        wk2 = e1.enter_context(nc.sbuf_tensor("e_wk2", [128, 8 * D], BF16))
        wv = e1.enter_context(nc.sbuf_tensor("e_wv", [128, 8 * D], BF16))
        wg = e1.enter_context(nc.sbuf_tensor("e_wg", [128, 8 * D], BF16))
        emit_load_weights(p, e1, [(io["b_q_w1"], wq1, D, D), (io["b_k_w1"], wk1, D, D),
                                  (io["b_q_w2"], wq2, D, D), (io["b_k_w2"], wk2, D, D),
                                  (io["b_v_w"], wv, D, D), (io["b_g_w"], wg, D, D)],
                          "e1")
        bvec = e1.enter_context(nc.sbuf_tensor("e_bvec", [128, 6 * 8], F32))
        brow_q = e1.enter_context(nc.sbuf_tensor("e_browq", [1, D], F32))
        brow_k = e1.enter_context(nc.sbuf_tensor("e_browk", [1, D], F32))
        p.dma("sync", bvec[:, 0:32], io["pre_bvec"][:, :], inc=[(s_ld, 16)])
        p.dma("sync", brow_q[:, :], io["b_q_b2"].rearrange("(o d) -> o d", o=1),
              inc=[(s_ld, 16)])
        p.dma("sync", brow_k[:, :], io["b_k_b2"].rearrange("(o d) -> o d", o=1),
              inc=[(s_ld, 16)])
        p.barrier("e1_w")

        S_ac = Slot(e1.enter_context(nc.sbuf_tensor("e_ac", [128, 8 * TILE], F32)))
        S_x2 = Slot(e1.enter_context(nc.sbuf_tensor("e_x2", [128, 8 * TILE], F32)))
        S_xb = Slot(e1.enter_context(nc.sbuf_tensor("e_xb", [128, 8 * TILE], BF16)))
        S_q1 = Slot(e1.enter_context(nc.sbuf_tensor("e_q1", [128, 8 * TILE], BF16)))
        S_k1 = Slot(e1.enter_context(nc.sbuf_tensor("e_k1", [128, 8 * TILE], BF16)))
        R_sq = Rot([e1.enter_context(nc.sbuf_tensor(f"e_sq{i}", [128, TILE], BF16))
                    for i in range(8)])
        R_cv = Rot([e1.enter_context(nc.sbuf_tensor(f"e_cv{i}",
                                                    [128, max(TILE, 512)], BF16))
                    for i in range(3)])
        sp1 = Slot(e1.enter_context(nc.sbuf_tensor("e_sp1", [128, 512], F32)))
        sp2 = Slot(e1.enter_context(nc.sbuf_tensor("e_sp2", [128, 512], F32)))
        stat = e1.enter_context(nc.sbuf_tensor("e_stat", [1, 4 * TILE], F32))
        R_pm = Rot([pctx.enter_context(nc.psum_tensor(f"e_pm{i}", [128, 512], F32))
                    for i in range(2)])
        P_s1 = Slot(pctx.enter_context(nc.psum_tensor("e_ps1", [1, 512], F32)))
        P_s2 = Slot(pctx.enter_context(nc.psum_tensor("e_ps2", [1, 512], F32)))
        P_bc = Slot(pctx.enter_context(nc.psum_tensor("e_pb", [128, 512], F32)))
        ST = TILE

        for t in range(ntiles):
            tok0 = t * TILE
            acw = S_ac.take()
            for c in range(8):
                p.dma("sync", S_ac.ap[:, c * TILE:(c + 1) * TILE],
                      io["acc1_s"][c * 128:(c + 1) * 128, tok0:tok0 + TILE],
                      wait=(acw if c == 0 else []), inc=[(s_ld, 16)])
            x2w = S_x2.take()
            for c in range(8):
                p.dma("sync", S_x2.ap[:, c * TILE:(c + 1) * TILE],
                      io["xi"][c * 128:(c + 1) * 128, tok0:tok0 + TILE],
                      wait=(x2w if c == 0 else []), inc=[(s_ld, 16)])
            ld_done = s_ld.v
            for c in range(8):
                p.do("vector", lambda e, c=c: e.tensor_add(
                    S_x2.ap[:, c * TILE:(c + 1) * TILE],
                    S_x2.ap[:, c * TILE:(c + 1) * TILE],
                    S_ac.ap[:, c * TILE:(c + 1) * TILE]),
                    wait=[(s_ld, ld_done)], inc=[s_dve])
            S_ac.mark(s_dve)
            for c in range(8):
                p.dma("gpsimd", io["x2_s"][c * 128:(c + 1) * 128, tok0:tok0 + TILE],
                      S_x2.ap[:, c * TILE:(c + 1) * TILE],
                      wait=[(s_dve, s_dve.v)], inc=[(S_x2.st(p), 16)])
            S_x2.mark(S_x2.st(p))
            for c in range(8):
                p.do("tensor", lambda e, c=c: e.matmul(
                    P_s1.ap[:, :TILE], C.ones_col_f[:, :],
                    S_x2.ap[:, c * TILE:(c + 1) * TILE],
                    start=(c == 0), stop=(c == 7)),
                    wait=[(s_dve, s_dve.v)] + (P_s1.take() if c == 0 else []),
                    inc=([s_mm] if c == 7 else []))
            for c in range(8):
                sq = R_sq.next()
                p.do("scalar", lambda e, c=c, sq=sq: e.activation(
                    sq.ap[:, :], S_x2.ap[:, c * TILE:(c + 1) * TILE], AF.Square),
                    wait=[(s_dve, s_dve.v)] + sq.take(), inc=[s_act])
                p.do("tensor", lambda e, c=c, sq=sq: e.matmul(
                    P_s2.ap[:, :TILE], C.ones_col_b[:, :], sq.ap[:, :],
                    start=(c == 0), stop=(c == 7)),
                    wait=[s_act] + (P_s2.take() if c == 0 else []),
                    inc=([s_mm] if c == 7 else []))
            R_sq.mark_all(s_mm)
            p.do("vector", lambda e: e.tensor_scalar_mul(
                stat[:, 0 * ST:0 * ST + TILE], P_s1.ap[:, :TILE], 1.0 / D),
                wait=[s_mm], inc=[s_dve])
            P_s1.mark(s_dve)
            p.do("scalar", lambda e: e.activation(
                stat[:, 1 * ST:1 * ST + TILE], stat[:, 0 * ST:0 * ST + TILE],
                AF.Square), wait=[s_dve], inc=[s_act])
            p.do("vector", lambda e: e.scalar_tensor_tensor(
                stat[:, 2 * ST:2 * ST + TILE], P_s2.ap[:, :TILE], 1.0 / D,
                stat[:, 1 * ST:1 * ST + TILE], ALU.mult, ALU.subtract),
                wait=[s_mm, s_act], inc=[s_dve])
            P_s2.mark(s_dve)
            p.do("scalar", lambda e: e.activation(
                stat[:, 2 * ST:2 * ST + TILE], stat[:, 2 * ST:2 * ST + TILE],
                AF.Ln, bias=LN_EPS), wait=[s_dve], inc=[s_act])
            p.do("scalar", lambda e: e.activation(
                stat[:, 2 * ST:2 * ST + TILE], stat[:, 2 * ST:2 * ST + TILE],
                AF.Exp, scale=-0.5), wait=[s_act], inc=[s_act])
            p.do("tensor", lambda e: e.matmul(
                P_bc.ap[:, :TILE], C.ones_row_f[:, :], stat[:, 0 * ST:0 * ST + TILE],
                start=True, stop=True), wait=[(s_act, s_act.v)] + P_bc.take(),
                inc=[s_mm])
            xbw = S_xb.take()
            for c in range(8):
                p.do("vector", lambda e, c=c: e.tensor_sub(
                    S_xb.ap[:, c * TILE:(c + 1) * TILE],
                    S_x2.ap[:, c * TILE:(c + 1) * TILE], P_bc.ap[:, :TILE]),
                    wait=[s_mm] + (xbw if c == 0 else []), inc=[s_dve])
            P_bc.mark(s_dve)
            S_x2.mark(s_dve)
            p.do("tensor", lambda e: e.matmul(
                P_bc.ap[:, :TILE], C.ones_row_f[:, :], stat[:, 2 * ST:2 * ST + TILE],
                start=True, stop=True), wait=[(s_dve, s_dve.v)] + P_bc.take(),
                inc=[s_mm])
            for c in range(8):
                p.do("vector", lambda e, c=c: e.tensor_mul(
                    S_xb.ap[:, c * TILE:(c + 1) * TILE],
                    S_xb.ap[:, c * TILE:(c + 1) * TILE], P_bc.ap[:, :TILE]),
                    wait=[s_mm], inc=[s_dve])
            P_bc.mark(s_dve)
            xb_done = s_dve.v

            def proj_b(w_sb, bidx, keep_slot, dram, tag2):
                if keep_slot is not None:
                    kw = keep_slot.take()
                for mc in range(8):
                    pm = R_pm.next()
                    for kc in range(8):
                        p.do("tensor", lambda e, mc=mc, kc=kc, pm=pm, w_sb=w_sb:
                             e.matmul(
                                 pm.ap[:, :TILE],
                                 w_sb[:, kc * D + mc * 128: kc * D + mc * 128 + 128],
                                 S_xb.ap[:, kc * TILE:(kc + 1) * TILE],
                                 start=(kc == 0), stop=(kc == 7)),
                             wait=[(s_dve, xb_done)] + (pm.take() if kc == 0 else []),
                             inc=([s_mm] if kc == 7 else []))
                    if keep_slot is not None:
                        p.do("scalar", lambda e, mc=mc, pm=pm, bidx=bidx: e.activation(
                            keep_slot.ap[:, mc * TILE:(mc + 1) * TILE],
                            pm.ap[:, :TILE], AF.Gelu,
                            bias=bvec[:, bidx * 8 + mc: bidx * 8 + mc + 1]),
                            wait=[s_mm] + (kw if mc == 0 else []), inc=[s_act])
                        pm.mark(s_act)
                    else:
                        cv = R_cv.next()
                        p.do("scalar", lambda e, mc=mc, pm=pm, bidx=bidx, cv=cv:
                             e.activation(
                                 cv.ap[:, :TILE], pm.ap[:, :TILE], AF.Gelu,
                                 bias=bvec[:, bidx * 8 + mc: bidx * 8 + mc + 1]),
                             wait=[s_mm] + cv.take(), inc=[s_act])
                        pm.mark(s_act)
                        p.dma("gpsimd",
                              dram[mc * 128:(mc + 1) * 128, tok0:tok0 + TILE],
                              cv.ap[:, :TILE], wait=[s_act], inc=[(cv.st(p), 16)])
                        cv.mark(cv.st(p))

            proj_b(wq1, 0, S_q1, None, "q1")
            proj_b(wk1, 1, S_k1, None, "k1")
            proj_b(wv, 2, None, io["vb_s"], "v")
            proj_b(wg, 3, None, io["gb_s"], "g")
            q1_done = s_act.v

            def proj2(src_slot, w_sb, browx, dram):
                for tk in range(TILE // 128):
                    for nh in range(2):
                        pm = R_pm.next()
                        for kc in range(8):
                            p.do("tensor", lambda e, tk=tk, nh=nh, kc=kc, pm=pm,
                                 w_sb=w_sb, src_slot=src_slot: e.matmul(
                                     pm.ap[:, :512],
                                     src_slot.ap[:, kc * TILE + tk * 128:
                                                 kc * TILE + tk * 128 + 128],
                                     w_sb[:, kc * D + nh * 512: kc * D + (nh + 1) * 512],
                                     start=(kc == 0), stop=False),
                                 wait=[(s_act, q1_done)] + (pm.take() if kc == 0 else []),
                                 inc=[])
                        p.do("tensor", lambda e, nh=nh, pm=pm, browx=browx:
                             e.matmul(
                                 pm.ap[:, :512], C.ones_row_f[:, :],
                                 browx[0:1, nh * 512:(nh + 1) * 512],
                                 start=False, stop=True), inc=[s_mm])
                        spw = sp1.take()
                        p.do("scalar", lambda e, pm=pm: e.activation(
                            sp1.ap[:, :], pm.ap[:, :512], AF.Relu),
                            wait=[s_mm] + spw, inc=[s_act])
                        spw2 = sp2.take()
                        p.do("scalar", lambda e, pm=pm: e.activation(
                            sp2.ap[:, :], pm.ap[:, :512], AF.Abs),
                            wait=[(s_mm, s_mm.v)] + spw2, inc=[s_act])
                        pm.mark(s_act)
                        p.do("scalar", lambda e: e.activation(
                            sp2.ap[:, :], sp2.ap[:, :], AF.Exp, scale=-1.0),
                            wait=[s_act], inc=[s_act])
                        p.do("scalar", lambda e: e.activation(
                            sp2.ap[:, :], sp2.ap[:, :], AF.Ln, bias=1.0),
                            wait=[s_act], inc=[s_act])
                        cv = R_cv.next()
                        p.do("vector", lambda e, cv=cv: e.tensor_add(
                            cv.ap[:, :512], sp1.ap[:, :], sp2.ap[:, :]),
                            wait=[s_act] + cv.take(), inc=[s_dve])
                        sp1.mark(s_dve)
                        sp2.mark(s_dve)
                        p.dma("gpsimd",
                              dram[tok0 + tk * 128: tok0 + tk * 128 + 128,
                                   nh * 512:(nh + 1) * 512],
                              cv.ap[:, :512], wait=[(s_dve, s_dve.v)],
                              inc=[(cv.st(p), 16)])
                        cv.mark(cv.st(p))

            proj2(S_q1, wq2, brow_q, io["q2_s"])
            proj2(S_k1, wk2, brow_k, io["k2_s"])
            S_q1.mark(s_mm)
            S_k1.mark(s_mm)
            S_xb.mark(s_mm)
        p.barrier("e1_end")

    with ExitStack() as e2:
        pctx = e2
        wp = e2.enter_context(nc.sbuf_tensor("e_wp", [128, 8 * D], BF16))
        gwt = e2.enter_context(nc.sbuf_tensor("e_gwt", [128, 8 * NE], BF16))
        emit_load_weights(p, e2, [(io["b_p_w"], wp, D, D)], "e2")
        emit_load_weights(p, e2, [(io["gate_wT"], gwt, D, NE)], "e2b")
        pbv = e2.enter_context(nc.sbuf_tensor("e_pbv", [128, 8], F32))
        p.dma("sync", pbv[:, :], io["pre_pbv"][:, :], inc=[(s_ld, 16)])
        p.barrier("e2_w")

        S_q2 = Slot(e2.enter_context(nc.sbuf_tensor("e2_q2", [128, NCH2 * 64], BF16)))
        S_k2 = Slot(e2.enter_context(nc.sbuf_tensor("e2_k2", [128, NCH2 * 64], BF16)))
        S_vh = Slot(e2.enter_context(nc.sbuf_tensor("e2_vh", [64, TB], BF16)))
        S_gh = Slot(e2.enter_context(nc.sbuf_tensor("e2_gh", [64, TB], BF16)))
        S_ms = Slot(e2.enter_context(nc.sbuf_tensor("e2_ms", [64, TB], BF16)))
        S_mr = Slot(e2.enter_context(nc.sbuf_tensor("e2_mr", [128, 8 * JR], BF16)))
        A_sb = Slot(e2.enter_context(nc.sbuf_tensor("e2_A", [64, 64], BF16)))
        zv = e2.enter_context(nc.sbuf_tensor("e2_z", [64, 1], F32))
        S_xs = Slot(e2.enter_context(nc.sbuf_tensor("e2_xs", [128, 8 * JR], F32)))
        S_x3 = Slot(e2.enter_context(nc.sbuf_tensor("e2_x3", [128, 8 * JR], F32)))
        S_l3 = Slot(e2.enter_context(nc.sbuf_tensor("e2_l3", [128, 8 * JR], BF16)))
        R_sqE = Rot([e2.enter_context(nc.sbuf_tensor(f"e2_sq{i}", [128, JR], BF16))
                    for i in range(8)])
        statE = e2.enter_context(nc.sbuf_tensor("e2_statE", [1, 4 * JR], F32))
        lgtE = Slot(e2.enter_context(nc.sbuf_tensor("e2_lg", [NE, JR], F32)))
        R_pmE = Rot([pctx.enter_context(nc.psum_tensor(f"e2_pm{i}", [128, 512], F32))
                    for i in range(2)])
        P_A = Slot(pctx.enter_context(nc.psum_tensor("e2_pA", [64, 512], F32)))
        P_s1E = Slot(pctx.enter_context(nc.psum_tensor("e2_ps1", [1, 512], F32)))
        P_s2E = Slot(pctx.enter_context(nc.psum_tensor("e2_ps2", [1, 512], F32)))
        P_bcE = Slot(pctx.enter_context(nc.psum_tensor("e2_pb", [128, 512], F32)))
        P_lg = Slot(pctx.enter_context(nc.psum_tensor("e2_plg", [NE, 512], F32)))
        STE = JR

        for b in range(NB):
            for h in range(NH):
                tokb = b * TB
                q2w = S_q2.take()
                p.dma("sync",
                      S_q2.ap[:, :].rearrange("p (c f) -> p c f", c=NCH2),
                      io["q2_s"][tokb:tokb + TB, 64 * h:64 * (h + 1)]
                      .rearrange("(c p) f -> p c f", p=128),
                      wait=q2w, inc=[(s_ld, 16)])
                k2w = S_k2.take()
                p.dma("sync",
                      S_k2.ap[:, :].rearrange("p (c f) -> p c f", c=NCH2),
                      io["k2_s"][tokb:tokb + TB, 64 * h:64 * (h + 1)]
                      .rearrange("(c p) f -> p c f", p=128),
                      wait=k2w, inc=[(s_ld, 16)])
                p.dma("sync", S_vh.ap[:, :],
                      io["vb_s"][64 * h:64 * (h + 1), tokb:tokb + TB],
                      wait=S_vh.take(), inc=[(s_ld, 16)])
                p.dma("sync", S_gh.ap[:, :],
                      io["gb_s"][64 * h:64 * (h + 1), tokb:tokb + TB],
                      wait=S_gh.take(), inc=[(s_ld, 16)])
                xsw = S_xs.take()
                for c in range(8):
                    p.dma("sync", S_xs.ap[:, c * JR:(c + 1) * JR],
                          io["x2_s"][c * 128:(c + 1) * 128,
                                     tokb + h * JR: tokb + (h + 1) * JR],
                          wait=(xsw if c == 0 else []), inc=[(s_ld, 16)])
                ld_done = s_ld.v

                for cc in range(NCH2):
                    p.do("tensor", lambda e, cc=cc: e.matmul(
                        P_A.ap[:, 0:64], S_k2.ap[:, cc * 64:(cc + 1) * 64],
                        S_q2.ap[:, cc * 64:(cc + 1) * 64],
                        start=(cc == 0), stop=(cc == NCH2 - 1)),
                        wait=[(s_ld, ld_done)] + (P_A.take() if cc == 0 else []),
                        inc=([s_mm] if cc == NCH2 - 1 else []))
                S_q2.mark(s_mm)
                S_k2.mark(s_mm)
                aw = A_sb.take()
                p.do("vector", lambda e: e.tensor_copy(A_sb.ap[:, :],
                                                       P_A.ap[:, 0:64]),
                     wait=[s_mm] + aw, inc=[s_dve])
                p.do("tensor", lambda e: e.matmul(
                    P_A.ap[:, 64:65], A_sb.ap[:, :], C.ones_col_b[0:64, :],
                    start=True, stop=True), wait=[s_dve] + P_A.take(),
                    inc=[s_mm])
                p.do("vector", lambda e: e.tensor_scalar(
                    zv[:, :], P_A.ap[:, 64:65], BSCALE, float(D),
                    ALU.mult, ALU.add), wait=[s_mm], inc=[s_dve])
                P_A.mark(s_dve)
                p.do("vector", lambda e: e.reciprocal(zv[:, :], zv[:, :]),
                     wait=[s_dve], inc=[s_dve])
                msw = S_ms.take()
                for cc in range(NCH2):
                    pm = R_pmE.next()
                    p.do("tensor", lambda e, cc=cc, pm=pm: e.matmul(
                        pm.ap[0:64, 0:128], A_sb.ap[:, :],
                        S_vh.ap[:, cc * 128:(cc + 1) * 128],
                        start=True, stop=True),
                        wait=[(s_dve, s_dve.v)] + pm.take(), inc=[s_mm])
                    p.do("vector", lambda e, cc=cc, pm=pm: e.scalar_tensor_tensor(
                        S_ms.ap[:, cc * 128:(cc + 1) * 128], pm.ap[0:64, 0:128],
                        BSCALE, S_vh.ap[:, cc * 128:(cc + 1) * 128],
                        ALU.mult, ALU.add),
                        wait=[s_mm] + (msw if cc == 0 else []), inc=[s_dve])
                    p.do("vector", lambda e, cc=cc: e.tensor_scalar(
                        S_ms.ap[:, cc * 128:(cc + 1) * 128],
                        S_ms.ap[:, cc * 128:(cc + 1) * 128], zv[:, 0:1], None,
                        ALU.mult), wait=[s_dve], inc=[s_dve])
                    p.do("vector", lambda e, cc=cc: e.tensor_mul(
                        S_ms.ap[:, cc * 128:(cc + 1) * 128],
                        S_ms.ap[:, cc * 128:(cc + 1) * 128],
                        S_gh.ap[:, cc * 128:(cc + 1) * 128]),
                        wait=[s_dve], inc=[s_dve])
                    pm.mark(s_dve)
                S_vh.mark(s_dve)
                S_vh.mark(s_mm)
                S_gh.mark(s_dve)
                A_sb.mark(s_mm)
                mrw = S_mr.take()
                for dt in range(16):
                    p.do("vector", lambda e, dt=dt: e.tensor_copy(
                        S_mr.ap[64 * (dt % 2):64 * (dt % 2) + 64,
                                (dt // 2) * JR:(dt // 2) * JR + JR],
                        S_ms.ap[:, dt:dt + 16 * (JR - 1) + 1:16]),
                        wait=[(s_dve, s_dve.v)] + (mrw if dt == 0 else []),
                        inc=[s_dve])
                S_ms.mark(s_dve)
                x3w = S_x3.take()
                for mc in range(8):
                    pm = R_pmE.next()
                    for kc in range(8):
                        p.do("tensor", lambda e, mc=mc, kc=kc, pm=pm: e.matmul(
                            pm.ap[:, :JR],
                            wp[:, kc * D + mc * 128: kc * D + mc * 128 + 128],
                            S_mr.ap[:, kc * JR:(kc + 1) * JR],
                            start=(kc == 0), stop=(kc == 7)),
                            wait=[(s_dve, s_dve.v)] + (pm.take() if kc == 0 else []),
                            inc=([s_mm] if kc == 7 else []))
                    p.do("vector", lambda e, mc=mc, pm=pm: e.scalar_tensor_tensor(
                        S_x3.ap[:, mc * JR:(mc + 1) * JR], pm.ap[:, :JR],
                        pbv[:, mc:mc + 1], S_xs.ap[:, mc * JR:(mc + 1) * JR],
                        ALU.add, ALU.add),
                        wait=[s_mm] + (x3w if mc == 0 else []), inc=[s_dve])
                    pm.mark(s_dve)
                S_mr.mark(s_mm)
                S_xs.mark(s_dve)
                for c in range(8):
                    p.dma("gpsimd",
                          io["x3o"][c * 128:(c + 1) * 128,
                                    tokb + h * JR: tokb + (h + 1) * JR],
                          S_x3.ap[:, c * JR:(c + 1) * JR],
                          wait=[(s_dve, s_dve.v)], inc=[(S_x3.st(p), 16)])
                S_x3.mark(S_x3.st(p))
                for c in range(8):
                    p.do("tensor", lambda e, c=c: e.matmul(
                        P_s1E.ap[:, :JR], C.ones_col_f[:, :],
                        S_x3.ap[:, c * JR:(c + 1) * JR],
                        start=(c == 0), stop=(c == 7)),
                        wait=[(s_dve, s_dve.v)] + (P_s1E.take() if c == 0 else []),
                        inc=([s_mm] if c == 7 else []))
                for c in range(8):
                    sq = R_sqE.next()
                    p.do("scalar", lambda e, c=c, sq=sq: e.activation(
                        sq.ap[:, :], S_x3.ap[:, c * JR:(c + 1) * JR], AF.Square),
                        wait=[(s_dve, s_dve.v)] + sq.take(), inc=[s_act])
                    p.do("tensor", lambda e, c=c, sq=sq: e.matmul(
                        P_s2E.ap[:, :JR], C.ones_col_b[:, :], sq.ap[:, :],
                        start=(c == 0), stop=(c == 7)),
                        wait=[s_act] + (P_s2E.take() if c == 0 else []),
                        inc=([s_mm] if c == 7 else []))
                R_sqE.mark_all(s_mm)
                p.do("vector", lambda e: e.tensor_scalar_mul(
                    statE[:, 0 * STE:0 * STE + JR], P_s1E.ap[:, :JR], 1.0 / D),
                    wait=[s_mm], inc=[s_dve])
                P_s1E.mark(s_dve)
                p.do("scalar", lambda e: e.activation(
                    statE[:, 1 * STE:1 * STE + JR], statE[:, 0 * STE:0 * STE + JR],
                    AF.Square), wait=[s_dve], inc=[s_act])
                p.do("vector", lambda e: e.scalar_tensor_tensor(
                    statE[:, 2 * STE:2 * STE + JR], P_s2E.ap[:, :JR], 1.0 / D,
                    statE[:, 1 * STE:1 * STE + JR], ALU.mult, ALU.subtract),
                    wait=[s_mm, s_act], inc=[s_dve])
                P_s2E.mark(s_dve)
                p.do("scalar", lambda e: e.activation(
                    statE[:, 2 * STE:2 * STE + JR], statE[:, 2 * STE:2 * STE + JR],
                    AF.Ln, bias=LN_EPS), wait=[s_dve], inc=[s_act])
                p.do("scalar", lambda e: e.activation(
                    statE[:, 2 * STE:2 * STE + JR], statE[:, 2 * STE:2 * STE + JR],
                    AF.Exp, scale=-0.5), wait=[s_act], inc=[s_act])
                p.do("tensor", lambda e: e.matmul(
                    P_bcE.ap[:, :JR], C.ones_row_f[:, :], statE[:, 0 * STE:0 * STE + JR],
                    start=True, stop=True), wait=[(s_act, s_act.v)] + P_bcE.take(),
                    inc=[s_mm])
                l3w = S_l3.take()
                for c in range(8):
                    p.do("vector", lambda e, c=c: e.tensor_sub(
                        S_l3.ap[:, c * JR:(c + 1) * JR],
                        S_x3.ap[:, c * JR:(c + 1) * JR], P_bcE.ap[:, :JR]),
                        wait=[s_mm] + (l3w if c == 0 else []), inc=[s_dve])
                P_bcE.mark(s_dve)
                S_x3.mark(s_dve)
                p.do("tensor", lambda e: e.matmul(
                    P_bcE.ap[:, :JR], C.ones_row_f[:, :], statE[:, 2 * STE:2 * STE + JR],
                    start=True, stop=True), wait=[(s_dve, s_dve.v)] + P_bcE.take(),
                    inc=[s_mm])
                for c in range(8):
                    p.do("vector", lambda e, c=c: e.tensor_mul(
                        S_l3.ap[:, c * JR:(c + 1) * JR],
                        S_l3.ap[:, c * JR:(c + 1) * JR], P_bcE.ap[:, :JR]),
                        wait=[s_mm], inc=[s_dve])
                P_bcE.mark(s_dve)
                for c in range(8):
                    p.dma("gpsimd",
                          io["ln3o"][c * 128:(c + 1) * 128,
                                     tokb + h * JR: tokb + (h + 1) * JR],
                          S_l3.ap[:, c * JR:(c + 1) * JR],
                          wait=[(s_dve, s_dve.v)], inc=[(S_l3.st(p), 16)])
                S_l3.mark(S_l3.st(p))
                for kc in range(8):
                    p.do("tensor", lambda e, kc=kc: e.matmul(
                        P_lg.ap[:, :JR], gwt[:, kc * NE:(kc + 1) * NE],
                        S_l3.ap[:, kc * JR:(kc + 1) * JR],
                        start=(kc == 0), stop=(kc == 7)),
                        wait=[(s_dve, s_dve.v)] + (P_lg.take() if kc == 0 else []),
                        inc=([s_mm] if kc == 7 else []))
                S_l3.mark(s_mm)
                lgw = lgtE.take()
                p.do("scalar", lambda e: e.activation(
                    lgtE.ap[:, :JR], P_lg.ap[:, :JR], AF.Copy),
                    wait=[s_mm] + lgw, inc=[s_act])
                P_lg.mark(s_act)
                p.dma("gpsimd",
                      io["lgo2"][:, tokb + h * JR: tokb + (h + 1) * JR],
                      lgtE.ap[:, :JR], wait=[s_act], inc=[(lgtE.st(p), 16)])
                lgtE.mark(lgtE.st(p))
        p.barrier("e2_end")



def emit_final_add(p, ctx, C, cfg, io):
    from contextlib import ExitStack
    nc = p.nc
    Tl, TILE = cfg["Tl"], cfg["TILE"]
    s_ld = p.sem("z_ld")
    s_dve = p.sem("z_dve")
    with ExitStack() as zctx:
        S_a = Slot(zctx.enter_context(nc.sbuf_tensor("z_a", [128, 8 * TILE], F32)))
        S_x = Slot(zctx.enter_context(nc.sbuf_tensor("z_x", [128, 8 * TILE], F32)))
        for t in range(Tl // TILE):
            tok0 = t * TILE
            aw = S_a.take()
            for c in range(8):
                p.dma("sync", S_a.ap[:, c * TILE:(c + 1) * TILE],
                      io["acc2_s"][c * 128:(c + 1) * 128, tok0:tok0 + TILE],
                      wait=(aw if c == 0 else []), inc=[(s_ld, 16)])
            xw = S_x.take()
            for c in range(8):
                p.dma("sync", S_x.ap[:, c * TILE:(c + 1) * TILE],
                      io["xi"][c * 128:(c + 1) * 128, tok0:tok0 + TILE],
                      wait=(xw if c == 0 else []), inc=[(s_ld, 16)])
            ld_done = s_ld.v
            for c in range(8):
                p.do("vector", lambda e, c=c: e.tensor_add(
                    S_x.ap[:, c * TILE:(c + 1) * TILE],
                    S_x.ap[:, c * TILE:(c + 1) * TILE],
                    S_a.ap[:, c * TILE:(c + 1) * TILE]),
                    wait=[(s_ld, ld_done)], inc=[s_dve])
            S_a.mark(s_dve)
            for c in range(8):
                p.dma("gpsimd", io["out_fm"][c * 128:(c + 1) * 128, tok0:tok0 + TILE],
                      S_x.ap[:, c * TILE:(c + 1) * TILE],
                      wait=[(s_dve, s_dve.v)], inc=[(S_x.st(p), 16)])
            S_x.mark(S_x.st(p))
        p.barrier("z_end")



def _declare_common(nc, cfg, io):
    consts = host_consts()
    for k in CONST_NAMES:
        arr = consts[k]
        dt = BF16 if arr.dtype == BF else F32
        io[k] = nc.dram_tensor(k, list(arr.shape), dt, kind="ExternalInput").ap()
    io["c_kth"] = nc.dram_tensor("c_kth", [128, 1], F32, kind="ExternalInput").ap()


def build_l2(cfg):
    from contextlib import ExitStack
    Tl, NGC = cfg["Tl"], cfg["NGC"]
    nc = bass.Bass("TRN2", target_bir_lowering=False, debug=False)
    io = {}
    io["xi"] = nc.dram_tensor("xi", [D, Tl], F32, kind="ExternalInput").ap()
    io["lni"] = nc.dram_tensor("lni", [D, Tl], BF16, kind="ExternalInput").ap()
    io["glg"] = nc.dram_tensor("glg", [128, NGC], F32, kind="ExternalInput").ap()
    io["olg"] = nc.dram_tensor("olg", [NE, Tl], F32, kind="ExternalInput").ap()
    for n in B_WNAMES:
        io["b_" + n] = nc.dram_tensor("b_" + n, [D, D], F32, kind="ExternalInput").ap()
    for n in B_BNAMES:
        io["b_" + n] = nc.dram_tensor("b_" + n, [D], F32, kind="ExternalInput").ap()
    io["ffn_w1T"] = nc.dram_tensor("ffn_w1T", [NE, D, FH], F32, kind="ExternalInput").ap()
    io["ffn_w2T"] = nc.dram_tensor("ffn_w2T", [NE, FH, D], F32, kind="ExternalInput").ap()
    io["ffn_b1"] = nc.dram_tensor("ffn_b1", [NE, FH], F32, kind="ExternalInput").ap()
    io["ffn_b2"] = nc.dram_tensor("ffn_b2", [NE, D], F32, kind="ExternalInput").ap()
    io["gate_wT"] = nc.dram_tensor("gate_wT", [D, NE], F32, kind="ExternalInput").ap()
    io["pre_b1"] = nc.dram_tensor("pre_b1", [128, 256], F32, kind="ExternalInput").ap()
    io["pre_bvec"] = nc.dram_tensor("pre_bvec", [128, 32], F32, kind="ExternalInput").ap()
    io["pre_pbv"] = nc.dram_tensor("pre_pbv", [128, 8], F32, kind="ExternalInput").ap()
    _declare_common(nc, cfg, io)
    io["acc1_s"] = nc.dram_tensor("acc1_s", [D, Tl], F32).ap()
    io["x2_s"] = nc.dram_tensor("x2_s", [D, Tl], F32).ap()
    io["q2_s"] = nc.dram_tensor("q2_s", [Tl, D], BF16).ap()
    io["k2_s"] = nc.dram_tensor("k2_s", [Tl, D], BF16).ap()
    io["vb_s"] = nc.dram_tensor("vb_s", [D, Tl], BF16).ap()
    io["gb_s"] = nc.dram_tensor("gb_s", [D, Tl], BF16).ap()
    io["x3o"] = nc.dram_tensor("x3o", [D, Tl], F32, kind="ExternalOutput").ap()
    io["ln3o"] = nc.dram_tensor("ln3o", [D, Tl], BF16, kind="ExternalOutput").ap()
    io["lgo2"] = nc.dram_tensor("lgo2", [NE, Tl], F32, kind="ExternalOutput").ap()

    io["gates_s"] = nc.dram_tensor("gates_s", [NE, Tl], BF16).ap()
    ctx = ExitStack()
    p = Prog(nc, ctx)
    s_cld = p.sem("cld")
    C = Consts(p, ctx, io, s_cld)
    p.do("sync", lambda e: e.nop(), wait=[(s_cld, s_cld.v)])
    p.barrier("consts")
    with ctx:
        emit_kth(p, ctx, C, cfg, io, "glg", "olg", "gates_s")
        emit_ffn(p, ctx, C, cfg, io, "lni", "acc1_s", "gates_s", "f1")
        emit_attn_b(p, ctx, C, cfg, io)
        p.build_block()
    return nc


def build_l3(cfg):
    Tl, NGC = cfg["Tl"], cfg["NGC"]
    nc = bass.Bass("TRN2", target_bir_lowering=False, debug=False)
    io = {}
    io["xi"] = nc.dram_tensor("xi", [D, Tl], F32, kind="ExternalInput").ap()
    io["lni"] = nc.dram_tensor("lni", [D, Tl], BF16, kind="ExternalInput").ap()
    io["glg"] = nc.dram_tensor("glg", [128, NGC], F32, kind="ExternalInput").ap()
    io["olg"] = nc.dram_tensor("olg", [NE, Tl], F32, kind="ExternalInput").ap()
    io["ffn_w1T"] = nc.dram_tensor("ffn_w1T", [NE, D, FH], F32, kind="ExternalInput").ap()
    io["ffn_w2T"] = nc.dram_tensor("ffn_w2T", [NE, FH, D], F32, kind="ExternalInput").ap()
    io["ffn_b1"] = nc.dram_tensor("ffn_b1", [NE, FH], F32, kind="ExternalInput").ap()
    io["ffn_b2"] = nc.dram_tensor("ffn_b2", [NE, D], F32, kind="ExternalInput").ap()
    io["pre_b1"] = nc.dram_tensor("pre_b1", [128, 256], F32, kind="ExternalInput").ap()
    _declare_common(nc, cfg, io)
    io["acc2_s"] = nc.dram_tensor("acc2_s", [D, Tl], F32).ap()
    io["out_fm"] = nc.dram_tensor("out_fm", [D, Tl], F32, kind="ExternalOutput").ap()

    io["gates_s"] = nc.dram_tensor("gates_s", [NE, Tl], BF16).ap()
    ctx = ExitStack()
    p = Prog(nc, ctx)
    s_cld = p.sem("cld")
    C = Consts(p, ctx, io, s_cld)
    p.do("sync", lambda e: e.nop(), wait=[(s_cld, s_cld.v)])
    p.barrier("consts")
    with ctx:
        emit_kth(p, ctx, C, cfg, io, "glg", "olg", "gates_s")
        emit_ffn(p, ctx, C, cfg, io, "lni", "acc2_s", "gates_s", "f2")
        emit_final_add(p, ctx, C, cfg, io)
        p.build_block()
    return nc



FULL_CFG = {"Tl": 8192, "TB": 4096, "TBS": 2048, "TB2": 2048, "TILE": 512,
            "NGC": 4096,
            "kth": 24576}


def _pc(v):
    v = np.asarray(v, np.float32)
    return np.ascontiguousarray(v.reshape(-1, 128).T)


def _prearranged(inputs):
    pre = {}
    pre["pre_normw"] = _pc(inputs["L0_attn_a_norm_w"])
    cw = np.zeros((128, 128), np.float32)
    for i, n in enumerate(("conv_q_w", "conv_k_w")):
        w = np.asarray(inputs["L0_attn_a_" + n], np.float32)
        for c in range(8):
            cw[:, i * 32 + c * 4: i * 32 + (c + 1) * 4] = w[c * 128:(c + 1) * 128]
    wv = np.asarray(inputs["L0_attn_a_conv_v_w"], np.float32)
    for c in range(16):
        cw[:, 64 + c * 4: 64 + (c + 1) * 4] = wv[c * 128:(c + 1) * 128]
    pre["pre_convw"] = cw
    pre["pre_postw"] = _pc(inputs["L0_attn_a_post_norm_w"])
    b1 = np.asarray(inputs["L0_ffn_b1"], np.float32)
    pre["pre_b1"] = np.ascontiguousarray(
        b1.reshape(8, 32, 128).transpose(2, 0, 1).reshape(128, 256))
    pre["pre_bvec"] = np.ascontiguousarray(np.concatenate(
        [_pc(inputs["L0_attn_b_" + n]) for n in ("q_b1", "k_b1", "v_b", "g_b")],
        axis=1))
    pre["pre_pbv"] = _pc(inputs["L0_attn_b_p_b"])
    return pre


def kernel(**inputs):
    cfg = FULL_CFG
    Tl = cfg["Tl"]
    x = np.asarray(inputs["L0_attn_a_hidden_states"])
    consts = host_consts()
    ckth = np.full((128, 1), cfg["kth"] - 0.5, np.float32)

    xs = [np.ascontiguousarray(
        x[2 * c:2 * c + 2].reshape(Tl, D).T) for c in range(NCORES)]

    nc1 = build_l1(cfg)
    m_common = {}
    for n in A_WEIGHT_NAMES + A_VEC_NAMES:
        m_common[n] = np.ascontiguousarray(np.asarray(inputs[n], np.float32))
    m_common["gate_wT"] = np.ascontiguousarray(
        np.asarray(inputs["L0_ffn_gate_weight"]).T.astype(np.float32, copy=False))
    pre = _prearranged(inputs)
    m_common["pre_normw"] = pre["pre_normw"]
    m_common["pre_convw"] = pre["pre_convw"]
    m_common["pre_postw"] = pre["pre_postw"]
    m_common.update(consts)
    in_maps = [dict(m_common, x_fm=xs[c]) for c in range(NCORES)]
    r1 = run_bass_kernel_spmd(nc1, in_maps, core_ids=list(range(NCORES))).results

    glg1 = np.ascontiguousarray(
        np.concatenate([r["lgo"].reshape(-1) for r in r1]).reshape(128, cfg["NGC"]))

    nc2 = build_l2(cfg)
    m2 = {}
    for wn, bn in (("q_w1", "q_b1"), ("q_w2", "q_b2"), ("k_w1", "k_b1"),
                   ("k_w2", "k_b2"), ("v_w", "v_b"), ("g_w", "g_b"),
                   ("p_w", "p_b")):
        m2["b_" + wn] = np.ascontiguousarray(
            np.asarray(inputs["L0_attn_b_" + wn], np.float32).T)
        m2["b_" + bn] = np.ascontiguousarray(
            np.asarray(inputs["L0_attn_b_" + bn], np.float32))
    m2["ffn_w1T"] = np.ascontiguousarray(
        np.asarray(inputs["L0_ffn_w1"], np.float32).transpose(0, 2, 1))
    m2["ffn_w2T"] = np.ascontiguousarray(
        np.asarray(inputs["L0_ffn_w2"], np.float32).transpose(0, 2, 1))
    m2["ffn_b1"] = np.ascontiguousarray(np.asarray(inputs["L0_ffn_b1"], np.float32))
    m2["ffn_b2"] = np.ascontiguousarray(np.asarray(inputs["L0_ffn_b2"], np.float32))
    m2["gate_wT"] = m_common["gate_wT"]
    m2["pre_b1"] = pre["pre_b1"]
    m2["pre_bvec"] = pre["pre_bvec"]
    m2["pre_pbv"] = pre["pre_pbv"]
    m2["glg"] = glg1
    m2["c_kth"] = ckth
    m2.update(consts)
    in_maps2 = [dict(m2, xi=r1[c]["xo"], lni=r1[c]["lno"], olg=r1[c]["lgo"])
                for c in range(NCORES)]
    r2 = run_bass_kernel_spmd(nc2, in_maps2, core_ids=list(range(NCORES))).results

    glg2 = np.ascontiguousarray(
        np.concatenate([r["lgo2"].reshape(-1) for r in r2]).reshape(128, cfg["NGC"]))

    nc3 = build_l3(cfg)
    m3 = {k: m2[k] for k in ("ffn_w1T", "ffn_w2T", "ffn_b1", "ffn_b2", "c_kth",
                             "pre_b1")}
    m3["glg"] = glg2
    m3.update(consts)
    in_maps3 = [dict(m3, xi=r2[c]["x3o"], lni=r2[c]["ln3o"], olg=r2[c]["lgo2"])
                for c in range(NCORES)]
    r3 = run_bass_kernel_spmd(nc3, in_maps3, core_ids=list(range(NCORES))).results

    out = np.empty((16, 4096, D), np.float32)
    for c in range(NCORES):
        out[2 * c:2 * c + 2] = r3[c]["out_fm"].T.reshape(2, 4096, D)
    return out



class ScanStream:
    def __init__(self, p, ctx, pctx, C, cfg, sid, grall, brall):
        nc = p.nc
        TB = cfg.get("TBS", cfg["TB"])
        t = f"s{sid}"
        self.ld = p.sem(f"b{t}_ld")
        self.mm = p.sem(f"b{t}_mm")
        self.dv = p.sem(f"b{t}_dv")
        self.ac = p.sem(f"b{t}_ac")
        self.st_ = p.sem(f"b{t}_st")
        S = lambda name, shape, dt: Slot(ctx.enter_context(
            nc.sbuf_tensor(f"b{t}_{name}", shape, dt)))
        self.q = S("q", [64, TB], BF16)
        self.k = S("k", [64, TB], BF16)
        self.v = S("v", [128, TB], BF16)
        self.gs = S("gs", [128, TB], BF16)
        self.r0 = 32 * sid
        self.gr = Slot(grall)
        self.br = Slot(brall)
        self.Sst = [ctx.enter_context(nc.sbuf_tensor(f"b{t}_S{i}", [64, 128], F32))
                    for i in range(2)]
        self.S0b = ctx.enter_context(nc.sbuf_tensor(f"b{t}_S0b", [64, 128], BF16))
        self.gc = ctx.enter_context(nc.sbuf_tensor(f"b{t}_gc", [1, 128], F32))
        self.tcol = ctx.enter_context(nc.sbuf_tensor(f"b{t}_tcol", [128, 1], F32))
        self.btm = ctx.enter_context(nc.sbuf_tensor(f"b{t}_btm", [128, 1], F32))
        self.grb = ctx.enter_context(nc.sbuf_tensor(f"b{t}_grb", [128, 128], F32))
        self.gl64 = ctx.enter_context(nc.sbuf_tensor(f"b{t}_gl", [64, 1], F32))
        self.bex = ctx.enter_context(nc.sbuf_tensor(f"b{t}_bex", [64, 128], F32))
        self.dex = ctx.enter_context(nc.sbuf_tensor(f"b{t}_dex", [64, 128], F32))
        self.bc64 = ctx.enter_context(nc.sbuf_tensor(f"b{t}_bc", [64, 1], F32))
        self.f0m = ctx.enter_context(nc.sbuf_tensor(f"b{t}_f0m", [128, 128], F32))
        self.g2 = ctx.enter_context(nc.sbuf_tensor(f"b{t}_g2", [128, 128], F32))
        self.d1 = ctx.enter_context(nc.sbuf_tensor(f"b{t}_d1", [128, 128], F32))
        self.d2 = ctx.enter_context(nc.sbuf_tensor(f"b{t}_d2", [128, 128], F32))
        self.m1a = ctx.enter_context(nc.sbuf_tensor(f"b{t}_m1a", [128, 128], F32))
        self.Mb = ctx.enter_context(nc.sbuf_tensor(f"b{t}_M", [128, 128], BF16))
        self.Acur = [ctx.enter_context(nc.sbuf_tensor(f"b{t}_A{i}", [128, 128], BF16))
                     for i in range(2)]
        self.Bfac = [ctx.enter_context(nc.sbuf_tensor(f"b{t}_B{j}", [128, 128], BF16))
                     for j in range(7)]
        self.Yb = [ctx.enter_context(nc.sbuf_tensor(f"b{t}_Y{i}", [128, 128], BF16))
                   for i in range(2)]
        self.Kb = ctx.enter_context(nc.sbuf_tensor(f"b{t}_Kb", [64, 128], BF16))
        self.Qb = ctx.enter_context(nc.sbuf_tensor(f"b{t}_Qb", [64, 128], BF16))
        self.Kd = ctx.enter_context(nc.sbuf_tensor(f"b{t}_Kd", [64, 128], BF16))
        self.Kdt = ctx.enter_context(nc.sbuf_tensor(f"b{t}_Kdt", [128, 64], BF16))
        self.Vsb = ctx.enter_context(nc.sbuf_tensor(f"b{t}_Vsb", [128, 128], BF16))
        self.w2b = ctx.enter_context(nc.sbuf_tensor(f"b{t}_w2", [128, 128], BF16))
        self.r1 = ctx.enter_context(nc.sbuf_tensor(f"b{t}_r1", [128, 128], F32))
        self.sqo = ctx.enter_context(nc.sbuf_tensor(f"b{t}_sqo", [128, 128], F32))
        self.ms = ctx.enter_context(nc.sbuf_tensor(f"b{t}_ms", [128, 1], F32))
        self.o1 = ctx.enter_context(nc.sbuf_tensor(f"b{t}_o1", [128, 128], F32))
        self.o2 = ctx.enter_context(nc.sbuf_tensor(f"b{t}_o2", [128, 128], BF16))
        self.oo = S("oout", [128, 128], BF16)
        self.pA = Slot(pctx.enter_context(nc.psum_tensor(f"b{t}_pA", [128, 512], F32)))
        self.pAP = Slot(pctx.enter_context(nc.psum_tensor(f"b{t}_pAP", [128, 512], F32)))


def emit_scan_loads(p, C, cfg, io, s, b, h):
    TB = cfg.get("TBS", cfg["TB"])
    for (sl_, dram, r0, nr) in (
            (s.q, io["q_s"], 64 * h, 64), (s.k, io["k_s"], 64 * h, 64),
            (s.v, io["v_s"], 128 * h, 128), (s.gs, io["gsil_s"], 128 * h, 128)):
        p.dma("sync", sl_.ap[:, :], dram[r0:r0 + nr, b * TB:(b + 1) * TB],
              wait=sl_.take(), inc=[(s.ld, 16)])
    p.dma("sync", s.gr.ap[s.r0:s.r0 + 1, :], io["g_s"][h:h + 1, b * TB:(b + 1) * TB],
          wait=s.gr.take(), inc=[(s.ld, 16)])
    p.dma("sync", s.br.ap[s.r0:s.r0 + 1, :], io["beta_s"][h:h + 1, b * TB:(b + 1) * TB],
          wait=s.br.take(), inc=[(s.ld, 16)])
    s.ld_done = s.ld.v
    p.do("vector", lambda e: e.memset(s.Sst[0][:, :], 0.0),
         wait=[(s.mm, s.mm.v), (s.ac, s.ac.v), (s.dv, s.dv.v)], inc=[s.dv])


def emit_scan_chunk_steps(p, C, cfg, io, s, b, h, cc):
    TB = cfg.get("TBS", cfg["TB"])
    A_ = lambda fn, wait=(), inc=(): p.do("scalar", fn, wait=wait, inc=inc)
    V_ = lambda fn, wait=(), inc=(): p.do("vector", fn, wait=wait, inc=inc)
    T_ = lambda fn, wait=(), inc=(): p.do("tensor", fn, wait=wait, inc=inc)
    sl = slice(cc * CHUNK, (cc + 1) * CHUNK)
    cur, nxt = s.Sst[cc % 2], s.Sst[(cc + 1) % 2]
    st = []

    def S_scan():
        V_(lambda e: e.tensor_tensor_scan(
            s.gc[:, :], s.gr.ap[s.r0:s.r0 + 1, sl],
            io["zrow_sb"][s.r0:s.r0 + 1, :], 0.0, ALU.add, ALU.add),
            wait=[(s.ld, s.ld_done), (s.ac, s.ac.v)], inc=[s.dv])
    st.append(S_scan)

    def S_gcb():
        T_(lambda e: e.matmul(s.pA.ap[:, 384:512], s.gc[:, :],
                              C.ones_row_f[0:1, :], start=True, stop=True),
           wait=[s.dv] + s.pA.take(), inc=[s.mm])
        T_(lambda e: e.matmul(s.pA.ap[:, 256:384], C.ones_row_f[:, :], s.gc[:, :],
                              start=True, stop=True), inc=[s.mm])
        T_(lambda e: e.matmul(s.pAP.ap[:, 384:512], s.br.ap[s.r0:s.r0 + 1, sl],
                              C.ones_f128[s.r0:s.r0 + 1, :], start=True,
                              stop=True),
           wait=[(s.ld, s.ld_done)] + s.pAP.take(), inc=[s.mm])
    st.append(S_gcb)

    def S_evac():
        A_(lambda e: e.activation(s.tcol[:, :], s.pA.ap[:, 384:385], AF.Copy),
           wait=[s.mm], inc=[s.ac])
        A_(lambda e: e.activation(s.grb[:, :], s.pA.ap[:, 256:384], AF.Copy),
           wait=[(s.mm, s.mm.v)], inc=[s.ac])
        A_(lambda e: e.activation(s.gl64[:, :], s.pA.ap[0:64, 383:384], AF.Copy),
           wait=[(s.mm, s.mm.v)], inc=[s.ac])
        A_(lambda e: e.activation(s.bex[:, :], s.pA.ap[0:64, 256:384], AF.Exp),
           wait=[(s.mm, s.mm.v)], inc=[s.ac])
        A_(lambda e: e.activation(s.dex[:, :], s.pA.ap[0:64, 256:384], AF.Exp,
                                  scale=-1.0, bias=s.gl64[:, 0:1]),
           wait=[(s.ac, s.ac.v)], inc=[s.ac])
        A_(lambda e: e.activation(s.bc64[:, :], s.gl64[:, :], AF.Exp),
           wait=[(s.ac, s.ac.v)], inc=[s.ac])
        A_(lambda e: e.activation(s.btm[:, :], s.pAP.ap[:, 384:385], AF.Copy),
           wait=[(s.mm, s.mm.v)], inc=[s.ac])
        s.pAP.mark(s.ac)
        s.pA.mark(s.ac)
        s.evd = s.ac.v
    st.append(S_evac)

    def S_decay():
        V_(lambda e: e.tensor_scalar(s.f0m[:, :], s.grb[:, :], s.tcol[:, 0:1],
                                     0.0, ALU.subtract, ALU.max),
           wait=[(s.ac, s.evd)], inc=[s.dv])
        A_(lambda e: e.activation(s.d1[:, :], s.f0m[:, :], AF.Exp, scale=-1.0),
           wait=[s.dv], inc=[s.ac])
        V_(lambda e: e.scalar_tensor_tensor(s.g2[:, :], s.grb[:, :],
                                            s.tcol[:, 0:1], C.negmask[:, :],
                                            ALU.subtract, ALU.add),
           wait=[(s.ac, s.evd)], inc=[s.dv])
        A_(lambda e: e.activation(s.d2[:, :], s.g2[:, :], AF.Exp),
           wait=[(s.dv, s.dv.v)], inc=[s.ac])
    st.append(S_decay)

    def S_kkq():
        T_(lambda e: e.matmul(s.pA.ap[:, 0:128], s.k.ap[:, sl], s.k.ap[:, sl],
                              start=True, stop=True),
           wait=[(s.ac, s.evd), (s.ld, s.ld_done)], inc=[s.mm])
        T_(lambda e: e.matmul(s.pA.ap[:, 128:256], s.k.ap[:, sl], s.q.ap[:, sl],
                              start=True, stop=True), inc=[s.mm])
        s.kkq = s.mm.v
    st.append(S_kkq)

    def S_m():
        V_(lambda e: e.tensor_mul(s.m1a[:, :], s.pA.ap[:, 0:128], s.d1[:, :]),
           wait=[(s.mm, s.kkq), (s.ac, s.ac.v)], inc=[s.dv])
        V_(lambda e: e.scalar_tensor_tensor(s.Mb[:, :], s.m1a[:, :],
                                            s.btm[:, 0:1], C.slmask[:, :],
                                            ALU.mult, ALU.mult),
           wait=[s.dv], inc=[s.dv])
        T_(lambda e: e.matmul(s.pA.ap[:, 256:384], s.Mb[:, :], C.id128[:, :],
                              start=True, stop=True),
           wait=[s.dv, (s.ac, s.evd)], inc=[s.mm])
        V_(lambda e: e.tensor_copy(s.Bfac[0][:, :], s.pA.ap[:, 256:384]),
           wait=[s.mm], inc=[s.dv])
    st.append(S_m)

    for j_ in range(1, 7):
        def S_sq(j=j_):
            ca, na = s.Acur[(j + 1) % 2], s.Acur[j % 2]
            cb = s.Bfac[j - 1]
            a_src = s.Mb if j == 1 else ca
            T_(lambda e: e.matmul(s.pA.ap[:, 256:384], cb[:, :], a_src[:, :],
                                  start=True, stop=True),
               wait=[(s.dv, s.dv.v)], inc=[s.mm])
            T_(lambda e: e.matmul(s.pA.ap[:, 384:512], a_src[:, :], cb[:, :],
                                  start=True, stop=True), inc=[s.mm])
            V_(lambda e: e.tensor_copy(na[:, :], s.pA.ap[:, 256:384]),
               wait=[s.mm], inc=[s.dv])
            V_(lambda e: e.tensor_copy(s.Bfac[j][:, :], s.pA.ap[:, 384:512]),
               inc=[s.dv])
            if j == 6:
                s.pA.mark(s.dv)
        st.append(S_sq)

    def S_kb():
        V_(lambda e: e.tensor_mul(s.Kb[:, :], s.k.ap[:, sl], s.bex[:, :]),
           wait=[(s.ac, s.evd)], inc=[s.dv])
        V_(lambda e: e.tensor_mul(s.Qb[:, :], s.q.ap[:, sl], s.bex[:, :]),
           inc=[s.dv])
        A_(lambda e: e.activation(s.S0b[:, :], cur[:, :], AF.Copy),
           wait=[(s.dv, s.dv.v)], inc=[s.ac])
        T_(lambda e: e.matmul(s.pAP.ap[:, 0:128], s.Kb[:, :], s.S0b[:, :],
                              start=True, stop=True),
           wait=[s.dv, s.ac] + s.pAP.take(), inc=[s.mm])
    st.append(S_kb)

    def S_vt():
        T_(lambda e: e.matmul(s.pAP.ap[:, 128:256], s.v.ap[:, sl],
                              C.id128[:, :], start=True, stop=True),
           wait=[(s.ld, s.ld_done)], inc=[s.mm])
        V_(lambda e: e.tensor_copy(s.Vsb[:, :], s.pAP.ap[:, 128:256]),
           wait=[s.mm], inc=[s.dv])
        V_(lambda e: e.tensor_sub(s.r1[:, :], s.Vsb[:, :], s.pAP.ap[:, 0:128]),
           wait=[(s.mm, s.mm.v), s.dv], inc=[s.dv])
        s.pAP.mark(s.dv)
        V_(lambda e: e.tensor_scalar(s.Yb[0][:, :], s.r1[:, :], s.btm[:, 0:1],
                                     None, ALU.mult), wait=[s.dv], inc=[s.dv])
    st.append(S_vt)

    for ji_, j_ in enumerate(range(6, -1, -1)):
        def S_ap(ji=ji_, j=j_):
            yc, yn = s.Yb[ji % 2], s.Yb[(ji + 1) % 2]
            T_(lambda e: e.matmul(s.pAP.ap[:, 0:128], s.Bfac[j][:, :], yc[:, :],
                                  start=True, stop=True),
               wait=[(s.dv, s.dv.v)] + s.pAP.take(), inc=[s.mm])
            V_(lambda e: e.tensor_sub(yn[:, :], yc[:, :], s.pAP.ap[:, 0:128]),
               wait=[s.mm], inc=[s.dv])
            s.pAP.mark(s.dv)
        st.append(S_ap)

    def S_kd():
        U = s.Yb[1]
        V_(lambda e: e.tensor_mul(s.Kd[:, :], s.k.ap[:, sl], s.dex[:, :]),
           wait=[(s.ac, s.evd)], inc=[s.dv])
        T_(lambda e: e.matmul(s.pAP.ap[:, 0:64], s.Kd[:, :],
                              C.id128[0:64, 0:64], start=True, stop=True),
           wait=[(s.dv, s.dv.v)] + s.pAP.take(), inc=[s.mm])
        V_(lambda e: e.tensor_copy(s.Kdt[:, :], s.pAP.ap[:, 0:64]),
           wait=[s.mm], inc=[s.dv])
        s.pAP.mark(s.dv)
    st.append(S_kd)

    def S_su():
        U = s.Yb[1]
        T_(lambda e: e.matmul(s.pAP.ap[0:64, 256:384], s.Kdt[:, :], U[:, :],
                              start=True, stop=True),
           wait=[(s.dv, s.dv.v)] + s.pAP.take(), inc=[s.mm])
        V_(lambda e: e.scalar_tensor_tensor(
            nxt[:, :], cur[:, :], s.bc64[:, 0:1], s.pAP.ap[0:64, 256:384],
            ALU.mult, ALU.add), wait=[s.mm, (s.ac, s.ac.v)], inc=[s.dv])
        s.pAP.mark(s.dv)
    st.append(S_su)

    def S_omm():
        U = s.Yb[1]
        V_(lambda e: e.tensor_mul(s.w2b[:, :], s.pA.ap[:, 128:256], s.d2[:, :]),
           wait=[(s.ac, s.ac.v), (s.mm, s.kkq), (s.dv, s.dv.v)], inc=[s.dv])
        s.pA.mark(s.dv)
        T_(lambda e: e.matmul(s.pAP.ap[:, 128:256], s.w2b[:, :], U[:, :],
                              start=True, stop=False),
           wait=[(s.dv, s.dv.v)] + s.pAP.take(), inc=[s.mm])
        T_(lambda e: e.matmul(s.pAP.ap[:, 128:256], s.Qb[:, :], s.S0b[:, :],
                              start=False, stop=True), inc=[s.mm])
    st.append(S_omm)

    def S_onorm():
        A_(lambda e: e.activation(s.sqo[:, :], s.pAP.ap[:, 128:256], AF.Square),
           wait=[s.mm], inc=[s.ac])
        V_(lambda e: e.reduce_sum(s.ms[:, :], s.sqo[:, :], axis=AX),
           wait=[s.ac], inc=[s.dv])
        A_(lambda e: e.activation(s.ms[:, :], s.ms[:, :], AF.Ln, scale=1.0 / DV,
                                  bias=EPS), wait=[s.dv], inc=[s.ac])
        A_(lambda e: e.activation(s.ms[:, :], s.ms[:, :], AF.Exp, scale=-0.5),
           wait=[s.ac], inc=[s.ac])
    st.append(S_onorm)

    def S_oout():
        V_(lambda e: e.tensor_scalar(s.o1[:, :], s.pAP.ap[:, 128:256],
                                     s.ms[:, 0:1], None, ALU.mult),
           wait=[(s.ac, s.ac.v)], inc=[s.dv])
        s.pAP.mark(s.dv)
        V_(lambda e: e.tensor_mul(s.o2[:, :], s.o1[:, :], io["onwB_sb"][:, :]),
           wait=[s.dv], inc=[s.dv])
        T_(lambda e: e.matmul(s.pA.ap[:, 128:256], s.o2[:, :], C.id128[:, :],
                              start=True, stop=True),
           wait=[(s.dv, s.dv.v)], inc=[s.mm])
        oow = s.oo.take()
        V_(lambda e: e.tensor_mul(s.oo.ap[:, :], s.pA.ap[:, 128:256],
                                  s.gs.ap[:, sl]),
           wait=[s.mm] + oow, inc=[s.dv])
        s.pA.mark(s.dv)
        p.dma("gpsimd",
              io["ofm_s"][128 * h:128 * (h + 1),
                          b * TB + cc * CHUNK: b * TB + (cc + 1) * CHUNK],
              s.oo.ap[:, :], wait=[(s.dv, s.dv.v)], inc=[(s.oo.st(p), 16)])
        s.oo.mark(s.oo.st(p))
    st.append(S_oout)
    return st


def emit_scan_chunk(p, C, cfg, io, s, b, h, cc):
    for f in emit_scan_chunk_steps(p, C, cfg, io, s, b, h, cc):
        f()


def emit_stage_b2(p, ctx, C, cfg, io):
    from contextlib import ExitStack
    pctx = ExitStack()
    nc = p.nc
    Tl = cfg["Tl"]
    TB = cfg.get("TBS", cfg["TB"])
    NB = Tl // TB
    NCH = TB // CHUNK

    s_ld0 = p.sem("b2_ld0")
    s_mm0 = p.sem("b2_mm0")
    s_ac0 = p.sem("b2_ac0")
    s_dv0 = p.sem("b2_dv0")
    onw_row = ctx.enter_context(nc.sbuf_tensor("b2_onwr", [1, 128], F32))
    onwB = ctx.enter_context(nc.sbuf_tensor("b2_onwB", [128, 128], F32))
    zrow = ctx.enter_context(nc.sbuf_tensor("b2_zrow", [65, 128], F32))
    grall = ctx.enter_context(nc.sbuf_tensor("b2_grall", [65, TB], F32))
    brall = ctx.enter_context(nc.sbuf_tensor("b2_brall", [65, TB], F32))
    gr3 = ctx.enter_context(nc.sbuf_tensor("b2_gr3", [1, TB], F32))
    br3 = ctx.enter_context(nc.sbuf_tensor("b2_br3", [1, TB], F32))
    io["zrow_sb"] = zrow
    io["onwB_sb"] = onwB
    P_tmp = Slot(pctx.enter_context(nc.psum_tensor("b2_ptmp", [128, 512], F32)))
    p.dma("sync", onw_row[:, :], io["L0_attn_a_o_norm_w"].rearrange("(o d) -> o d", o=1),
          inc=[(s_ld0, 16)])
    p.do("vector", lambda e: e.memset(zrow[:, :], 0.0), inc=[s_dv0])
    p.do("tensor", lambda e: e.matmul(
        P_tmp.ap[:, 0:128], C.ones_row_f[:, :], onw_row[:, :],
        start=True, stop=True), wait=[s_ld0], inc=[s_mm0])
    p.do("scalar", lambda e: e.activation(onwB[:, :], P_tmp.ap[:, 0:128], AF.Copy),
         wait=[s_mm0], inc=[s_ac0])
    P_tmp.mark(s_ac0)
    p.barrier("b2_init")
    pctx.close()

    from contextlib import ExitStack as ES
    pctx2 = ES()
    NS = 4
    streams = [ScanStream(p, ctx, pctx2, C, cfg, i,
                          grall if i < 3 else gr3, brall if i < 3 else br3)
               for i in range(NS)]
    streams[3].r0 = 0
    pairs = [(b, h) for b in range(NB) for h in range(NH)]
    for i in range(0, len(pairs), NS):
        duo = pairs[i:i + NS]
        for k, (b, h) in enumerate(duo):
            emit_scan_loads(p, C, cfg, io, streams[k], b, h)
        for cc in range(NCH):
            lists = [emit_scan_chunk_steps(p, C, cfg, io, streams[k], b, h, cc)
                     for k, (b, h) in enumerate(duo)]
            for i in range(max(len(l) for l in lists)):
                for l in lists:
                    if i < len(l):
                        l[i]()
        for k in range(len(duo)):
            s = streams[k]
            for sl_ in (s.q, s.k, s.v, s.gs, s.gr, s.br):
                sl_.mark(s.dv)
                sl_.mark(s.mm)
    p.barrier("b2_end")
    pctx2.close()



class ScanStream3:
    def __init__(self, p, ctx, pctx, C, cfg, sid):
        nc = p.nc
        TB = cfg["TB"]
        t = f"v{sid}"
        self.ld = p.sem(f"b{t}_ld")
        self.mm = p.sem(f"b{t}_mm")
        self.dv = p.sem(f"b{t}_dv")
        self.ac = p.sem(f"b{t}_ac")
        S = lambda name, shape, dt: Slot(ctx.enter_context(
            nc.sbuf_tensor(f"b{t}_{name}", shape, dt)))
        T = lambda name, shape, dt: ctx.enter_context(
            nc.sbuf_tensor(f"b{t}_{name}", shape, dt))
        TB2 = cfg["TB2"]
        self.qk = [S("q", [128, TB2], BF16), S("k", [128, TB2], BF16)]
        self.v = [S("v0", [128, TB2], BF16), S("v1", [128, TB2], BF16)]
        self.gs = [S("gs0", [128, TB2], BF16), S("gs1", [128, TB2], BF16)]
        self.gbg = S("gbg", [64, TB2], F32)
        self.gbb = S("gbb", [64, TB2], F32)
        self.Sst = [T(f"S{i}", [128, 128], F32) for i in range(2)]
        self.S0b = T("S0b", [128, 128], BF16)
        self.gc = [T("gc0", [1, 128], F32), T("gc1", [1, 128], F32)]
        self.bt = [T("bt0", [1, 128], F32), T("bt1", [1, 128], F32)]
        self.grb = T("grb", [128, 256], F32)
        self.gcol = T("gcol", [128, 256], F32)
        self.bcol = T("bcol", [128, 256], F32)
        self.glP = T("glP", [128, 1], F32)
        self.bexP = T("bexP", [128, 128], F32)
        self.dexP = T("dexP", [128, 128], F32)
        self.bcP = T("bcP", [128, 1], F32)
        self.t1 = T("t1", [128, 256], F32)
        self.md = T("md", [128, 512], F32)
        self.dd = T("dd", [128, 512], F32)
        self.m1a = T("m1a", [128, 256], F32)
        self.m1b = T("m1b", [128, 256], F32)
        self.Mb = T("Mb", [128, 256], BF16)
        self.Bf0 = T("Bf0", [128, 256], BF16)
        self.LVL = [T(f"L{j}", [128, 512], BF16) for j in range(1, 7)]
        self.Yb = [T(f"Y{i}", [128, 256], BF16) for i in range(2)]
        self.KbP = T("KbP", [128, 128], BF16)
        self.QbP = T("QbP", [128, 128], BF16)
        self.KdP = T("KdP", [128, 128], BF16)
        self.KdtP = T("KdtP", [128, 128], BF16)
        self.VsbP = T("VsbP", [128, 256], BF16)
        self.w2bP = T("w2bP", [128, 256], BF16)
        self.r1 = T("r1", [128, 256], F32)
        self.sqo = T("sqo", [128, 256], F32)
        self.ms = T("ms", [128, 2], F32)
        self.o1 = T("o1", [128, 256], F32)
        self.o2 = T("o2", [128, 256], BF16)
        self.oo = [S("oo0", [128, 128], BF16), S("oo1", [128, 128], BF16)]
        self.pKK = Slot(pctx.enter_context(nc.psum_tensor(f"b{t}_pKK", [128, 512], F32)))
        self.pW = Slot(pctx.enter_context(nc.psum_tensor(f"b{t}_pW", [128, 512], F32)))
        self.pAP = Slot(pctx.enter_context(nc.psum_tensor(f"b{t}_pAP", [128, 512], F32)))
        self.pF = Slot(pctx.enter_context(nc.psum_tensor(f"b{t}_pF", [128, 1024], BF16)))


def emit_scan3_loads(p, C, cfg, io, s, b, hp, seg, first):
    TB, TB2 = cfg["TB"], cfg["TB2"]
    h0 = 2 * hp
    c0 = b * TB + seg * TB2
    for i, (sl_, dram, r0) in enumerate((
            (s.qk[0], io["q_s"], 128 * hp), (s.qk[1], io["k_s"], 128 * hp),
            (s.v[0], io["v_s"], 128 * h0), (s.v[1], io["v_s"], 128 * (h0 + 1)),
            (s.gs[0], io["gsil_s"], 128 * h0), (s.gs[1], io["gsil_s"], 128 * (h0 + 1)))):
        p.dma("sync", sl_.ap[:, :], dram[r0:r0 + 128, c0:c0 + TB2],
              wait=sl_.take(), inc=[(s.ld, 16)])
    gw = s.gbg.take()
    bw = s.gbb.take()
    for i in range(2):
        p.dma("sync", s.gbg.ap[32 * i:32 * i + 1, :],
              io["g_s"][h0 + i:h0 + i + 1, c0:c0 + TB2],
              wait=(gw if i == 0 else []), inc=[(s.ld, 16)])
        p.dma("sync", s.gbb.ap[32 * i:32 * i + 1, :],
              io["beta_s"][h0 + i:h0 + i + 1, c0:c0 + TB2],
              wait=(bw if i == 0 else []), inc=[(s.ld, 16)])
    s.ld_done = s.ld.v
    if first:
        p.do("vector", lambda e: e.memset(s.Sst[0][:, :], 0.0),
             wait=[(s.mm, s.mm.v), (s.ac, s.ac.v), (s.dv, s.dv.v)], inc=[s.dv])


def emit_scan3_chunk(p, C, cfg, io, s, b, hp, cc):
    TB = cfg["TB"]
    h0 = 2 * hp
    A_ = lambda fn, wait=(), inc=(): p.do("scalar", fn, wait=wait, inc=inc)
    V_ = lambda fn, wait=(), inc=(): p.do("vector", fn, wait=wait, inc=inc)
    T_ = lambda fn, wait=(), inc=(): p.do("tensor", fn, wait=wait, inc=inc)
    import os as _os
    BCUT3 = float(_os.environ.get("BCUT3", "99"))
    NCHS = cfg["TB2"] // CHUNK
    sl = slice((cc % NCHS) * CHUNK, (cc % NCHS + 1) * CHUNK)
    cur, nxt = s.Sst[cc % 2], s.Sst[(cc + 1) % 2]
    H = [slice(0, 128), slice(128, 256)]
    R_ = [slice(0, 64), slice(64, 128)]

    for i in range(2):
        V_(lambda e, i=i: e.tensor_tensor_scan(
            s.gc[i][:, :], s.gbg.ap[32 * i:32 * i + 1, sl],
            io["zrow_sb"][32 * i:32 * i + 1, :],
            0.0, ALU.add, ALU.add),
            wait=[(s.ld, s.ld_done), (s.ac, s.ac.v)], inc=[s.dv])
    if BCUT3 < 1.2:
        return
    for i in range(2):
        T_(lambda e, i=i: e.matmul(s.pW.ap[:, i * 128:(i + 1) * 128],
                                   C.ones_row_f[:, :], s.gc[i][:, :],
                                   start=True, stop=True),
           wait=[(s.dv, s.dv.v)] + (s.pW.take() if i == 0 else []), inc=[s.mm])
    A_(lambda e: e.activation(s.grb[:, :], s.pW.ap[:, 0:256], AF.Copy),
       wait=[s.mm], inc=[s.ac])
    s.pW.mark(s.ac)
    if BCUT3 < 1.4:
        return
    for i in range(2):
        T_(lambda e, i=i: e.matmul(s.pW.ap[:, 256 + i * 128:256 + (i + 1) * 128],
                                   s.gc[i][:, :], C.ones_row_f[0:1, :],
                                   start=True, stop=True),
           wait=(s.pW.take() if i == 0 else []), inc=[s.mm])
    A_(lambda e: e.activation(s.gcol[:, :], s.pW.ap[:, 256:512], AF.Copy),
       wait=[s.mm], inc=[s.ac])
    s.pW.mark(s.ac)
    if BCUT3 < 1.6:
        return
    for i in range(2):
        V_(lambda e, i=i, sl=sl: e.tensor_copy(
            s.bt[i][:, :], s.gbb.ap[32 * i:32 * i + 1, sl]),
           wait=[(s.ld, s.ld_done)], inc=[s.dv])
    for i in range(2):
        T_(lambda e, i=i: e.matmul(
            s.pW.ap[:, 256 + i * 128:256 + (i + 1) * 128],
            s.bt[i][:, :], C.ones_row_f[0:1, :],
            start=True, stop=True),
           wait=[(s.dv, s.dv.v)] + (s.pW.take() if i == 0 else []), inc=[s.mm])
    A_(lambda e: e.activation(s.bcol[:, :], s.pW.ap[:, 256:512], AF.Copy),
       wait=[s.mm], inc=[s.ac])
    s.pW.mark(s.ac)
    if BCUT3 < 2:
        return
    for i in range(2):
        A_(lambda e, i=i: e.activation(s.glP[R_[i], :],
                                       s.grb[R_[i], H[i].start + 127:H[i].start + 128],
                                       AF.Copy),
           wait=[(s.ac, s.ac.v)], inc=[s.ac])
        if BCUT3 < 2.2:
            continue
        A_(lambda e, i=i: e.activation(s.bexP[R_[i], :], s.grb[R_[i], H[i]], AF.Exp),
           wait=[(s.ac, s.ac.v)], inc=[s.ac])
        if BCUT3 < 2.4:
            continue
        A_(lambda e, i=i: e.activation(s.dexP[R_[i], :], s.grb[R_[i], H[i]], AF.Exp,
                                       scale=-1.0, bias=s.glP[R_[i], 0:1]),
           wait=[(s.ac, s.ac.v)], inc=[s.ac])
    if BCUT3 < 2.6:
        return
    A_(lambda e: e.activation(s.bcP[:, :], s.glP[:, :], AF.Exp),
       wait=[(s.ac, s.ac.v)], inc=[s.ac])
    if BCUT3 < 2.8:
        return
    V_(lambda e: e.tensor_sub(s.t1[:, :], s.grb[:, :], s.gcol[:, :]),
       wait=[(s.ac, s.ac.v)], inc=[s.dv])
    if BCUT3 < 2.85:
        return
    V_(lambda e: e.tensor_scalar(s.md[:, 0:256], s.t1[:, :], -1.0, 0.0,
                                 ALU.mult, ALU.min),
       wait=[s.dv], inc=[s.dv])
    if BCUT3 < 2.9:
        return
    V_(lambda e: e.tensor_add(s.md[:, 256:512], s.t1[:, :], C.negmask2[:, :]),
       wait=[(s.dv, s.dv.v)], inc=[s.dv])
    if BCUT3 < 2.95:
        return
    A_(lambda e: e.activation(s.dd[:, :], s.md[:, :], AF.Exp),
       wait=[(s.dv, s.dv.v)], inc=[s.ac])
    if BCUT3 < 3:
        return
    for i in range(2):
        T_(lambda e, i=i, sl=sl: e.matmul(
            s.pKK.ap[:, i * 128:(i + 1) * 128],
            s.qk[1].ap[R_[i], sl], s.qk[1].ap[R_[i], sl],
            start=True, stop=True),
           wait=[(s.ld, s.ld_done)] + (s.pKK.take() if i == 0 else []), inc=[s.mm])
        T_(lambda e, i=i, sl=sl: e.matmul(
            s.pKK.ap[:, 256 + i * 128:256 + (i + 1) * 128],
            s.qk[1].ap[R_[i], sl], s.qk[0].ap[R_[i], sl],
            start=True, stop=True), inc=[s.mm])
    kkq = s.mm.v
    if BCUT3 < 3.3:
        return
    V_(lambda e: e.tensor_mul(s.m1a[:, :], s.pKK.ap[:, 0:256], s.dd[:, 0:256]),
       wait=[(s.mm, kkq), (s.ac, s.ac.v)], inc=[s.dv])
    V_(lambda e: e.tensor_mul(s.m1b[:, :], s.m1a[:, :], s.bcol[:, :]),
       wait=[s.dv], inc=[s.dv])
    V_(lambda e: e.tensor_mul(s.Mb[:, :], s.m1b[:, :], C.slmask2[:, :]),
       wait=[s.dv], inc=[s.dv])
    for i in range(2):
        T_(lambda e, i=i: e.transpose(s.pF.ap[:, i * 128:(i + 1) * 128],
                                      s.Mb[:, H[i]], C.id128[:, :]),
           wait=[(s.dv, s.dv.v)] + (s.pF.take() if i == 0 else []), inc=[s.mm])
    V_(lambda e: e.tensor_copy(s.Bf0[:, :], s.pF.ap[:, 0:256]),
       wait=[s.mm], inc=[s.dv])
    s.pF.mark(s.dv)
    if BCUT3 < 3.6:
        return
    for j in range(1, 7):
        Aprev = s.Mb if j == 1 else s.LVL[j - 2]
        aof = 0 if j == 1 else 0
        Bprev = s.Bf0 if j == 1 else None
        for i in range(2):
            if j == 1:
                asrc = s.Mb[:, H[i]]
                bsrc = s.Bf0[:, H[i]]
            else:
                asrc = s.LVL[j - 2][:, i * 128:(i + 1) * 128]
                bsrc = s.LVL[j - 2][:, 256 + i * 128:256 + (i + 1) * 128]
            T_(lambda e, i=i, asrc=asrc, bsrc=bsrc: e.matmul(
                s.pW.ap[:, i * 128:(i + 1) * 128], bsrc, asrc,
                start=True, stop=True),
               wait=[(s.dv, s.dv.v)] + (s.pW.take() if i == 0 else []), inc=[s.mm])
            T_(lambda e, i=i, asrc=asrc, bsrc=bsrc: e.matmul(
                s.pW.ap[:, 256 + i * 128:256 + (i + 1) * 128], asrc, bsrc,
                start=True, stop=True), inc=[s.mm])
        V_(lambda e, j=j: e.tensor_copy(s.LVL[j - 1][:, :], s.pW.ap[:, :]),
           wait=[s.mm], inc=[s.dv])
        s.pW.mark(s.dv)
    if BCUT3 < 4:
        return
    V_(lambda e, sl=sl: e.tensor_mul(s.KbP[:, :], s.qk[1].ap[:, sl], s.bexP[:, :]),
       wait=[(s.ac, s.ac.v)], inc=[s.dv])
    V_(lambda e, sl=sl: e.tensor_mul(s.QbP[:, :], s.qk[0].ap[:, sl], s.bexP[:, :]),
       inc=[s.dv])
    A_(lambda e, cur=cur: e.activation(s.S0b[:, :], cur[:, :], AF.Copy),
       wait=[(s.dv, s.dv.v)], inc=[s.ac])
    for i in range(2):
        T_(lambda e, i=i: e.matmul(s.pAP.ap[:, i * 128:(i + 1) * 128],
                                   s.KbP[R_[i], :], s.S0b[R_[i], :],
                                   start=True, stop=True),
           wait=[s.dv, s.ac] + (s.pAP.take() if i == 0 else []), inc=[s.mm])
    for i in range(2):
        T_(lambda e, i=i, sl=sl: e.transpose(
            s.pF.ap[:, 256 + i * 128:256 + (i + 1) * 128], s.v[i].ap[:, sl],
            C.id128[:, :]),
           wait=s.pF.take() if i == 0 else [], inc=[s.mm])
    V_(lambda e: e.tensor_copy(s.VsbP[:, :], s.pF.ap[:, 256:512]),
       wait=[s.mm], inc=[s.dv])
    s.pF.mark(s.dv)
    V_(lambda e: e.tensor_sub(s.r1[:, :], s.VsbP[:, :], s.pAP.ap[:, 0:256]),
       wait=[(s.mm, s.mm.v), s.dv], inc=[s.dv])
    s.pAP.mark(s.dv)
    V_(lambda e: e.tensor_mul(s.Yb[0][:, :], s.r1[:, :], s.bcol[:, :]),
       wait=[s.dv], inc=[s.dv])
    if BCUT3 < 5:
        return
    for ji, j in enumerate(range(6, -1, -1)):
        yc, yn = s.Yb[ji % 2], s.Yb[(ji + 1) % 2]
        Bsl = (s.Bf0 if j == 0 else s.LVL[j - 1])
        bof = 0 if j == 0 else 256
        for i in range(2):
            T_(lambda e, i=i, yc=yc, Bsl=Bsl, bof=bof: e.matmul(
                s.pAP.ap[:, i * 128:(i + 1) * 128],
                Bsl[:, bof + i * 128:bof + (i + 1) * 128], yc[:, H[i]],
                start=True, stop=True),
               wait=[(s.dv, s.dv.v)] + (s.pAP.take() if i == 0 else []), inc=[s.mm])
        V_(lambda e, yc=yc, yn=yn: e.tensor_sub(yn[:, :], yc[:, :],
                                                s.pAP.ap[:, 0:256]),
           wait=[s.mm], inc=[s.dv])
        s.pAP.mark(s.dv)
    U = s.Yb[1]
    V_(lambda e: e.tensor_mul(s.w2bP[:, :], s.pKK.ap[:, 256:512], s.dd[:, 256:512]),
       wait=[(s.ac, s.ac.v), (s.mm, kkq)], inc=[s.dv])
    s.pKK.mark(s.dv)
    for i in range(2):
        T_(lambda e, i=i: e.matmul(s.pAP.ap[:, i * 128:(i + 1) * 128],
                                   s.w2bP[:, H[i]], U[:, H[i]],
                                   start=True, stop=False),
           wait=[(s.dv, s.dv.v)] + (s.pAP.take() if i == 0 else []), inc=[s.mm])
        T_(lambda e, i=i: e.matmul(s.pAP.ap[:, i * 128:(i + 1) * 128],
                                   s.QbP[R_[i], :], s.S0b[R_[i], :],
                                   start=False, stop=True), inc=[s.mm])
    A_(lambda e: e.activation(s.sqo[:, :], s.pAP.ap[:, 0:256], AF.Square),
       wait=[s.mm], inc=[s.ac])
    V_(lambda e: e.tensor_reduce(
        s.ms[:, :], s.sqo[:, :].rearrange("p (h f) -> p h f", h=2), AX, ALU.add),
       wait=[s.ac], inc=[s.dv])
    A_(lambda e: e.activation(s.ms[:, :], s.ms[:, :], AF.Ln, scale=1.0 / DV,
                              bias=EPS), wait=[s.dv], inc=[s.ac])
    A_(lambda e: e.activation(s.ms[:, :], s.ms[:, :], AF.Exp, scale=-0.5),
       wait=[s.ac], inc=[s.ac])
    for i in range(2):
        V_(lambda e, i=i: e.tensor_scalar(s.o1[:, H[i]], s.pAP.ap[:, i * 128:(i + 1) * 128],
                                          s.ms[:, i:i + 1], None, ALU.mult),
           wait=[(s.ac, s.ac.v)], inc=[s.dv])
    s.pAP.mark(s.dv)
    V_(lambda e: e.tensor_mul(s.o2[:, :], s.o1[:, :], io["onw2_sb"][:, :]),
       wait=[(s.dv, s.dv.v)], inc=[s.dv])
    for i in range(2):
        T_(lambda e, i=i: e.transpose(s.pF.ap[:, i * 128:(i + 1) * 128],
                                      s.o2[:, H[i]], C.id128[:, :]),
           wait=[(s.dv, s.dv.v)] + (s.pF.take() if i == 0 else []), inc=[s.mm])
    for i in range(2):
        oow = s.oo[i].take()
        V_(lambda e, i=i, sl=sl: e.tensor_mul(
            s.oo[i].ap[:, :], s.pF.ap[:, i * 128:(i + 1) * 128], s.gs[i].ap[:, sl]),
           wait=[s.mm] + oow, inc=[s.dv])
    s.pF.mark(s.dv)
    for i in range(2):
        p.dma("gpsimd",
              io["ofm_s"][128 * (h0 + i):128 * (h0 + i + 1),
                          b * TB + cc * CHUNK: b * TB + (cc + 1) * CHUNK],
              s.oo[i].ap[:, :], wait=[(s.dv, s.dv.v)], inc=[(s.oo[i].st(p), 16)])
        s.oo[i].mark(s.oo[i].st(p))
    if BCUT3 < 6:
        return
    V_(lambda e, sl=sl: e.tensor_mul(s.KdP[:, :], s.qk[1].ap[:, sl], s.dexP[:, :]),
       wait=[(s.ac, s.ac.v)], inc=[s.dv])
    for i in range(2):
        T_(lambda e, i=i: e.transpose(s.pF.ap[:, 512 + i * 64:512 + (i + 1) * 64],
                                      s.KdP[R_[i], :], C.id128[R_[i], R_[i]]),
           wait=[(s.dv, s.dv.v)] + (s.pF.take() if i == 0 else []), inc=[s.mm])
    V_(lambda e: e.tensor_copy(s.KdtP[:, :], s.pF.ap[:, 512:640]),
       wait=[s.mm], inc=[s.dv])
    s.pF.mark(s.dv)
    for i in range(2):
        T_(lambda e, i=i: e.matmul(s.pAP.ap[0:64, i * 128:(i + 1) * 128],
                                   s.KdtP[:, i * 64:(i + 1) * 64], U[:, H[i]],
                                   start=True, stop=True),
           wait=[(s.dv, s.dv.v)] + (s.pAP.take() if i == 0 else []), inc=[s.mm])
    for i in range(2):
        V_(lambda e, i=i, cur=cur, nxt=nxt: e.scalar_tensor_tensor(
            nxt[R_[i], :], cur[R_[i], :], s.bcP[R_[i], 0:1],
            s.pAP.ap[0:64, i * 128:(i + 1) * 128], ALU.mult, ALU.add),
           wait=[s.mm, (s.ac, s.ac.v)], inc=[s.dv])
    s.pAP.mark(s.dv)


def emit_stage_b3(p, ctx, C, cfg, io):
    from contextlib import ExitStack
    pctx = ExitStack()
    nc = p.nc
    Tl, TB = cfg["Tl"], cfg["TB"]
    NB = Tl // TB
    NCH = TB // CHUNK

    s_ld0 = p.sem("b3_ld0")
    s_mm0 = p.sem("b3_mm0")
    s_ac0 = p.sem("b3_ac0")
    s_dv0 = p.sem("b3_dv0")
    onw_row = ctx.enter_context(nc.sbuf_tensor("b3_onwr", [1, 128], F32))
    onw2 = ctx.enter_context(nc.sbuf_tensor("b3_onw2", [128, 256], F32))
    zrow = ctx.enter_context(nc.sbuf_tensor("b3_zrow", [64, 128], F32))
    io["zrow_sb"] = zrow
    io["onw2_sb"] = onw2
    P_tmp = Slot(pctx.enter_context(nc.psum_tensor("b3_ptmp", [128, 512], F32)))
    p.dma("sync", onw_row[:, :], io["L0_attn_a_o_norm_w"].rearrange("(o d) -> o d", o=1),
          inc=[(s_ld0, 16)])
    p.do("vector", lambda e: e.memset(zrow[:, :], 0.0), inc=[s_dv0])
    p.do("tensor", lambda e: e.matmul(
        P_tmp.ap[:, 0:128], C.ones_row_f[:, :], onw_row[:, :],
        start=True, stop=True), wait=[s_ld0], inc=[s_mm0])
    p.do("scalar", lambda e: e.activation(onw2[:, 0:128], P_tmp.ap[:, 0:128], AF.Copy),
         wait=[s_mm0], inc=[s_ac0])
    p.do("scalar", lambda e: e.activation(onw2[:, 128:256], P_tmp.ap[:, 0:128], AF.Copy),
         wait=[(s_mm0, s_mm0.v)], inc=[s_ac0])
    P_tmp.mark(s_ac0)
    p.barrier("b3_init")
    pctx.close()

    from contextlib import ExitStack as ES
    pctx2 = ES()
    streams = [ScanStream3(p, ctx, pctx2, C, cfg, i) for i in range(2)]
    NCHS = cfg["TB2"] // CHUNK
    pairs = [(b, hp) for b in range(NB) for hp in range(NH // 2)]
    for i in range(0, len(pairs), 2):
        duo = pairs[i:i + 2]
        for seg in range(NCH // NCHS):
            for k, (b, hp) in enumerate(duo):
                emit_scan3_loads(p, C, cfg, io, streams[k], b, hp, seg, seg == 0)
            for cc in range(seg * NCHS, (seg + 1) * NCHS):
                for k, (b, hp) in enumerate(duo):
                    emit_scan3_chunk(p, C, cfg, io, streams[k], b, hp, cc)
            for k in range(len(duo)):
                s = streams[k]
                for group in (s.qk, s.v, s.gs, [s.gbg, s.gbb]):
                    for sl_ in group:
                        sl_.mark(s.dv)
                        sl_.mark(s.mm)
    p.barrier("b3_end")
    pctx2.close()
